# version 02
# baseline (speedup 1.0000x reference)
import numpy as np
import ml_dtypes
import concourse.bass as bass
import concourse.mybir as mybir
from concourse.bass_utils import run_bass_kernel_spmd
from contextlib import ExitStack

F32 = mybir.dt.float32
BF16 = mybir.dt.bfloat16
I32 = mybir.dt.int32
AF = mybir.ActivationFunctionType
ALU = mybir.AluOpType
BFNP = ml_dtypes.bfloat16

NCORES = 8
B, S, D = 2, 8192, 2048
TPC = S // 4
NT = TPC // 128
NCD = D // 128
HEADS, HD, KVH, GQ = 16, 128, 8, 2
GI = 5504
EI = 1408
ISH = 2816
NE = 8
NH1, HS1 = 32, 64
CC = 16
LAM = 5.0
EPS = 1e-5
MAGIC = 12582912.0
INVSQ = 1.0 / np.sqrt(np.float32(D))


class Ctr:
    def __init__(self, sem):
        self.sem = sem
        self.n = 0

    def inc(self, instr, k=1):
        instr.then_inc(self.sem, k)
        self.n += k
        return instr


def f32(x):
    return np.ascontiguousarray(x, dtype=np.float32)


def bfn(x):
    return np.ascontiguousarray(x, dtype=BFNP)


def setup_consts(nc, es, block, gc: Ctr):
    rowi = es.enter_context(nc.sbuf_tensor("c_rowi", [128, 1], F32))
    coli = es.enter_context(nc.sbuf_tensor("c_coli", [128, 128], F32))
    identf = es.enter_context(nc.sbuf_tensor("c_identf", [128, 128], F32))
    identb = es.enter_context(nc.sbuf_tensor("c_identb", [128, 128], BF16))

    def g(gp):
        gp.iota(rowi[:], pattern=[[0, 1]], base=0, channel_multiplier=1,
                allow_small_or_imprecise_dtypes=True)
        gp.iota(coli[:], pattern=[[1, 128]], base=0, channel_multiplier=0,
                allow_small_or_imprecise_dtypes=True)
        gc.inc(gp.engine_nop())
    block.gpsimd(g)

    def finish(v, vc: Ctr):
        v.tensor_scalar(out=identf[:], in0=coli[:], scalar1=rowi[:, 0:1], scalar2=None,
                        op0=ALU.is_equal, op1=ALU.bypass)
        vc.inc(v.tensor_copy(identb[:], identf[:]))
    return identf, identb, rowi, coli, finish


PREP_T = [("WQp", 2048, 2048), ("WKp", 1024, 2048), ("WVp", 1024, 2048), ("WO", 2048, 2048)] + \
         [(f"eg{e}", EI, 2048) for e in range(NE)] + \
         [(f"eu{e}", EI, 2048) for e in range(NE)] + \
         [(f"ed{e}", 2048, EI) for e in range(NE)] + \
         [("sg", ISH, 2048), ("su", ISH, 2048), ("sd", 2048, ISH), ("gw", 8, 2048)]
PREP_N = [("Wx", 2048, 2048), ("Ww", 2048, 2048), ("Wk", 2048, 2048),
          ("Wv", 2048, 2048), ("Wr", 2048, 2048), ("Wout", 2048, 2048)]
PREP_G = [("gg", GI, 2048), ("gu", GI, 2048), ("gd", 2048, GI)]


def row_slices(rows):
    q, r = divmod(rows, NCORES)
    out, s = [], 0
    for c in range(NCORES):
        n = q + (1 if c < r else 0)
        out.append((s, n))
        s += n
    return out


def build_prep(sl):
    nc = bass.Bass("TRN2", target_bir_lowering=False)
    es = ExitStack()
    with es:
        ins, outs = {}, {}
        for key, rows, cols in PREP_T:
            ins[key] = nc.dram_tensor(key, [sl[key], cols], F32, kind="ExternalInput").ap()
            outs[key] = nc.dram_tensor("o_" + key, [cols, sl[key]], BF16, kind="ExternalOutput").ap()
        for key, rows, cols in PREP_N:
            ins[key] = nc.dram_tensor(key, [sl[key], cols], F32, kind="ExternalInput").ap()
            outs[key] = nc.dram_tensor("o_" + key, [sl[key], cols], BF16, kind="ExternalOutput").ap()
        for key, rows, cols in PREP_G:
            ins[key] = nc.dram_tensor(key, [sl[key], cols], I32, kind="ExternalInput").ap()
            outs[key] = nc.dram_tensor("o_" + key, [cols, sl[key]], BF16, kind="ExternalOutput").ap()
        ins["lut"] = nc.dram_tensor("lut", [1, 16], F32, kind="ExternalInput").ap()

        raw = es.enter_context(nc.sbuf_tensor("raw", [128, 2, 5504], F32))
        selm = es.enter_context(nc.sbuf_tensor("selm", [128, 5504], BF16))
        fidx = es.enter_context(nc.sbuf_tensor("fidx", [128, 5504], BF16))
        mask = es.enter_context(nc.sbuf_tensor("mask", [128, 5504], BF16))
        stage = es.enter_context(nc.sbuf_tensor("stage", [128, 11264], BF16))
        lutrow = es.enter_context(nc.sbuf_tensor("lutrow", [1, 16], F32))
        lutb = es.enter_context(nc.sbuf_tensor("lutb", [128, 16], F32))
        onesf = es.enter_context(nc.sbuf_tensor("onesf", [1, 128], F32))
        pbc = es.enter_context(nc.psum_tensor("pbc", [128, 16], F32))
        pst = [es.enter_context(nc.psum_tensor(f"pst{i}", [128, 128], BF16)) for i in range(2)]

        ld = Ctr(es.enter_context(nc.semaphore("ld")))
        vc = Ctr(es.enter_context(nc.semaphore("vc")))
        tc = Ctr(es.enter_context(nc.semaphore("tc")))
        st = Ctr(es.enter_context(nc.semaphore("st")))
        gc = Ctr(es.enter_context(nc.semaphore("gc")))
        block = es.enter_context(nc.Block())

        identf, identb, rowi, coli, const_fin = setup_consts(nc, es, block, gc)

        block.sync(lambda sy: ld.inc(sy.dma_start(out=lutrow[:], in_=ins["lut"][:]), 16))

        def v0(v):
            v.wait_ge(ld.sem, ld.n)
            v.wait_ge(gc.sem, gc.n)
            const_fin(v, vc)
            vc.inc(v.tensor_copy(onesf[:], identb[0:1, 0:128]) if False else v.memset(onesf[:], 1.0))
        block.vector(v0)

        def t0(te):
            te.wait_ge(vc.sem, vc.n)
            tc.inc(te.matmul(pbc[:], onesf[:], lutrow[:], start=True, stop=True))
        block.tensor(t0)

        def v1(v):
            v.wait_ge(tc.sem, tc.n)
            vc.inc(v.tensor_copy(lutb[:], pbc[:]))
        block.vector(v1)

        import os
        plans = [(k, sl[k], c, "t") for k, r, c in PREP_T] + \
                [(k, sl[k], c, "n") for k, r, c in PREP_N] + \
                [(k, sl[k], c, "g") for k, r, c in PREP_G]
        if os.environ.get("PREP_ONLY"):
            keys = set(os.environ["PREP_ONLY"].split(","))
            plans = [p for p in plans if p[0] in keys]

        load_done = []
        tr_evac_done = []
        for key, rs, cols, kind in plans:
            ntile = (rs + 127) // 128
            ncc = cols // 128
            for it in range(ntile):
                p = min(128, rs - it * 128)
                slot = len(load_done)
                buf = slot % 2

                def do_load(sy, key=key, it=it, p=p, buf=buf, cols=cols, kind=kind, slot=slot):
                    if slot >= 2:
                        sy.wait_ge(vc.sem, load_done[slot - 2])
                    dst = raw[0:p, buf, 0:cols]
                    if kind == "g":
                        dst = dst.bitcast(I32)
                    ld.inc(sy.dma_start(out=dst, in_=ins[key][it * 128:it * 128 + p, :]), 16)
                block.sync(do_load)
                ldt = ld.n

                if kind == "n":
                    def cast_n(v, p=p, buf=buf, cols=cols, ldt=ldt):
                        v.wait_ge(ld.sem, ldt)
                        if st.n:
                            v.wait_ge(st.sem, st.n)
                        vc.inc(v.tensor_copy(selm[0:p, 0:cols], raw[0:p, buf, 0:cols]))
                    block.vector(cast_n)
                    load_done.append(vc.n)

                    def store_n(gp, key=key, it=it, p=p, cols=cols, vct=vc.n):
                        gp.wait_ge(vc.sem, vct)
                        st.inc(gp.dma_start(out=outs[key][it * 128:it * 128 + p, :],
                                            in_=selm[0:p, 0:cols]), 16)
                    block.gpsimd(store_n)
                    continue

                if kind == "t":
                    def cast_t(v, p=p, buf=buf, cols=cols, ldt=ldt, tct=tc.n, stn=st.n):
                        v.wait_ge(ld.sem, ldt)
                        v.wait_ge(tc.sem, tct)
                        if stn:
                            v.wait_ge(st.sem, stn)
                        vc.inc(v.tensor_copy(selm[0:p, 0:cols], raw[0:p, buf, 0:cols]))
                    block.vector(cast_t)
                else:
                    def cast_g(v, p=p, buf=buf, cols=cols, ldt=ldt, tct=tc.n, stn=st.n):
                        v.wait_ge(ld.sem, ldt)
                        v.wait_ge(tc.sem, tct)
                        if stn:
                            v.wait_ge(st.sem, stn)
                        v.tensor_copy(fidx[0:p, 0:cols], raw[0:p, buf, 0:cols].bitcast(I32))
                        v.memset(selm[0:p, 0:cols], 0.0)
                        for j in range(16):
                            v.tensor_scalar(out=mask[0:p, 0:cols], in0=fidx[0:p, 0:cols],
                                            scalar1=float(j), scalar2=None,
                                            op0=ALU.is_equal, op1=ALU.bypass)
                            v.scalar_tensor_tensor(out=selm[0:p, 0:cols], in0=mask[0:p, 0:cols],
                                                   scalar=lutb[0:p, j:j + 1], op0=ALU.mult,
                                                   op1=ALU.add, in1=selm[0:p, 0:cols])
                        vc.inc(v.engine_nop())
                    block.vector(cast_g)
                load_done.append(vc.n)
                vct_tile = vc.n

                for cc in range(ncc):
                    gti = len(tr_evac_done)
                    def do_tr(te, p=p, cc=cc, vct=vct_tile, gti=gti):
                        te.wait_ge(vc.sem, vct)
                        if gti >= 2:
                            te.wait_ge(vc.sem, tr_evac_done[gti - 2])
                        tc.inc(te.transpose(pst[gti % 2][:, 0:p],
                                            selm[0:p, cc * 128:cc * 128 + 128],
                                            identity=identb[0:p, 0:p]))
                    block.tensor(do_tr)

                    def do_evac(v, p=p, it=it, cc=cc, rs=rs, tct=tc.n, gti=gti):
                        v.wait_ge(tc.sem, tct)
                        vc.inc(v.tensor_copy(stage[:, cc * rs + it * 128: cc * rs + it * 128 + p],
                                             pst[gti % 2][:, 0:p]))
                    block.vector(do_evac)
                    tr_evac_done.append(vc.n)

            if kind in ("t", "g"):
                def store_t(gp, key=key, rs=rs, ncc=ncc, vct=vc.n):
                    gp.wait_ge(vc.sem, vct)
                    for cc in range(ncc):
                        st.inc(gp.dma_start(out=outs[key][cc * 128:cc * 128 + 128, :],
                                            in_=stage[:, cc * rs: cc * rs + rs]), 16)
                block.gpsimd(store_t)
                def guard(v, stn=st.n):
                    v.wait_ge(st.sem, stn)
                    vc.inc(v.engine_nop())
                block.vector(guard)

        def fin(sy):
            sy.wait_ge(st.sem, st.n)
        block.sync(fin)
    return nc


def run_prep(inputs):
    WQp = f32(inputs["L0_attn_W_Q"].reshape(HD, HEADS, D).transpose(1, 0, 2).reshape(D, D))
    WKp = f32(inputs["L0_attn_W_K"].reshape(HD, KVH, D).transpose(1, 0, 2).reshape(KVH * HD, D))
    WVp = f32(inputs["L0_attn_W_V"].reshape(HD, KVH, D).transpose(1, 0, 2).reshape(KVH * HD, D))
    full = {
        "WQp": WQp, "WKp": WKp, "WVp": WVp, "WO": f32(inputs["L0_attn_W_O"]),
        "sg": f32(inputs["L1_ffn_shared_gate"]), "su": f32(inputs["L1_ffn_shared_up"]),
        "sd": f32(inputs["L1_ffn_shared_down"]), "gw": f32(inputs["L1_ffn_gate_w"]),
        "Wx": f32(inputs["L1_attn_Wx"]), "Ww": f32(inputs["L1_attn_Ww"]),
        "Wk": f32(inputs["L1_attn_Wk"]), "Wv": f32(inputs["L1_attn_Wv"]),
        "Wr": f32(inputs["L1_attn_Wr"]), "Wout": f32(inputs["L1_attn_Wout"]),
        "gg": np.ascontiguousarray(inputs["L0_ffn_gate_idx"], dtype=np.int32),
        "gu": np.ascontiguousarray(inputs["L0_ffn_up_idx"], dtype=np.int32),
        "gd": np.ascontiguousarray(inputs["L0_ffn_down_idx"], dtype=np.int32),
    }
    for e in range(NE):
        full[f"eg{e}"] = f32(inputs["L1_ffn_expert_gate"][e])
        full[f"eu{e}"] = f32(inputs["L1_ffn_expert_up"][e])
        full[f"ed{e}"] = f32(inputs["L1_ffn_expert_down"][e])
    lut = f32(inputs["L0_ffn_lut"]).reshape(1, 16)

    all_specs = PREP_T + PREP_N + PREP_G
    slices = {k: row_slices(r) for k, r, c in all_specs}
    sl = {k: max(n for _, n in slices[k]) for k, r, c in all_specs}
    nc = build_prep(sl)
    in_maps = []
    for c in range(NCORES):
        m = {"lut": lut}
        for k, r, cols in all_specs:
            s0, n = slices[k][c]
            arr = full[k][s0:s0 + n]
            if n < sl[k]:
                pad = np.zeros((sl[k] - n,) + arr.shape[1:], arr.dtype)
                arr = np.concatenate([arr, pad], 0)
            m[k] = np.ascontiguousarray(arr)
        in_maps.append(m)
    res = run_bass_kernel_spmd(nc, in_maps, core_ids=list(range(NCORES)))
    W = {}
    for k, r, cols in PREP_T + PREP_G:
        parts = []
        for c in range(NCORES):
            s0, n = slices[k][c]
            parts.append(np.asarray(res.results[c]["o_" + k])[:, :n])
        W[k + "T"] = np.ascontiguousarray(np.concatenate(parts, axis=1))
    for k, r, cols in PREP_N:
        parts = []
        for c in range(NCORES):
            s0, n = slices[k][c]
            parts.append(np.asarray(res.results[c]["o_" + k])[:n])
        W[k] = np.ascontiguousarray(np.concatenate(parts, axis=0))
    return W





def build_l0front():
    nc = bass.Bass("TRN2", target_bir_lowering=False)
    es = ExitStack()
    with es:
        x_in = nc.dram_tensor("x", [TPC, D], F32, kind="ExternalInput").ap()
        wq_in = nc.dram_tensor("WQpT", [D, D], BF16, kind="ExternalInput").ap()
        wk_in = nc.dram_tensor("WKpT", [D, KVH * HD], BF16, kind="ExternalInput").ap()
        wv_in = nc.dram_tensor("WVpT", [D, KVH * HD], BF16, kind="ExternalInput").ap()
        pos_in = nc.dram_tensor("posrow", [1, TPC], F32, kind="ExternalInput").ap()
        xn_out = nc.dram_tensor("xn", [TPC, D], BF16, kind="ExternalOutput").ap()
        xnT_out = nc.dram_tensor("xnTo", [D, TPC], BF16, kind="ExternalOutput").ap()
        qT_out = nc.dram_tensor("qT", [HEADS, HD, TPC], BF16, kind="ExternalOutput").ap()
        kT_out = nc.dram_tensor("kT", [KVH, HD, TPC], BF16, kind="ExternalOutput").ap()
        v_out = nc.dram_tensor("vtok", [KVH, TPC, HD], BF16, kind="ExternalOutput").ap()

        xbuf = es.enter_context(nc.sbuf_tensor("xbuf", [128, 2, D], F32))
        sqb = es.enter_context(nc.sbuf_tensor("sqb", [128, D], BF16))
        stats = es.enter_context(nc.sbuf_tensor("stats", [128, 6, NT], F32))
        xnt = es.enter_context(nc.sbuf_tensor("xnt", [128, 2, D], BF16))
        xnT = es.enter_context(nc.sbuf_tensor("xnT", [128, NCD, TPC], BF16))
        posr = es.enter_context(nc.sbuf_tensor("posr", [1, TPC], F32))
        drow = es.enter_context(nc.sbuf_tensor("drow", [1, 6, 128], F32))
        cosb = es.enter_context(nc.sbuf_tensor("cosb", [128, TPC], F32))
        sinb = es.enter_context(nc.sbuf_tensor("sinb", [128, TPC], F32))
        tred = es.enter_context(nc.sbuf_tensor("tred", [128, TPC], F32))
        tred2 = es.enter_context(nc.sbuf_tensor("tred2", [128, TPC], F32))
        permM = es.enter_context(nc.sbuf_tensor("permM", [128, 128], BF16))
        prow = es.enter_context(nc.sbuf_tensor("prow", [128, 2, 128], F32))
        wbuf = es.enter_context(nc.sbuf_tensor("wbuf", [128, 2, NCD, 128], BF16))
        wvb = es.enter_context(nc.sbuf_tensor("wvb", [128, NCD, 512], BF16))
        hsb = es.enter_context(nc.sbuf_tensor("hsb", [128, D], BF16))
        rope1 = es.enter_context(nc.sbuf_tensor("rope1", [128, D], F32))
        hrot = es.enter_context(nc.sbuf_tensor("hrot", [128, D], BF16))
        vsb = es.enter_context(nc.sbuf_tensor("vsb", [128, 1024], BF16))

        pA = es.enter_context(nc.psum_tensor("pA", [128, D], F32))
        pst = [es.enter_context(nc.psum_tensor(f"pst{i}", [128, 128], BF16)) for i in range(2)]
        pP = es.enter_context(nc.psum_tensor("pP", [128, 512], F32))
        pR = es.enter_context(nc.psum_tensor("pR", [128, 512], F32))

        ld = Ctr(es.enter_context(nc.semaphore("ld")))
        vc = Ctr(es.enter_context(nc.semaphore("vc")))
        tc = Ctr(es.enter_context(nc.semaphore("tc")))
        sc = Ctr(es.enter_context(nc.semaphore("sc")))
        st = Ctr(es.enter_context(nc.semaphore("st")))
        gc = Ctr(es.enter_context(nc.semaphore("gc")))
        block = es.enter_context(nc.Block())

        identf, identb, rowi, coli, const_fin = setup_consts(nc, es, block, gc)

        def drow_i(gp):
            gp.iota(drow[0:1, 0, :], pattern=[[1, 128]], base=0, channel_multiplier=0,
                    allow_small_or_imprecise_dtypes=True)
            gc.inc(gp.engine_nop())
        block.gpsimd(drow_i)

        block.sync(lambda sy: ld.inc(sy.dma_start(out=posr[:], in_=pos_in[:]), 16))

        def v_consts(v):
            v.wait_ge(gc.sem, gc.n)
            v.wait_ge(ld.sem, ld.n)
            const_fin(v, vc)
            v.tensor_scalar(out=prow[:, 0, 0:1], in0=rowi[:, 0:1], scalar1=-64.0, scalar2=None,
                            op0=ALU.add, op1=ALU.bypass)
            v.tensor_scalar(out=prow[:, 1, 0:1], in0=rowi[:, 0:1], scalar1=64.0, scalar2=None,
                            op0=ALU.add, op1=ALU.bypass)
            v.tensor_scalar(out=tred[0:128, 0:128], in0=coli[:], scalar1=prow[:, 0, 0:1],
                            scalar2=None, op0=ALU.is_equal, op1=ALU.bypass)
            v.tensor_scalar(out=tred2[0:128, 0:128], in0=coli[:], scalar1=prow[:, 1, 0:1],
                            scalar2=None, op0=ALU.is_equal, op1=ALU.bypass)
            v.tensor_tensor(out=permM[:], in0=tred2[0:128, 0:128], in1=tred[0:128, 0:128],
                            op=ALU.subtract)
            v.tensor_scalar(out=drow[0:1, 1, :], in0=drow[0:1, 0, :], scalar1=64.0, scalar2=None,
                            op0=ALU.is_ge, op1=ALU.bypass)
            v.scalar_tensor_tensor(out=drow[0:1, 2, :], in0=drow[0:1, 1, :], scalar=-64.0,
                                   op0=ALU.mult, op1=ALU.add, in1=drow[0:1, 0, :])
            vc.inc(v.engine_nop())
        block.vector(v_consts)

        def s_invf(s):
            s.wait_ge(vc.sem, vc.n)
            sc.inc(s.activation(out=drow[0:1, 3, :], in_=drow[0:1, 2, :], func=AF.Exp,
                                bias=0.0, scale=-2.0 * float(np.log(10000.0)) / HD))
        block.scalar(s_invf)

        def v_invf2(v):
            v.wait_ge(sc.sem, sc.n)
            vc.inc(v.tensor_scalar(out=drow[0:1, 4, :], in0=drow[0:1, 3, :],
                                   scalar1=float(1.0 / (2 * np.pi)), scalar2=None,
                                   op0=ALU.mult, op1=ALU.bypass))
        block.vector(v_invf2)

        def t_ang(te):
            te.wait_ge(vc.sem, vc.n)
            for w in range(4):
                tc.inc(te.matmul(pA[:, w * 512:(w + 1) * 512], drow[0:1, 4, :],
                                 posr[:, w * 512:(w + 1) * 512], start=True, stop=True))
        block.tensor(t_ang)

        def v_rangered(v):
            v.wait_ge(tc.sem, tc.n)
            v.tensor_scalar(out=tred[:], in0=pA[:], scalar1=MAGIC, scalar2=MAGIC,
                            op0=ALU.add, op1=ALU.subtract)
            v.tensor_tensor(out=tred2[:], in0=pA[:], in1=tred[:], op=ALU.subtract)
            vc.inc(v.tensor_scalar(out=tred[:], in0=tred2[:], scalar1=float(2 * np.pi),
                                   scalar2=None, op0=ALU.mult, op1=ALU.bypass))
        block.vector(v_rangered)

        def s_sin(s):
            s.wait_ge(vc.sem, vc.n)
            sc.inc(s.activation(out=sinb[:], in_=tred[:], func=AF.Sin, bias=0.0, scale=1.0))
        block.scalar(s_sin)

        def v_rangered2(v):
            v.wait_ge(sc.sem, sc.n)
            v.tensor_scalar(out=tred2[:], in0=pA[:], scalar1=0.25, scalar2=None,
                            op0=ALU.add, op1=ALU.bypass)
            v.tensor_scalar(out=tred[:], in0=tred2[:], scalar1=MAGIC, scalar2=MAGIC,
                            op0=ALU.add, op1=ALU.subtract)
            v.tensor_tensor(out=tred2[:], in0=tred2[:], in1=tred[:], op=ALU.subtract)
            vc.inc(v.tensor_scalar(out=tred[:], in0=tred2[:], scalar1=float(2 * np.pi),
                                   scalar2=None, op0=ALU.mult, op1=ALU.bypass))
        block.vector(v_rangered2)

        def s_cos(s):
            s.wait_ge(vc.sem, vc.n)
            sc.inc(s.activation(out=cosb[:], in_=tred[:], func=AF.Sin, bias=0.0, scale=1.0))
        block.scalar(s_cos)
        tables_done = sc.n

        pass1_cast = []
        for it in range(NT):
            def ld1(sy, it=it, slot=it):
                if slot >= 2:
                    sy.wait_ge(vc.sem, pass1_cast[slot - 2])
                ld.inc(sy.dma_start(out=xbuf[:, it % 2, :], in_=x_in[it * 128:(it + 1) * 128, :]), 16)
            block.sync(ld1)
            ldt = ld.n

            def v_sum(v, it=it, ldt=ldt):
                v.wait_ge(ld.sem, ldt)
                vc.inc(v.tensor_reduce(out=stats[:, 0, it:it + 1], in_=xbuf[:, it % 2, :],
                                       op=ALU.add, axis=mybir.AxisListType.X))
            block.vector(v_sum)

            def s_sq(s, it=it, ldt=ldt, vct=vc.n):
                s.wait_ge(ld.sem, ldt)
                if it >= 1:
                    s.wait_ge(vc.sem, vct)
                sc.inc(s.activation(out=sqb[:], in_=xbuf[:, it % 2, :], func=AF.Square,
                                    bias=0.0, scale=1.0))
            block.scalar(s_sq)

            def v_sqsum(v, it=it, sct=sc.n):
                v.wait_ge(sc.sem, sct)
                vc.inc(v.tensor_reduce(out=stats[:, 1, it:it + 1], in_=sqb[:],
                                       op=ALU.add, axis=mybir.AxisListType.X))
            block.vector(v_sqsum)
            pass1_cast.append(vc.n)

        def v_stats(v):
            v.tensor_scalar(out=stats[:, 2, :], in0=stats[:, 0, :], scalar1=1.0 / D, scalar2=None,
                            op0=ALU.mult, op1=ALU.bypass)
            v.tensor_tensor(out=stats[:, 5, :], in0=stats[:, 2, :], in1=stats[:, 2, :], op=ALU.mult)
            v.tensor_scalar(out=stats[:, 3, :], in0=stats[:, 1, :], scalar1=1.0 / D, scalar2=None,
                            op0=ALU.mult, op1=ALU.bypass)
            v.tensor_tensor(out=stats[:, 3, :], in0=stats[:, 3, :], in1=stats[:, 5, :], op=ALU.subtract)
            vc.inc(v.tensor_scalar(out=stats[:, 3, :], in0=stats[:, 3, :], scalar1=EPS, scalar2=None,
                                   op0=ALU.add, op1=ALU.bypass))
        block.vector(v_stats)

        def s_rstd(s):
            s.wait_ge(vc.sem, vc.n)
            sc.inc(s.activation(out=stats[:, 4, :], in_=stats[:, 3, :], func=AF.Sqrt,
                                bias=0.0, scale=1.0))
        block.scalar(s_rstd)

        def v_recip(v):
            v.wait_ge(sc.sem, sc.n)
            vc.inc(v.reciprocal(out=stats[:, 4, :], in_=stats[:, 4, :]))
        block.vector(v_recip)
        stats_done = vc.n

        xn_cast = []
        for it in range(NT):
            def ld2(sy, it=it, slot=it, p1=pass1_cast[-1]):
                if slot >= 2:
                    sy.wait_ge(vc.sem, xn_cast[slot - 2][0])
                else:
                    sy.wait_ge(vc.sem, p1)
                ld.inc(sy.dma_start(out=xbuf[:, it % 2, :], in_=x_in[it * 128:(it + 1) * 128, :]), 16)
            block.sync(ld2)
            ldt = ld.n

            def v_xn(v, it=it, ldt=ldt, tct=tc.n, stn=st.n):
                v.wait_ge(ld.sem, ldt)
                v.wait_ge(vc.sem, stats_done)
                v.wait_ge(tc.sem, tct)
                v.wait_ge(st.sem, stn)
                vc.inc(v.tensor_scalar(out=xnt[:, it % 2, :], in0=xbuf[:, it % 2, :],
                                       scalar1=stats[:, 2, it:it + 1], scalar2=stats[:, 4, it:it + 1],
                                       op0=ALU.subtract, op1=ALU.mult))
            block.vector(v_xn)
            vct = vc.n
            xn_cast.append((vct, None))

            def st_xn(gp, it=it, vct=vct):
                gp.wait_ge(vc.sem, vct)
                st.inc(gp.dma_start(out=xn_out[it * 128:(it + 1) * 128, :], in_=xnt[:, it % 2, :]), 16)
            block.gpsimd(st_xn)

            for cc in range(NCD):
                gti = it * NCD + cc
                def tr_xn(te, it=it, cc=cc, vct=vct, gti=gti):
                    te.wait_ge(vc.sem, vct)
                    if gti >= 2:
                        te.wait_ge(vc.sem, xn_evac[gti - 2])
                    tc.inc(te.transpose(pst[gti % 2][:], xnt[:, it % 2, cc * 128:(cc + 1) * 128],
                                        identity=identb[:]))
                if gti == 0:
                    xn_evac = []
                block.tensor(tr_xn)

                def ev_xn(v, it=it, cc=cc, tct=tc.n, gti=gti):
                    v.wait_ge(tc.sem, tct)
                    vc.inc(v.tensor_copy(xnT[:, cc, it * 128:(it + 1) * 128], pst[gti % 2][:]))
                block.vector(ev_xn)
                xn_evac.append(vc.n)
        xnT_done = vc.n

        def st_xnT(gp):
            gp.wait_ge(vc.sem, xnT_done)
            for cc in range(NCD):
                st.inc(gp.dma_start(out=xnT_out[cc * 128:(cc + 1) * 128, :], in_=xnT[:, cc, :]), 16)
        block.gpsimd(st_xnT)

        wq_r = wq_in.rearrange("(cc p) n -> p cc n", p=128)
        wk_r = wk_in.rearrange("(cc p) n -> p cc n", p=128)
        wv_r = wv_in.rearrange("(cc p) n -> p cc n", p=128)
        units = [("q", h) for h in range(HEADS)] + [("k", h) for h in range(KVH)]
        unit_done = []
        for ui, (kindu, h) in enumerate(units):
            def ldw(sy, ui=ui, kindu=kindu, h=h):
                if ui >= 2:
                    sy.wait_ge(vc.sem, unit_done[ui - 2])
                src = wq_r if kindu == "q" else wk_r
                ld.inc(sy.dma_start(out=wbuf[:, ui % 2, :, :],
                                    in_=src[:, :, h * 128:(h + 1) * 128]), 16)
            block.sync(ldw)
            ldt = ld.n

            for w in range(4):
                def mm_proj(te, ui=ui, w=w, ldt=ldt, vct=xnT_done, evt=vc.n):
                    te.wait_ge(ld.sem, ldt)
                    te.wait_ge(vc.sem, vct)
                    if w >= 2:
                        te.wait_ge(vc.sem, evt)
                    pb = pP if w % 2 == 0 else pR
                    for cc in range(NCD):
                        mm = te.matmul(pb[:], wbuf[:, ui % 2, cc, :],
                                       xnT[:, cc, w * 512:(w + 1) * 512],
                                       start=(cc == 0), stop=(cc == NCD - 1))
                        if cc == NCD - 1:
                            tc.inc(mm)
                block.tensor(mm_proj)

                def ev_proj(v, w=w, tct=tc.n):
                    v.wait_ge(tc.sem, tct)
                    pb = pP if w % 2 == 0 else pR
                    vc.inc(v.tensor_copy(hsb[:, w * 512:(w + 1) * 512], pb[:]))
                block.vector(ev_proj)
            head_evac = vc.n

            for w in range(4):
                def mm_rot(te, w=w, vct=head_evac, evt=vc.n):
                    te.wait_ge(vc.sem, vct)
                    if w >= 2:
                        te.wait_ge(vc.sem, evt)
                    pb = pP if w % 2 == 0 else pR
                    tc.inc(te.matmul(pb[:], permM[:], hsb[:, w * 512:(w + 1) * 512],
                                     start=True, stop=True))
                block.tensor(mm_rot)

                def v_rope(v, w=w, tct=tc.n, sct=tables_done):
                    v.wait_ge(tc.sem, tct)
                    v.wait_ge(sc.sem, sct)
                    pb = pP if w % 2 == 0 else pR
                    sl = slice(w * 512, (w + 1) * 512)
                    v.tensor_tensor(out=tred[:, 0:512], in0=pb[:], in1=sinb[:, sl], op=ALU.mult)
                    v.tensor_tensor(out=tred2[:, 0:512], in0=hsb[:, sl], in1=cosb[:, sl], op=ALU.mult)
                    vc.inc(v.tensor_tensor(out=hrot[:, sl], in0=tred[:, 0:512],
                                           in1=tred2[:, 0:512], op=ALU.add))
                block.vector(v_rope)
            unit_done.append(vc.n)

            def st_head(gp, kindu=kindu, h=h, vct=vc.n):
                gp.wait_ge(vc.sem, vct)
                dst = qT_out[h] if kindu == "q" else kT_out[h]
                st.inc(gp.dma_start(out=dst[:, :], in_=hrot[:]), 16)
            block.gpsimd(st_head)
            def gd(v, stn=st.n):
                v.wait_ge(st.sem, stn)
                vc.inc(v.engine_nop())
            block.vector(gd)
            unit_done[-1] = vc.n

        v_r = v_out.rearrange("kv (tt p) d -> tt p kv d", p=128)
        for wnd in range(2):
            def ldwv(sy, wnd=wnd):
                if wnd >= 1:
                    sy.wait_ge(tc.sem, tc.n)
                ld.inc(sy.dma_start(out=wvb[:, :, :], in_=wv_r[:, :, wnd * 512:(wnd + 1) * 512]), 16)
            block.sync(ldwv)
            ldt = ld.n
            for it in range(NT):
                def mm_v(te, it=it, wnd=wnd, ldt=ldt, evt=vc.n):
                    te.wait_ge(ld.sem, ldt)
                    if it >= 2:
                        te.wait_ge(vc.sem, evt)
                    pb = pP if it % 2 == 0 else pR
                    for cc in range(NCD):
                        mm = te.matmul(pb[:], xnT[:, cc, it * 128:(it + 1) * 128],
                                       wvb[:, cc, :], start=(cc == 0), stop=(cc == NCD - 1))
                        if cc == NCD - 1:
                            tc.inc(mm)
                block.tensor(mm_v)

                def ev_v(v, it=it, tct=tc.n, stn=st.n):
                    v.wait_ge(tc.sem, tct)
                    v.wait_ge(st.sem, stn)
                    pb = pP if it % 2 == 0 else pR
                    vc.inc(v.tensor_copy(vsb[:, 0:512], pb[:]))
                block.vector(ev_v)

                def st_v(gp, it=it, wnd=wnd, vct=vc.n):
                    gp.wait_ge(vc.sem, vct)
                    dst = v_r[it, :, wnd * 4:(wnd + 1) * 4, :]
                    st.inc(gp.dma_start(out=dst, in_=vsb[:, 0:512]), 16)
                block.gpsimd(st_v)

        def fin(sy):
            sy.wait_ge(st.sem, st.n)
        block.sync(fin)
    return nc


def build_l0attn():
    nc = bass.Bass("TRN2", target_bir_lowering=False)
    es = ExitStack()
    with es:
        qT_in = nc.dram_tensor("qT", [HEADS, HD, TPC], BF16, kind="ExternalInput").ap()
        kT_in = nc.dram_tensor("kTf", [KVH, HD, S], BF16, kind="ExternalInput").ap()
        v_in = nc.dram_tensor("vf", [KVH, S, HD], BF16, kind="ExternalInput").ap()
        wo_in = nc.dram_tensor("WOT", [D, D], BF16, kind="ExternalInput").ap()
        s_out = nc.dram_tensor("s_att", [TPC, D], BF16, kind="ExternalOutput").ap()

        NKC = S // 128
        NGRP = NKC // 2

        kvb = es.enter_context(nc.sbuf_tensor("kvb", [128, 1, S], BF16))
        vvb = es.enter_context(nc.sbuf_tensor("vvb", [128, 1, NKC, 128], BF16))
        qhb = es.enter_context(nc.sbuf_tensor("qhb", [128, 2, TPC], BF16))
        esb = es.enter_context(nc.sbuf_tensor("esb", [128, 2, 1024], BF16))
        ctxa = es.enter_context(nc.sbuf_tensor("ctxa", [128, HEADS, TPC], BF16))
        ctmp = es.enter_context(nc.sbuf_tensor("ctmp", [128, 512], BF16))
        wot = es.enter_context(nc.sbuf_tensor("wot", [128, NCD, D], BF16))
        rsum = es.enter_context(nc.sbuf_tensor("rsum", [1, 2, 512], F32))
        ones128 = es.enter_context(nc.sbuf_tensor("ones128", [128, 1], BF16))
        ones1f = es.enter_context(nc.sbuf_tensor("ones1f", [1, 128], F32))
        otile = es.enter_context(nc.sbuf_tensor("otile", [128, 2, 512], BF16))

        pS = es.enter_context(nc.psum_tensor("pS", [128, 2, 1024], F32))
        pCtx = es.enter_context(nc.psum_tensor("pCtx", [128, 512], F32))
        pSum = es.enter_context(nc.psum_tensor("pSum", [1, 512], F32))
        pBC = es.enter_context(nc.psum_tensor("pBC", [128, 512], F32))
        pO = es.enter_context(nc.psum_tensor("pO", [128, 512], F32))

        ld = Ctr(es.enter_context(nc.semaphore("ld")))
        vc = Ctr(es.enter_context(nc.semaphore("vc")))
        tc = Ctr(es.enter_context(nc.semaphore("tc")))
        sc = Ctr(es.enter_context(nc.semaphore("sc")))
        st = Ctr(es.enter_context(nc.semaphore("st")))
        gc = Ctr(es.enter_context(nc.semaphore("gc")))
        block = es.enter_context(nc.Block())

        def g0(gp):
            gc.inc(gp.engine_nop())
        block.gpsimd(g0)

        def v0(v):
            v.memset(ones128[:], 1.0)
            vc.inc(v.memset(ones1f[:], 1.0))
        block.vector(v0)

        wot_r = wo_in.rearrange("(cc p) n -> p cc n", p=128)
        block.sync(lambda sy: ld.inc(sy.dma_start(out=wot[:], in_=wot_r), 16))

        kv_ready = []
        q_ready = []
        for kv in range(KVH):
            def ldkv(sy, kv=kv, tct=tc.n):
                if kv >= 1:
                    sy.wait_ge(tc.sem, kv_free[kv - 1])
                ld.inc(sy.dma_start(out=kvb[:, 0, :], in_=kT_in[kv]), 16)
                vr = v_in[kv].rearrange("(c p) d -> p c d", p=128)
                ld.inc(sy.dma_start(out=vvb[:, 0, :, :], in_=vr), 16)
            if kv == 0:
                kv_free = []
            block.sync(ldkv)
            kv_ready.append(ld.n)

            for g in range(GQ):
                h = kv * GQ + g
                def ldq(sy, h=h, tct=tc.n):
                    if h >= 2:
                        sy.wait_ge(tc.sem, q_free[h - 2])
                    ld.inc(sy.dma_start(out=qhb[:, h % 2, :], in_=qT_in[h]), 16)
                if h == 0:
                    q_free = []
                block.sync(ldq)
                q_ready.append(ld.n)

                for w in range(4):
                    exp_done = []
                    score_done = []
                    for grp in range(NGRP):
                        def mm_sc(te, kv=kv, h=h, w=w, grp=grp, ldt=max(kv_ready[kv], q_ready[h]), vct=vc.n):
                            te.wait_ge(ld.sem, ldt)
                            if grp == 0:
                                te.wait_ge(vc.sem, vct)
                            if grp >= 2:
                                te.wait_ge(sc.sem, exp_done[grp - 2])
                            for j in range(2):
                                kc = grp * 2 + j
                                mm = te.matmul(pS[:, grp % 2, j * 512:j * 512 + 512],
                                               kvb[:, 0, kc * 128:(kc + 1) * 128],
                                               qhb[:, h % 2, w * 512:(w + 1) * 512],
                                               start=True, stop=True)
                                if j == 1:
                                    tc.inc(mm)
                        block.tensor(mm_sc)
                        score_done.append(tc.n)

                        def s_exp(s, grp=grp, tct=tc.n, vct=vc.n):
                            s.wait_ge(tc.sem, tct)
                            if grp >= 2:
                                s.wait_ge(tc.sem, pv_done[grp - 2])
                            sc.inc(s.activation(out=esb[:, grp % 2, :], in_=pS[:, grp % 2, :],
                                                func=AF.Exp, bias=0.0, scale=float(INVSQ)))
                        block.scalar(s_exp)
                        exp_done.append(sc.n)

                        def mm_spv(te, kv=kv, grp=grp, sct=sc.n):
                            te.wait_ge(sc.sem, sct)
                            for j in range(2):
                                kc = grp * 2 + j
                                te.matmul(pSum[:], ones128[:],
                                          esb[:, grp % 2, j * 512:j * 512 + 512],
                                          start=(kc == 0), stop=(kc == NKC - 1))
                                mm = te.matmul(pCtx[:], vvb[:, 0, kc, :],
                                               esb[:, grp % 2, j * 512:j * 512 + 512],
                                               start=(kc == 0), stop=(kc == NKC - 1))
                                if j == 1:
                                    tc.inc(mm)
                        block.tensor(mm_spv)
                        if grp == 0:
                            pv_done = []
                        pv_done.append(tc.n)

                    def v_rs(v, w=w, tct=tc.n):
                        v.wait_ge(tc.sem, tct)
                        v.reciprocal(out=rsum[:, w % 2, :], in_=pSum[:])
                        vc.inc(v.tensor_copy(ctmp[:], pCtx[:]))
                    block.vector(v_rs)

                    def mm_bc(te, w=w, vct=vc.n):
                        te.wait_ge(vc.sem, vct)
                        tc.inc(te.matmul(pBC[:], ones1f[:], rsum[:, w % 2, :],
                                         start=True, stop=True))
                    block.tensor(mm_bc)

                    def v_norm(v, h=h, w=w, tct=tc.n):
                        v.wait_ge(tc.sem, tct)
                        vc.inc(v.tensor_tensor(out=ctxa[:, h, w * 512:(w + 1) * 512],
                                               in0=ctmp[:], in1=pBC[:], op=ALU.mult))
                    block.vector(v_norm)
                q_free.append(tc.n)
            kv_free.append(tc.n)
        ctx_done = vc.n

        for it in range(NT):
            for dw in range(4):
                def mm_o(te, it=it, dw=dw, vct=ctx_done, evt=vc.n):
                    te.wait_ge(vc.sem, vct)
                    te.wait_ge(ld.sem, 16)
                    slot = it * 4 + dw
                    if slot >= 2:
                        te.wait_ge(vc.sem, evt)
                    for cc in range(NCD):
                        mm = te.matmul(pO[:] if False else (pS[:, 0, 0:512] if slot % 2 == 0 else pS[:, 1, 0:512]),
                                       ctxa[:, cc, it * 128:(it + 1) * 128],
                                       wot[:, cc, dw * 512:(dw + 1) * 512],
                                       start=(cc == 0), stop=(cc == NCD - 1))
                        if cc == NCD - 1:
                            tc.inc(mm)
                block.tensor(mm_o)

                def ev_o(v, it=it, dw=dw, tct=tc.n, stn=st.n):
                    v.wait_ge(tc.sem, tct)
                    if st.n:
                        v.wait_ge(st.sem, stn)
                    slot = it * 4 + dw
                    vc.inc(v.tensor_copy(otile[:, slot % 2, :],
                                         pS[:, 0, 0:512] if slot % 2 == 0 else pS[:, 1, 0:512]))
                block.vector(ev_o)

                def st_o(gp, it=it, dw=dw, vct=vc.n):
                    gp.wait_ge(vc.sem, vct)
                    slot = it * 4 + dw
                    st.inc(gp.dma_start(out=s_out[it * 128:(it + 1) * 128, dw * 512:(dw + 1) * 512],
                                        in_=otile[:, slot % 2, :]), 16)
                block.gpsimd(st_o)

        block.sync(lambda sy: sy.wait_ge(st.sem, st.n))
    return nc


def build_l0ffn():
    nc = bass.Bass("TRN2", target_bir_lowering=False)
    es = ExitStack()
    with es:
        NIB = GI // 128
        x_in = nc.dram_tensor("x", [TPC, D], F32, kind="ExternalInput").ap()
        xnT_in = nc.dram_tensor("xnTo", [D, TPC], BF16, kind="ExternalInput").ap()
        s_in = nc.dram_tensor("s_att", [TPC, D], BF16, kind="ExternalInput").ap()
        gg_in = nc.dram_tensor("ggT", [D, GI], BF16, kind="ExternalInput").ap()
        gu_in = nc.dram_tensor("guT", [D, GI], BF16, kind="ExternalInput").ap()
        gd_in = nc.dram_tensor("gdT", [GI, D], BF16, kind="ExternalInput").ap()
        x1_out = nc.dram_tensor("x1", [TPC, D], BF16, kind="ExternalOutput").ap()

        xnT = es.enter_context(nc.sbuf_tensor("xnT", [128, NCD, TPC], BF16))
        wgu = es.enter_context(nc.sbuf_tensor("wgu", [128, 2, 2, NCD, 128], BF16))
        hgb = es.enter_context(nc.sbuf_tensor("hgb", [128, NIB, 512], BF16))
        gdw = es.enter_context(nc.sbuf_tensor("gdw", [128, 1, NIB, 512], BF16))
        silub = es.enter_context(nc.sbuf_tensor("silub", [128, 512], BF16))
        sb = es.enter_context(nc.sbuf_tensor("sb", [128, 2, 512], BF16))
        xb = es.enter_context(nc.sbuf_tensor("xb", [128, 2, 512], F32))
        gtl = es.enter_context(nc.sbuf_tensor("gtl", [128, 2, 512], BF16))
        x1b = es.enter_context(nc.sbuf_tensor("x1b", [128, 2, 512], BF16))

        pG = es.enter_context(nc.psum_tensor("pG", [128, 512], F32))
        pU = es.enter_context(nc.psum_tensor("pU", [128, 512], F32))
        pD = es.enter_context(nc.psum_tensor("pD", [128, 2, 512], F32))

        ld = Ctr(es.enter_context(nc.semaphore("ld")))
        vc = Ctr(es.enter_context(nc.semaphore("vc")))
        tc = Ctr(es.enter_context(nc.semaphore("tc")))
        sc = Ctr(es.enter_context(nc.semaphore("sc")))
        st = Ctr(es.enter_context(nc.semaphore("st")))
        gc = Ctr(es.enter_context(nc.semaphore("gc")))
        block = es.enter_context(nc.Block())

        block.gpsimd(lambda gp: gc.inc(gp.engine_nop()))

        xnT_r = xnT_in.rearrange("(cc p) n -> p cc n", p=128)
        block.sync(lambda sy: ld.inc(sy.dma_start(out=xnT[:], in_=xnT_r), 16))
        xnT_ld = ld.n

        gg_r = gg_in.rearrange("(cc p) n -> p cc n", p=128)
        gu_r = gu_in.rearrange("(cc p) n -> p cc n", p=128)
        gd_r = gd_in.rearrange("(ib p) n -> p ib n", p=128)

        for ts in range(4):
            tok = slice(ts * 512, (ts + 1) * 512)
            silu_done = []
            for ib in range(NIB):
                def ldw(sy, ib=ib, ts=ts, vct=vc.n):
                    slot = ts * NIB + ib
                    if slot >= 2:
                        sy.wait_ge(vc.sem, wgu_free[slot - 2])
                    ld.inc(sy.dma_start(out=wgu[:, ib % 2, 0, :, :],
                                         in_=gg_r[:, :, ib * 128:(ib + 1) * 128]), 16)
                    ld.inc(sy.dma_start(out=wgu[:, ib % 2, 1, :, :],
                                        in_=gu_r[:, :, ib * 128:(ib + 1) * 128]), 16)
                if ts == 0 and ib == 0:
                    wgu_free = []
                block.sync(ldw)
                ldt = ld.n

                def mm_gu(te, ib=ib, ts=ts, ldt=ldt, sct=sc.n, vct=vc.n):
                    te.wait_ge(ld.sem, ldt)
                    te.wait_ge(ld.sem, xnT_ld)
                    if ib >= 1:
                        te.wait_ge(vc.sem, vct)
                    for cc in range(NCD):
                        te.matmul(pG[:], wgu[:, ib % 2, 0, cc, :], xnT[:, cc, ts * 512:(ts + 1) * 512],
                                  start=(cc == 0), stop=(cc == NCD - 1))
                        mm = te.matmul(pU[:], wgu[:, ib % 2, 1, cc, :], xnT[:, cc, ts * 512:(ts + 1) * 512],
                                       start=(cc == 0), stop=(cc == NCD - 1))
                        if cc == NCD - 1:
                            tc.inc(mm)
                block.tensor(mm_gu)

                def s_silu(s, ib=ib, tct=tc.n):
                    s.wait_ge(tc.sem, tct)
                    sc.inc(s.activation(out=silub[:], in_=pG[:], func=AF.Silu, bias=0.0, scale=1.0))
                block.scalar(s_silu)

                def v_h(v, ib=ib, sct=sc.n):
                    v.wait_ge(sc.sem, sct)
                    vc.inc(v.tensor_tensor(out=hgb[:, ib, :], in0=silub[:], in1=pU[:], op=ALU.mult))
                block.vector(v_h)
                wgu_free.append(vc.n)
                silu_done.append(vc.n)
            h_done = vc.n

            for hw in range(4):
                def ldd(sy, ts=ts, hw=hw, tct=tc.n):
                    slot = ts * 4 + hw
                    if slot >= 1:
                        sy.wait_ge(tc.sem, gdw_free[slot - 1])
                    ld.inc(sy.dma_start(out=gdw[:, 0, :, :],
                                        in_=gd_r[:, :, hw * 512:(hw + 1) * 512]), 16)
                if ts == 0 and hw == 0:
                    gdw_free = []
                block.sync(ldd)
                ldt = ld.n

                for tsub in range(4):
                    it = ts * 4 + tsub
                    def ld_xs(sy, it=it, tsub=tsub, vct=vc.n):
                        slot2 = (ts * 4 + hw) * 4 + tsub
                        if slot2 >= 2:
                            sy.wait_ge(vc.sem, xs_free[slot2 - 2])
                        ld.inc(sy.dma_start(out=sb[:, tsub % 2, :],
                                             in_=s_in[it * 128:(it + 1) * 128, hw * 512:(hw + 1) * 512]), 16)
                        ld.inc(sy.dma_start(out=xb[:, tsub % 2, :],
                                            in_=x_in[it * 128:(it + 1) * 128, hw * 512:(hw + 1) * 512]), 16)
                    if ts == 0 and hw == 0 and tsub == 0:
                        xs_free = []
                    block.sync(ld_xs)
                    ldt2 = ld.n

                    def mm_d(te, ts=ts, hw=hw, tsub=tsub, ldt=ldt, hct=h_done, sct=sc.n):
                        te.wait_ge(ld.sem, ldt)
                        te.wait_ge(vc.sem, hct)
                        if tsub >= 2:
                            te.wait_ge(sc.sem, sct)
                        for ic in range(NIB):
                            mm = te.matmul(pD[:, tsub % 2, :], hgb[:, ic, tsub * 128:(tsub + 1) * 128],
                                           gdw[:, 0, ic, :], start=(ic == 0), stop=(ic == NIB - 1))
                            if ic == NIB - 1:
                                tc.inc(mm)
                    block.tensor(mm_d)

                    def s_sg(s, tsub=tsub, tct=tc.n, vct=vc.n):
                        s.wait_ge(tc.sem, tct)
                        if vc.n:
                            s.wait_ge(vc.sem, vct)
                        sc.inc(s.activation(out=gtl[:, tsub % 2, :], in_=pD[:, tsub % 2, :],
                                            func=AF.Silu, bias=0.0, scale=1.0))
                    block.scalar(s_sg)

                    def v_comb(v, tsub=tsub, ldt2=ldt2, sct=sc.n, stn=st.n):
                        v.wait_ge(sc.sem, sct)
                        v.wait_ge(ld.sem, ldt2)
                        if st.n:
                            v.wait_ge(st.sem, stn)
                        v.tensor_tensor(out=x1b[:, tsub % 2, :], in0=gtl[:, tsub % 2, :],
                                        in1=sb[:, tsub % 2, :], op=ALU.mult)
                        vc.inc(v.tensor_tensor(out=x1b[:, tsub % 2, :], in0=x1b[:, tsub % 2, :],
                                               in1=xb[:, tsub % 2, :], op=ALU.add))
                    block.vector(v_comb)
                    xs_free.append(vc.n)

                    def st_x1(gp, it=it, hw=hw, tsub=tsub, vct=vc.n):
                        gp.wait_ge(vc.sem, vct)
                        st.inc(gp.dma_start(out=x1_out[it * 128:(it + 1) * 128, hw * 512:(hw + 1) * 512],
                                            in_=x1b[:, tsub % 2, :]), 16)
                    block.gpsimd(st_x1)
                gdw_free.append(tc.n)

        block.sync(lambda sy: sy.wait_ge(st.sem, st.n))
    return nc


def build_l1front():
    nc = bass.Bass("TRN2", target_bir_lowering=False)
    es = ExitStack()
    NCH = TPC // CC
    NG = NCH // CC
    with es:
        x1_in = nc.dram_tensor("x1", [TPC, D], BF16, kind="ExternalInput").ap()
        W_in = {k: nc.dram_tensor(k, [D, D], BF16, kind="ExternalInput").ap()
                for k in ["Wx", "Ww", "Wk", "Wv", "Wr"]}
        gam_in = nc.dram_tensor("gamma", [NCD, 128], F32, kind="ExternalInput").ap()
        bet_in = nc.dram_tensor("beta", [NCD, 128], F32, kind="ExternalInput").ap()
        bw_in = nc.dram_tensor("bw", [NCD, 128], F32, kind="ExternalInput").ap()
        bwf_in = nc.dram_tensor("bwf", [1, D], F32, kind="ExternalInput").ap()
        xn1_out = nc.dram_tensor("xn1", [TPC, D], BF16, kind="ExternalOutput").ap()
        y_out = nc.dram_tensor("y_feat", [D, TPC], BF16, kind="ExternalOutput").ap()
        rT_out = nc.dram_tensor("rT", [D, TPC], BF16, kind="ExternalOutput").ap()
        Sr_out = nc.dram_tensor("Srange", [D, HS1], F32, kind="ExternalOutput").ap()
        Dr_out = nc.dram_tensor("Drange", [D, 1], F32, kind="ExternalOutput").ap()
        import os as _os
        DBG = bool(_os.environ.get("L1DBG"))
        if DBG:
            dbg_l = nc.dram_tensor("dbg_l", [128, TPC], F32, kind="ExternalOutput").ap()
            dbg_Lc = nc.dram_tensor("dbg_Lc", [128, TPC], F32, kind="ExternalOutput").ap()
            dbg_e1 = nc.dram_tensor("dbg_e1", [128, TPC], F32, kind="ExternalOutput").ap()
            dbg_x2 = nc.dram_tensor("dbg_x2", [128, TPC], BF16, kind="ExternalOutput").ap()
            dbg_t1 = nc.dram_tensor("dbg_t1", [128, TPC], BF16, kind="ExternalOutput").ap()
            dbg_pre = nc.dram_tensor("dbg_pre", [128, TPC], F32, kind="ExternalOutput").ap()
            dbg_A = nc.dram_tensor("dbg_A", [16, 8, 512], BF16, kind="ExternalOutput").ap()
        vscr = nc.dram_tensor("vscr", [TPC, D], BF16, kind="Internal").ap()
        kscr = nc.dram_tensor("kscr", [2, TPC, 128], BF16, kind="Internal").ap()
        lscr = nc.dram_tensor("lscr", [D, TPC], F32, kind="Internal").ap()

        xn2T = es.enter_context(nc.sbuf_tensor("xn2T", [128, NCD, TPC], BF16))
        sqb = es.enter_context(nc.sbuf_tensor("sqb", [128, D], BF16))
        stt = es.enter_context(nc.sbuf_tensor("stt", [128, 8, NT], F32))
        gbrow = es.enter_context(nc.sbuf_tensor("gbrow", [128, 3, NCD], F32))
        grow3 = es.enter_context(nc.sbuf_tensor("grow3", [16, 3, 128], F32))
        bwfs = es.enter_context(nc.sbuf_tensor("bwfs", [1, D], F32))
        onesb = es.enter_context(nc.sbuf_tensor("onesb", [1, TPC], BF16))
        bwrow = es.enter_context(nc.sbuf_tensor("bwrow", [1, D], BF16))
        wslA = es.enter_context(nc.sbuf_tensor("wslA", [128, 2, NCD, 128], BF16))
        lfb = es.enter_context(nc.sbuf_tensor("lfb", [128, TPC], F32))
        Lcb = es.enter_context(nc.sbuf_tensor("Lcb", [128, TPC], F32))
        scr2 = es.enter_context(nc.sbuf_tensor("scr2", [128, 512], F32))
        scr2b = es.enter_context(nc.sbuf_tensor("scr2b", [128, 512], F32))
        dbgpre_sb = es.enter_context(nc.sbuf_tensor("dbgpre_sb", [128, TPC], F32)) if DBG else None
        yev = es.enter_context(nc.sbuf_tensor("yev", [128, 2, 512], BF16))

        pP = es.enter_context(nc.psum_tensor("pP", [128, 2, 512], F32))
        pA = es.enter_context(nc.psum_tensor("pA", [16, 512], F32))
        pM = es.enter_context(nc.psum_tensor("pM", [128, 128], F32))
        pY = es.enter_context(nc.psum_tensor("pY", [128, 2, 512], F32))
        pst = es.enter_context(nc.psum_tensor("pstT", [128, 128], BF16))

        ld = Ctr(es.enter_context(nc.semaphore("ld")))
        vc = Ctr(es.enter_context(nc.semaphore("vc")))
        tc = Ctr(es.enter_context(nc.semaphore("tc")))
        sc = Ctr(es.enter_context(nc.semaphore("sc")))
        st = Ctr(es.enter_context(nc.semaphore("st")))
        gc = Ctr(es.enter_context(nc.semaphore("gc")))
        block = es.enter_context(nc.Block())

        identf, identb, rowi, coli, const_fin = setup_consts(nc, es, block, gc)

        def loads1(sy):
            ld.inc(sy.dma_start(out=grow3[:, 0, :], in_=gam_in[:, :]), 16)
            ld.inc(sy.dma_start(out=grow3[:, 1, :], in_=bet_in[:, :]), 16)
            ld.inc(sy.dma_start(out=grow3[:, 2, :], in_=bw_in[:, :]), 16)
            ld.inc(sy.dma_start(out=bwfs[:, :], in_=bwf_in[:, :]), 16)
        block.sync(loads1)
        gb_ld = ld.n

        def v00(v):
            v.wait_ge(gc.sem, gc.n)
            v.wait_ge(ld.sem, gb_ld)
            const_fin(v, vc)
            v.memset(onesb[:], 1.0)
            vc.inc(v.engine_nop())
        block.vector(v00)

        for j in range(3):
            def tr_g(te, j=j, vct=vc.n):
                te.wait_ge(vc.sem, vct)
                if j >= 1:
                    te.wait_ge(vc.sem, gev[j - 1])
                tc.inc(te.transpose(pst.bitcast(F32)[:, 0:16], grow3[:, j, :],
                                    identity=identf[0:16, 0:16]))
            if j == 0:
                gev = []
            block.tensor(tr_g)

            def ev_g(v, j=j, tct=tc.n):
                v.wait_ge(tc.sem, tct)
                vc.inc(v.tensor_copy(gbrow[:, j, :], pst.bitcast(F32)[:, 0:16]))
            block.vector(ev_g)
            gev.append(vc.n)

        def v_bwrow(v):
            vc.inc(v.tensor_copy(bwrow[:], bwfs[:]))
        block.vector(v_bwrow)
        prelim_done = vc.n

        esA = ExitStack()
        x1b = esA.enter_context(nc.sbuf_tensor("x1b", [128, D], BF16))
        xnsc = esA.enter_context(nc.sbuf_tensor("xnsc", [128, D], BF16))
        lnsc = esA.enter_context(nc.sbuf_tensor("lnsc", [128, D], BF16))
        xn_evac = []
        for it in range(NT):
            def ld_x1(sy, it=it, vct=vc.n):
                if it >= 1:
                    sy.wait_ge(vc.sem, x1_free[it - 1])
                ld.inc(sy.dma_start(out=x1b[:], in_=x1_in[it * 128:(it + 1) * 128, :]), 16)
            if it == 0:
                x1_free = []
            block.sync(ld_x1)
            ldt = ld.n

            def v_st1(v, it=it, ldt=ldt):
                v.wait_ge(ld.sem, ldt)
                vc.inc(v.tensor_reduce(out=stt[:, 0, it:it + 1], in_=x1b[:],
                                       op=ALU.add, axis=mybir.AxisListType.X))
            block.vector(v_st1)

            def s_sq1(s, it=it, ldt=ldt, vct=vc.n):
                s.wait_ge(ld.sem, ldt)
                s.wait_ge(vc.sem, vct)
                sc.inc(s.activation(out=sqb[:], in_=x1b[:], func=AF.Square, bias=0.0, scale=1.0))
            block.scalar(s_sq1)

            def v_st2(v, it=it, sct=sc.n):
                v.wait_ge(sc.sem, sct)
                v.tensor_reduce(out=stt[:, 1, it:it + 1], in_=sqb[:], op=ALU.add,
                                axis=mybir.AxisListType.X)
                v.tensor_scalar(out=stt[:, 2, it:it + 1], in0=stt[:, 0, it:it + 1],
                                scalar1=1.0 / D, scalar2=None, op0=ALU.mult, op1=ALU.bypass)
                v.tensor_tensor(out=stt[:, 3, it:it + 1], in0=stt[:, 2, it:it + 1],
                                in1=stt[:, 2, it:it + 1], op=ALU.mult)
                v.tensor_scalar(out=stt[:, 4, it:it + 1], in0=stt[:, 1, it:it + 1],
                                scalar1=1.0 / D, scalar2=None, op0=ALU.mult, op1=ALU.bypass)
                v.tensor_tensor(out=stt[:, 4, it:it + 1], in0=stt[:, 4, it:it + 1],
                                in1=stt[:, 3, it:it + 1], op=ALU.subtract)
                vc.inc(v.tensor_scalar(out=stt[:, 4, it:it + 1], in0=stt[:, 4, it:it + 1],
                                       scalar1=EPS, scalar2=None, op0=ALU.add, op1=ALU.bypass))
            block.vector(v_st2)

            def s_rstd1(s, it=it, vct=vc.n):
                s.wait_ge(vc.sem, vct)
                sc.inc(s.activation(out=stt[:, 5, it:it + 1], in_=stt[:, 4, it:it + 1],
                                    func=AF.Sqrt, bias=0.0, scale=1.0))
            block.scalar(s_rstd1)

            def v_xn1(v, it=it, sct=sc.n, stn=st.n):
                v.wait_ge(sc.sem, sct)
                if stn:
                    v.wait_ge(st.sem, stn)
                v.reciprocal(out=stt[:, 5, it:it + 1], in_=stt[:, 5, it:it + 1])
                vc.inc(v.tensor_scalar(out=xnsc[:], in0=x1b[:],
                                       scalar1=stt[:, 2, it:it + 1], scalar2=stt[:, 5, it:it + 1],
                                       op0=ALU.subtract, op1=ALU.mult))
            block.vector(v_xn1)
            xn1_done = vc.n

            def st_xn1(gp, it=it, vct=xn1_done):
                gp.wait_ge(vc.sem, vct)
                st.inc(gp.dma_start(out=xn1_out[it * 128:(it + 1) * 128, :], in_=xnsc[:]), 16)
            block.gpsimd(st_xn1)

            def v_st3(v, it=it, vct=xn1_done):
                v.wait_ge(vc.sem, vct)
                vc.inc(v.tensor_reduce(out=stt[:, 0, it:it + 1], in_=xnsc[:],
                                       op=ALU.add, axis=mybir.AxisListType.X))
            block.vector(v_st3)

            def s_sq2(s, it=it, vct=vc.n):
                s.wait_ge(vc.sem, vct)
                sc.inc(s.activation(out=sqb[:], in_=xnsc[:], func=AF.Square, bias=0.0, scale=1.0))
            block.scalar(s_sq2)

            def v_st4(v, it=it, sct=sc.n):
                v.wait_ge(sc.sem, sct)
                v.tensor_reduce(out=stt[:, 1, it:it + 1], in_=sqb[:], op=ALU.add,
                                axis=mybir.AxisListType.X)
                v.tensor_scalar(out=stt[:, 2, it:it + 1], in0=stt[:, 0, it:it + 1],
                                scalar1=1.0 / D, scalar2=None, op0=ALU.mult, op1=ALU.bypass)
                v.tensor_tensor(out=stt[:, 3, it:it + 1], in0=stt[:, 2, it:it + 1],
                                in1=stt[:, 2, it:it + 1], op=ALU.mult)
                v.tensor_scalar(out=stt[:, 4, it:it + 1], in0=stt[:, 1, it:it + 1],
                                scalar1=1.0 / D, scalar2=None, op0=ALU.mult, op1=ALU.bypass)
                v.tensor_tensor(out=stt[:, 4, it:it + 1], in0=stt[:, 4, it:it + 1],
                                in1=stt[:, 3, it:it + 1], op=ALU.subtract)
                vc.inc(v.tensor_scalar(out=stt[:, 4, it:it + 1], in0=stt[:, 4, it:it + 1],
                                       scalar1=EPS, scalar2=None, op0=ALU.add, op1=ALU.bypass))
            block.vector(v_st4)

            def s_rstd2(s, it=it, vct=vc.n):
                s.wait_ge(vc.sem, vct)
                sc.inc(s.activation(out=stt[:, 5, it:it + 1], in_=stt[:, 4, it:it + 1],
                                    func=AF.Sqrt, bias=0.0, scale=1.0))
            block.scalar(s_rstd2)

            def v_ln2(v, it=it, sct=sc.n, tct=tc.n):
                v.wait_ge(sc.sem, sct)
                v.wait_ge(tc.sem, tct)
                v.reciprocal(out=stt[:, 5, it:it + 1], in_=stt[:, 5, it:it + 1])
                vc.inc(v.tensor_scalar(out=lnsc[:], in0=xnsc[:],
                                       scalar1=stt[:, 2, it:it + 1], scalar2=stt[:, 5, it:it + 1],
                                       op0=ALU.subtract, op1=ALU.mult))
            block.vector(v_ln2)
            vct_tile = vc.n
            x1_free.append(vc.n)

            for cc in range(NCD):
                gti = it * NCD + cc
                def tr2(te, it=it, cc=cc, vct=vct_tile, gti=gti):
                    te.wait_ge(vc.sem, vct)
                    if gti >= 1:
                        te.wait_ge(vc.sem, xn_evac[gti - 1])
                    tc.inc(te.transpose(pst[:], lnsc[:, cc * 128:(cc + 1) * 128],
                                        identity=identb[:]))
                block.tensor(tr2)

                def ev2(v, it=it, cc=cc, tct=tc.n):
                    v.wait_ge(tc.sem, tct)
                    vc.inc(v.tensor_scalar(out=xn2T[:, cc, it * 128:(it + 1) * 128], in0=pst[:],
                                           scalar1=gbrow[:, 0, cc:cc + 1], scalar2=gbrow[:, 1, cc:cc + 1],
                                           op0=ALU.mult, op1=ALU.add))
                block.vector(ev2)
                xn_evac.append(vc.n)
        xn2T_done = vc.n
        esA.close()

        if DBG:
            def st_dbgx2(gp, vct=xn2T_done):
                gp.wait_ge(vc.sem, vct)
                st.inc(gp.dma_start(out=dbg_x2[:, :], in_=xn2T[:, 0, :]), 16)
            block.gpsimd(st_dbgx2)

        esB = ExitStack()
        t1 = esB.enter_context(nc.sbuf_tensor("t1", [128, NCD, TPC], BF16))
        wv5 = esB.enter_context(nc.sbuf_tensor("wv5", [128, NCD, 512], BF16))
        wx_r = W_in["Wx"].rearrange("(cc p) n -> p cc n", p=128)
        wv_r = W_in["Wv"].rearrange("(cc p) n -> p cc n", p=128)
        ww_r = W_in["Ww"].rearrange("(cc p) n -> p cc n", p=128)
        for db in range(NCD):
            def ld_wx(sy, db=db):
                if db >= 2:
                    sy.wait_ge(vc.sem, wx_free[db - 2])
                ld.inc(sy.dma_start(out=wslA[:, db % 2, :, :], in_=wx_r[:, :, db * 128:(db + 1) * 128]), 16)
            if db == 0:
                wx_free = []
            block.sync(ld_wx)
            ldt = ld.n
            for w in range(4):
                def mm_t1(te, db=db, w=w, ldt=ldt, vct=xn2T_done, evt=vc.n):
                    te.wait_ge(ld.sem, ldt)
                    te.wait_ge(vc.sem, vct)
                    if db * 4 + w >= 2:
                        te.wait_ge(vc.sem, evt)
                    for cc in range(NCD):
                        mm = te.matmul(pP[:, w % 2, :], wslA[:, db % 2, cc, :],
                                       xn2T[:, cc, w * 512:(w + 1) * 512],
                                       start=(cc == 0), stop=(cc == NCD - 1))
                        if cc == NCD - 1:
                            tc.inc(mm)
                block.tensor(mm_t1)

                def ev_t1(v, db=db, w=w, tct=tc.n):
                    v.wait_ge(tc.sem, tct)
                    vc.inc(v.tensor_copy(t1[:, db, w * 512:(w + 1) * 512], pP[:, w % 2, :]))
                block.vector(ev_t1)
            wx_free.append(vc.n)
        t1_done = vc.n
        if DBG:
            def st_dbgt1(gp, vct=t1_done):
                gp.wait_ge(vc.sem, vct)
                st.inc(gp.dma_start(out=dbg_t1[:, :], in_=t1[:, 0, :]), 16)
            block.gpsimd(st_dbgt1)

        stl_hist = []
        for pc in range(NCD):
            def ld_ww(sy, pc=pc, t1mm=tc.n):
                if pc >= 2:
                    sy.wait_ge(tc.sem, ww_free[pc - 2])
                else:
                    sy.wait_ge(tc.sem, t1mm)
                ld.inc(sy.dma_start(out=wslA[:, pc % 2, :, :], in_=ww_r[:, :, pc * 128:(pc + 1) * 128]), 16)
            if pc == 0:
                ww_free = []
            block.sync(ld_ww)
            ldt = ld.n
            for w in range(4):
                WSL = slice(w * 512, (w + 1) * 512)
                def mm_pre(te, pc=pc, w=w, ldt=ldt, t1t=t1_done, vct=vc.n):
                    te.wait_ge(ld.sem, ldt)
                    te.wait_ge(vc.sem, t1t)
                    if pc * 4 + w >= 2:
                        te.wait_ge(vc.sem, vct)
                    for cc in range(NCD):
                        te.matmul(pP[:, w % 2, :], wslA[:, pc % 2, cc, :], t1[:, cc, WSL],
                                  start=(cc == 0), stop=False)
                    tc.inc(te.matmul(pP[:, w % 2, :], bwrow[0:1, pc * 128:(pc + 1) * 128],
                                     onesb[:, 0:512], start=False, stop=True))
                block.tensor(mm_pre)
                mmt = tc.n

                def s_abs(s, w=w, tct=mmt, vct=vc.n):
                    s.wait_ge(tc.sem, tct)
                    if vc.n:
                        s.wait_ge(vc.sem, vct)
                    sc.inc(s.activation(out=scr2[:, :], in_=pP[:, w % 2, :], func=AF.Abs,
                                        bias=0.0, scale=1.0))
                block.scalar(s_abs)

                def s_exp(s, w=w, sct=sc.n):
                    s.wait_ge(sc.sem, sct)
                    sc.inc(s.activation(out=scr2b[:, :], in_=scr2[:, :], func=AF.Exp,
                                        bias=0.0, scale=-1.0))
                block.scalar(s_exp)

                def v_relu(v, w=w, tct=mmt, sct=sc.n, stp=stl_hist[-1] if stl_hist else 0):
                    v.wait_ge(tc.sem, tct)
                    v.wait_ge(sc.sem, sct)
                    if w == 0 and stp:
                        v.wait_ge(st.sem, stp)
                    v.tensor_scalar(out=lfb[:, WSL], in0=pP[:, w % 2, :], scalar1=0.0,
                                    scalar2=None, op0=ALU.max, op1=ALU.bypass)
                    if DBG and pc == 0:
                        v.tensor_copy(dbgpre_sb[:, WSL], pP[:, w % 2, :])
                    vc.inc(v.tensor_scalar(out=scr2b[:, :], in0=scr2b[:, :], scalar1=1.0,
                                           scalar2=None, op0=ALU.add, op1=ALU.bypass))
                block.vector(v_relu)

                def s_ln(s, w=w, vct=vc.n):
                    s.wait_ge(vc.sem, vct)
                    sc.inc(s.activation(out=scr2[:, :], in_=scr2b[:, :], func=AF.Ln,
                                        bias=0.0, scale=1.0))
                block.scalar(s_ln)

                def v_l(v, w=w, sct=sc.n):
                    v.wait_ge(sc.sem, sct)
                    vc.inc(v.tensor_tensor(out=lfb[:, WSL], in0=lfb[:, WSL], in1=scr2[:, :], op=ALU.add))
                    v.wait_ge(vc.sem, vc.n)
                    vc.inc(v.tensor_scalar(out=lfb[:, WSL], in0=lfb[:, WSL], scalar1=-1.0,
                                           scalar2=-LAM, op0=ALU.mult, op1=ALU.max))
                block.vector(v_l)
            ww_free.append(tc.n)

            def st_l(gp, pc=pc, vct=vc.n, stl=st.n):
                gp.wait_ge(vc.sem, vct)
                if pc >= 1:
                    gp.wait_ge(st.sem, stl)
                st.inc(gp.dma_start(out=lscr[pc * 128:(pc + 1) * 128, :], in_=lfb[:]), 16)
            block.gpsimd(st_l)
            if DBG and pc == 0:
                def st_dbgpre(gp, vct=vc.n):
                    gp.wait_ge(vc.sem, vct)
                    st.inc(gp.dma_start(out=dbg_pre[:, :], in_=dbgpre_sb[:]), 16)
                block.gpsimd(st_dbgpre)
            stl_hist.append(st.n)
        l_stored = st.n

        for wnd in range(4):
            def ld_wv(sy, wnd=wnd, tct=tc.n):
                sy.wait_ge(tc.sem, tct)
                ld.inc(sy.dma_start(out=wv5[:, :, :], in_=wv_r[:, :, wnd * 512:(wnd + 1) * 512]), 16)
            block.sync(ld_wv)
            ldt = ld.n
            for it in range(NT):
                def mm_v(te, it=it, wnd=wnd, ldt=ldt, evt=vc.n):
                    te.wait_ge(ld.sem, ldt)
                    if wnd * NT + it >= 2:
                        te.wait_ge(vc.sem, evt)
                    for cc in range(NCD):
                        mm = te.matmul(pP[:, it % 2, :], xn2T[:, cc, it * 128:(it + 1) * 128],
                                       wv5[:, cc, :], start=(cc == 0), stop=(cc == NCD - 1))
                        if cc == NCD - 1:
                            tc.inc(mm)
                block.tensor(mm_v)

                def ev_v(v, it=it, tct=tc.n, stn=st.n):
                    v.wait_ge(tc.sem, tct)
                    v.wait_ge(st.sem, stn)
                    vc.inc(v.tensor_copy(yev[:, it % 2, :], pP[:, it % 2, :]))
                block.vector(ev_v)

                def st_v(gp, it=it, wnd=wnd, vct=vc.n):
                    gp.wait_ge(vc.sem, vct)
                    st.inc(gp.dma_start(out=vscr[it * 128:(it + 1) * 128, wnd * 512:(wnd + 1) * 512],
                                        in_=yev[:, it % 2, :]), 16)
                block.gpsimd(st_v)
        v_done_st = st.n
        esB.close()

        esC = ExitStack()
        rb = esC.enter_context(nc.sbuf_tensor("rb", [128, TPC], BF16))
        kb = esC.enter_context(nc.sbuf_tensor("kb", [128, TPC], BF16))
        rt_ = esC.enter_context(nc.sbuf_tensor("rt_", [128, TPC], BF16))
        kt_ = esC.enter_context(nc.sbuf_tensor("kt_", [128, TPC], BF16))
        kbar = esC.enter_context(nc.sbuf_tensor("kbar", [128, TPC], BF16))
        rring = esC.enter_context(nc.sbuf_tensor("rring", [128, TPC], BF16))
        ri = esC.enter_context(nc.sbuf_tensor("ri", [128, 2 * TPC], BF16))
        LE = esC.enter_context(nc.sbuf_tensor("LE", [128, NCH], F32))
        Bc = esC.enter_context(nc.sbuf_tensor("Bc", [128, NCH], F32))
        dec = esC.enter_context(nc.sbuf_tensor("dec", [128, NCH], F32))
        dec0 = esC.enter_context(nc.sbuf_tensor("dec0", [128, NCH], F32))
        Ssb = esC.enter_context(nc.sbuf_tensor("Ssb", [128, 128], BF16))
        Sf32 = esC.enter_context(nc.sbuf_tensor("Sf32", [128, HS1], F32))
        ktr = esC.enter_context(nc.sbuf_tensor("ktr", [128, NT, 128], BF16))
        vpad = esC.enter_context(nc.sbuf_tensor("vpad", [64, 2, CC, 128], BF16))
        kpad = esC.enter_context(nc.sbuf_tensor("kpad", [64, 2, CC, 128], BF16))
        amask = esC.enter_context(nc.sbuf_tensor("amask", [16, 256], BF16))
        maskraw = esC.enter_context(nc.sbuf_tensor("maskraw", [16, 256], F32))
        asb = esC.enter_context(nc.sbuf_tensor("asb", [64, 2, 256], BF16))
        e1f = esC.enter_context(nc.sbuf_tensor("e1f", [128, TPC], F32))

        def g_mask(gp):
            gp.wait_ge(vc.sem, prelim_done)
            gp.iota(maskraw[:, :], pattern=[[0, 16], [1, 16]], base=0,
                    channel_multiplier=0, allow_small_or_imprecise_dtypes=True)
            gc.inc(gp.engine_nop())
        block.gpsimd(g_mask)

        def v_mask(v):
            v.wait_ge(gc.sem, gc.n)
            v.memset(ri[:], 0.0)
            v.memset(vpad[:], 0.0)
            v.memset(kpad[:], 0.0)
            v.memset(asb[:], 0.0)
            vc.inc(v.tensor_scalar(out=amask[:, :], in0=maskraw[:, :], scalar1=rowi[0:16, 0:1],
                                   scalar2=None, op0=ALU.is_ge, op1=ALU.bypass))
        block.vector(v_mask)
        mask_done = vc.n

        wk_r = W_in["Wk"].rearrange("(cc p) n -> p cc n", p=128)
        wr_r = W_in["Wr"].rearrange("(cc p) n -> p cc n", p=128)
        yev_done = []
        string_hist, stkb_hist, stS_hist = [], [], []
        for pc in range(NCD):
            PS = slice(pc * 128, (pc + 1) * 128)
            def ld_wp(sy, pc=pc, lst=l_stored,
                      krp=kr_hist[-1] if pc else 0, scp=scan_hist[-1] if pc else 0):
                if pc >= 1:
                    sy.wait_ge(tc.sem, krp)
                    sy.wait_ge(vc.sem, scp)
                sy.wait_ge(st.sem, lst)
                ld.inc(sy.dma_start(out=wslA[:, 0, :, :], in_=wk_r[:, :, PS]), 16)
                ld.inc(sy.dma_start(out=wslA[:, 1, :, :], in_=wr_r[:, :, PS]), 16)
                ld.inc(sy.dma_start(out=lfb[:], in_=lscr[PS, :]), 16)
            if pc == 0:
                kr_hist, scan_hist = [], []
            block.sync(ld_wp)
            ldt = ld.n

            for w in range(4):
                WSL = slice(w * 512, (w + 1) * 512)
                def mm_kr(te, pc=pc, w=w, ldt=ldt, vct=vc.n, mk=mask_done):
                    te.wait_ge(ld.sem, ldt)
                    te.wait_ge(vc.sem, mk)
                    if not (pc == 0 and w == 0):
                        te.wait_ge(vc.sem, vct)
                    for cc in range(NCD):
                        te.matmul(pP[:, 0, :], wslA[:, 0, cc, :], xn2T[:, cc, WSL],
                                  start=(cc == 0), stop=(cc == NCD - 1))
                        mm = te.matmul(pP[:, 1, :], wslA[:, 1, cc, :], xn2T[:, cc, WSL],
                                       start=(cc == 0), stop=(cc == NCD - 1))
                        if cc == NCD - 1:
                            tc.inc(mm)
                block.tensor(mm_kr)

                def s_sig(s, w=w, tct=tc.n, vct=vc.n):
                    s.wait_ge(tc.sem, tct)
                    sc.inc(s.activation(out=rb[:, WSL], in_=pP[:, 1, :], func=AF.Sigmoid,
                                        bias=0.0, scale=1.0))
                block.scalar(s_sig)

                def v_k(v, w=w, tct=tc.n, sct=sc.n):
                    v.wait_ge(tc.sem, tct)
                    v.wait_ge(sc.sem, sct)
                    vc.inc(v.tensor_copy(kb[:, WSL], pP[:, 0, :]))
                block.vector(v_k)
            kr_done = vc.n
            kr_hist.append(tc.n)

            def v_scan(v, pc=pc, vct=kr_done, ldt=ldt):
                v.wait_ge(ld.sem, ldt)
                v.wait_ge(vc.sem, vct)
                vc.inc(v.tensor_tensor_scan(out=Lcb[:], data0=lfb[:], data1=lfb[:], initial=0.0,
                                            op0=ALU.add, op1=ALU.bypass))
                v.wait_ge(vc.sem, vc.n)
                vc.inc(v.tensor_copy(LE[:], Lcb[:].rearrange("p (c s) -> p c s", s=CC)[:, :, CC - 1].squeeze()))
                v.wait_ge(vc.sem, vc.n)
                v.memset(Bc[:, 0:1], 0.0)
                vc.inc(v.tensor_copy(Bc[:, 1:NCH], LE[:, 0:NCH - 1]))
                v.wait_ge(vc.sem, vc.n)
                vc.inc(v.engine_nop())
            block.vector(v_scan)
            scan_hist.append(vc.n)

            if DBG and pc == 0:
                def st_dbg1(gp, ldt=ldt, vct=vc.n):
                    gp.wait_ge(ld.sem, ldt)
                    gp.wait_ge(vc.sem, vct)
                    st.inc(gp.dma_start(out=dbg_l[:, :], in_=lfb[:]), 16)
                block.gpsimd(st_dbg1)

            def v_lrel(v, pc=pc):
                for c in range(NCH):
                    v.tensor_scalar(out=Lcb[:, c * CC:(c + 1) * CC], in0=Lcb[:, c * CC:(c + 1) * CC],
                                    scalar1=Bc[:, c:c + 1], scalar2=None,
                                    op0=ALU.subtract, op1=ALU.bypass)
                vc.inc(v.engine_nop())
            block.vector(v_lrel)
            lrel_done = vc.n

            if DBG and pc == 0:
                def st_dbg2(gp, vct=lrel_done):
                    gp.wait_ge(vc.sem, vct)
                    st.inc(gp.dma_start(out=dbg_Lc[:, :], in_=Lcb[:]), 16)
                block.gpsimd(st_dbg2)

            def s_exps(s, pc=pc, vct=lrel_done):
                s.wait_ge(vc.sem, vct)
                s.activation(out=e1f[:], in_=Lcb[:], func=AF.Exp, bias=0.0, scale=1.0)
                sc.inc(s.activation(out=kt_[:], in_=Lcb[:], func=AF.Exp, bias=0.0, scale=-1.0))
            block.scalar(s_exps)

            if DBG and pc == 0:
                def st_dbg3(gp, sct=sc.n):
                    gp.wait_ge(sc.sem, sct)
                    st.inc(gp.dma_start(out=dbg_e1[:, :], in_=e1f[:]), 16)
                block.gpsimd(st_dbg3)

            def v_prod(v, pc=pc, sct=sc.n):
                v.wait_ge(sc.sem, sct)
                v.tensor_tensor(out=rt_[:], in0=rb[:], in1=e1f[:], op=ALU.mult)
                vc.inc(v.tensor_tensor(out=kt_[:], in0=kb[:], in1=kt_[:], op=ALU.mult))
                v.wait_ge(vc.sem, vc.n)
                vc.inc(v.tensor_copy(dec[:], e1f[:].rearrange("p (c s) -> p c s", s=CC)[:, :, CC - 1].squeeze()))
                v.wait_ge(vc.sem, vc.n)
                vc.inc(v.tensor_copy(dec0[:], dec[:]))
                v.wait_ge(vc.sem, vc.n)
                for c in range(NCH):
                    v.tensor_scalar(out=kbar[:, c * CC:(c + 1) * CC], in0=kt_[:, c * CC:(c + 1) * CC],
                                    scalar1=dec[:, c:c + 1], scalar2=None, op0=ALU.mult, op1=ALU.bypass)
                riv = ri[:].rearrange("p (c g s) -> p c g s", g=2, s=CC)
                rtv = rt_[:].rearrange("p (c s) -> p c s", s=CC)
                vc.inc(v.engine_nop())
                v.wait_ge(vc.sem, vc.n)
                v.tensor_copy(riv[0:64, :, 0, :], rtv[0:64, :, :])
                vc.inc(v.tensor_copy(riv[64:128, :, 1, :], rtv[64:128, :, :]))
                v.wait_ge(vc.sem, vc.n)
                vc.inc(v.engine_nop())
            block.vector(v_prod)
            prod_done = vc.n

            def v_ring(v, pc=pc, vct=prod_done, srp=string_hist[-1] if pc else 0):
                if srp:
                    v.wait_ge(st.sem, srp)
                vc.inc(v.tensor_tensor_scan(out=scr2[:, 0:NCH], data0=dec[:], data1=dec[:],
                                            initial=1.0, op0=ALU.mult, op1=ALU.bypass))
                v.wait_ge(vc.sem, vc.n)
                v.memset(dec[:, 0:1], 1.0)
                vc.inc(v.tensor_copy(dec[:, 1:NCH], scr2[:, 0:NCH - 1]))
                v.wait_ge(vc.sem, vc.n)
                for c in range(NCH):
                    v.tensor_scalar(out=rring[:, c * CC:(c + 1) * CC], in0=rt_[:, c * CC:(c + 1) * CC],
                                    scalar1=dec[:, c:c + 1], scalar2=None, op0=ALU.mult, op1=ALU.bypass)
                vc.inc(v.engine_nop())
            block.vector(v_ring)
            ring_done = vc.n

            def st_ring(gp, pc=pc, vct=ring_done):
                gp.wait_ge(vc.sem, vct)
                st.inc(gp.dma_start(out=rT_out[PS, :], in_=rring[:]), 16)
                st.inc(gp.dma_start(out=Dr_out[PS, :], in_=scr2[:, NCH - 1:NCH]), 16)
            block.gpsimd(st_ring)
            string_hist.append(st.n)

            ktr_ev = []
            for tt in range(NT):
                def tr_kb(te, tt=tt, vct=prod_done, kbp=stkb_hist[-1] if stkb_hist else 0):
                    te.wait_ge(vc.sem, vct)
                    if tt == 0 and kbp:
                        te.wait_ge(st.sem, kbp)
                    if tt >= 1:
                        te.wait_ge(vc.sem, ktr_ev[tt - 1])
                    tc.inc(te.transpose(pst[:], kbar[:, tt * 128:(tt + 1) * 128], identity=identb[:]))
                block.tensor(tr_kb)

                def ev_kb(v, tt=tt, tct=tc.n):
                    v.wait_ge(tc.sem, tct)
                    vc.inc(v.tensor_copy(ktr[:, tt, :], pst[:]))
                block.vector(ev_kb)
                ktr_ev.append(vc.n)

            def st_kb(gp, pc=pc, vct=vc.n, rlp=None):
                gp.wait_ge(vc.sem, vct)
                if pc >= 2:
                    gp.wait_ge(ld.sem, rec_ld_last[pc - 2])
                st.inc(gp.dma_start(out=kscr[pc % 2].rearrange("(tt p) d -> p tt d", p=128),
                                    in_=ktr[:]), 16)
            if pc == 0:
                rec_ld_last = {}
            block.gpsimd(st_kb)
            kb_st = st.n
            stkb_hist.append(st.n)

            def v_S0(v, vct=prod_done):
                v.memset(Ssb[:], 0.0)
                vc.inc(v.engine_nop())
                v.wait_ge(vc.sem, vc.n)
            block.vector(v_S0)
            s_state = vc.n

            for cg in range(NG):
                def ld_rec(sy, pc=pc, cg=cg, kst=kb_st, vdn=v_done_st, mmt=tc.n):
                    sy.wait_ge(st.sem, max(kst, vdn))
                    if cg >= 2:
                        sy.wait_ge(tc.sem, grp_mm_done[cg - 2])
                    kk_ = kscr[pc % 2].rearrange("(c s) d -> s c d", s=CC)[:, cg * CC:(cg + 1) * CC, :]
                    ld.inc(sy.dma_start(out=kpad[0:16, cg % 2, :, :], in_=kk_), 16)
                    ld.inc(sy.dma_start(out=kpad[32:48, cg % 2, :, :], in_=kk_), 16)
                    vv_ = vscr.rearrange("(c s) d -> s c d", s=CC)[:, cg * CC:(cg + 1) * CC, :]
                    ld.inc(sy.dma_start(out=vpad[0:16, cg % 2, :, 0:64],
                                        in_=vv_[:, :, pc * 128:pc * 128 + 64]), 16)
                    ld.inc(sy.dma_start(out=vpad[32:48, cg % 2, :, 64:128],
                                        in_=vv_[:, :, pc * 128 + 64:pc * 128 + 128]), 16)
                if cg == 0:
                    grp_mm_done = {}
                block.sync(ld_rec)
                rec_ld = ld.n
                rec_ld_last[pc] = rec_ld

                for ci in range(CC):
                    c = cg * CC + ci
                    def mm_A(te, pc=pc, c=c, ci=ci, vct=prod_done, evt=vc.n):
                        te.wait_ge(vc.sem, vct)
                        if ci == 0:
                            te.wait_ge(vc.sem, evt)
                        tc.inc(te.matmul(pA[:, ci * 32:(ci + 1) * 32],
                                         kt_[:, c * CC:(c + 1) * CC],
                                         ri[:, c * 2 * CC:(c + 1) * 2 * CC],
                                         start=(ci == 0), stop=(ci == CC - 1)))
                    block.tensor(mm_A)

                def v_amask(v, cg=cg, tct=tc.n):
                    v.wait_ge(tc.sem, tct)
                    pAv = pA[:].rearrange("s (c g t) -> s c g t", g=2, t=CC)
                    mkv = amask[:].rearrange("s (c t) -> s c t", t=CC)
                    abv0 = asb[0:16, cg % 2, :].rearrange("s (c t) -> s c t", t=CC)
                    abv1 = asb[32:48, cg % 2, :].rearrange("s (c t) -> s c t", t=CC)
                    vc.inc(v.tensor_tensor(out=abv0, in0=pAv[:, :, 0, :].squeeze(), in1=mkv, op=ALU.mult))
                    v.wait_ge(vc.sem, vc.n)
                    vc.inc(v.tensor_tensor(out=abv1, in0=pAv[:, :, 1, :].squeeze(), in1=mkv, op=ALU.mult))
                block.vector(v_amask)
                if DBG and pc == 0:
                    def st_dbgA(gp, cg=cg, vct=vc.n):
                        gp.wait_ge(vc.sem, vct)
                        st.inc(gp.dma_start(out=dbg_A[:, cg, :], in_=asb[:, cg % 2, :]), 16)
                    block.gpsimd(st_dbgA)
                a_done = vc.n

                for ci in range(CC):
                    c = cg * CC + ci
                    def mm_chunk(te, pc=pc, c=c, ci=ci, cg=cg, at=a_done, rl=rec_ld):
                        te.wait_ge(vc.sem, at)
                        te.wait_ge(ld.sem, rl)
                        te.wait_ge(vc.sem, supd_done[c - 1] if c > 0 else s_state)
                        if c % 32 == 0 and len(yev_done) >= 1:
                            te.wait_ge(vc.sem, yev_done[-1])
                        ycol = slice((c % 32) * 16, (c % 32) * 16 + 16)
                        pyb = pY[:, 0, :]
                        first = (c % 32 == 0)
                        last = (c % 32 == 31)
                        te.matmul(pyb[:, ycol], Ssb[:, :],
                                  rt_[:, c * CC:(c + 1) * CC], start=first, stop=False)
                        te.matmul(pyb[:, ycol], vpad[:, cg % 2, ci, :],
                                  asb[:, cg % 2, ci * CC:(ci + 1) * CC],
                                  start=False, stop=last)
                        tc.inc(te.matmul(pM[:], kpad[:, cg % 2, ci, :], vpad[:, cg % 2, ci, :],
                                         start=True, stop=True))
                    block.tensor(mm_chunk)

                    def v_supd(v, c=c, tct=tc.n):
                        v.wait_ge(tc.sem, tct)
                        if c > 0:
                            v.wait_ge(vc.sem, supd_done[c - 1])
                        v.scalar_tensor_tensor(out=Ssb[0:64, 0:64], in0=Ssb[0:64, 0:64],
                                               scalar=dec0[0:64, c:c + 1], op0=ALU.mult,
                                               op1=ALU.add, in1=pM[0:64, 0:64])
                        vc.inc(v.scalar_tensor_tensor(out=Ssb[64:128, 64:128], in0=Ssb[64:128, 64:128],
                                                      scalar=dec0[64:128, c:c + 1], op0=ALU.mult,
                                                      op1=ALU.add, in1=pM[64:128, 64:128]))
                    block.vector(v_supd)
                    if c == 0:
                        supd_done = {}
                    supd_done[c] = vc.n
                grp_mm_done[cg] = tc.n

                if cg % 2 == 1:
                    qw = cg // 2
                    def v_yev(v, qw=qw, tct=tc.n, stn=st.n):
                        v.wait_ge(tc.sem, tct)
                        v.wait_ge(st.sem, stn)
                        vc.inc(v.tensor_copy(yev[:, qw % 2, :], pY[:, 0, :]))
                    block.vector(v_yev)
                    yev_done.append(vc.n)

                    def st_y(gp, pc=pc, qw=qw, vct=vc.n):
                        gp.wait_ge(vc.sem, vct)
                        st.inc(gp.dma_start(out=y_out[PS, qw * 512:(qw + 1) * 512],
                                            in_=yev[:, qw % 2, :]), 16)
                    block.gpsimd(st_y)

            def v_Sout(v, vct=supd_done[NCH - 1], ssp=stS_hist[-1] if stS_hist else 0):
                v.wait_ge(vc.sem, vct)
                if ssp:
                    v.wait_ge(st.sem, ssp)
                v.tensor_copy(Sf32[0:64, :], Ssb[0:64, 0:64])
                vc.inc(v.tensor_copy(Sf32[64:128, :], Ssb[64:128, 64:128]))
            block.vector(v_Sout)

            def st_S(gp, pc=pc, vct=vc.n):
                gp.wait_ge(vc.sem, vct)
                st.inc(gp.dma_start(out=Sr_out[PS, :], in_=Sf32[:]), 16)
            block.gpsimd(st_S)
            stS_hist.append(st.n)
        esC.close()

        block.sync(lambda sy: sy.wait_ge(st.sem, st.n))
    return nc


def build_l1fix():
    import os as _os
    nc = bass.Bass("TRN2", target_bir_lowering=False)
    es = ExitStack()
    with es:
        y_in = nc.dram_tensor("y_feat", [D, TPC], BF16, kind="ExternalInput").ap()
        rT_in = nc.dram_tensor("rT", [D, TPC], BF16, kind="ExternalInput").ap()
        wout_in = nc.dram_tensor("Wout", [D, D], BF16, kind="ExternalInput").ap()
        Sp = [nc.dram_tensor(f"Sp{i}", [D, HS1], F32, kind="ExternalInput").ap() for i in range(3)]
        Dp = [nc.dram_tensor(f"Dp{i}", [128, NCD], F32, kind="ExternalInput").ap() for i in range(3)]
        s1_out = nc.dram_tensor("s1", [TPC, D], BF16, kind="ExternalOutput").ap()
        if _os.environ.get("L1EDBG"):
            dbg_yb = nc.dram_tensor("dbg_yb", [128, 2, TPC], BF16, kind="ExternalOutput").ap()

        yb = es.enter_context(nc.sbuf_tensor("yb", [128, NCD, TPC], BF16))
        rTb = es.enter_context(nc.sbuf_tensor("rTb", [128, TPC], BF16))
        Sin = es.enter_context(nc.sbuf_tensor("Sin", [128, NCD, HS1], F32))
        Din = es.enter_context(nc.sbuf_tensor("Din", [128, 3, NCD], F32))
        Spb = es.enter_context(nc.sbuf_tensor("Spb", [128, 2, NCD, HS1], F32))
        Spad = es.enter_context(nc.sbuf_tensor("Spad", [128, 128], BF16))
        wob = es.enter_context(nc.sbuf_tensor("wob", [128, NCD, D], BF16))
        otile = es.enter_context(nc.sbuf_tensor("otile", [128, 2, 512], BF16))
        pY = es.enter_context(nc.psum_tensor("pY", [128, 2, 512], F32))
        pO = es.enter_context(nc.psum_tensor("pO", [128, 2, 512], F32))

        ld = Ctr(es.enter_context(nc.semaphore("ld")))
        vc = Ctr(es.enter_context(nc.semaphore("vc")))
        tc = Ctr(es.enter_context(nc.semaphore("tc")))
        st = Ctr(es.enter_context(nc.semaphore("st")))
        gc = Ctr(es.enter_context(nc.semaphore("gc")))
        block = es.enter_context(nc.Block())

        block.gpsimd(lambda gp: gc.inc(gp.engine_nop()))

        y_r = y_in.rearrange("(cc p) n -> p cc n", p=128)
        def ld0(sy):
            ld.inc(sy.dma_start(out=yb[:], in_=y_r), 16)
            dsts = [Spb[:, 0, :, :], Spb[:, 1, :, :], Sin[:, :, :]]
            for i in range(3):
                ld.inc(sy.dma_start(out=dsts[i], in_=Sp[i].rearrange("(cc p) j -> p cc j", p=128)), 16)
                ld.inc(sy.dma_start(out=Din[:, i, :], in_=Dp[i][:, :]), 16)
        block.sync(ld0)
        ld_sum = ld.n

        if _os.environ.get("L1EDBG"):
            def st_dbgy(gp):
                gp.wait_ge(ld.sem, ld_sum)
                st.inc(gp.dma_start(out=dbg_yb[:, 0, :], in_=yb[:, 1, :]), 16)
                st.inc(gp.dma_start(out=dbg_yb[:, 1, :], in_=yb[:, 2, :]), 16)
            block.gpsimd(st_dbgy)

        def v_fold(v):
            v.wait_ge(gc.sem, gc.n)
            v.wait_ge(ld.sem, ld_sum)
            for cc in range(NCD):
                v.scalar_tensor_tensor(out=Spb[:, 1, cc, :], in0=Spb[:, 0, cc, :],
                                       scalar=Din[:, 1, cc:cc + 1], op0=ALU.mult,
                                       op1=ALU.add, in1=Spb[:, 1, cc, :])
            vc.inc(v.engine_nop())
            v.wait_ge(vc.sem, vc.n)
            for cc in range(NCD):
                v.scalar_tensor_tensor(out=Sin[:, cc, :], in0=Spb[:, 1, cc, :],
                                       scalar=Din[:, 2, cc:cc + 1], op0=ALU.mult,
                                       op1=ALU.add, in1=Sin[:, cc, :])
            vc.inc(v.engine_nop())
        block.vector(v_fold)
        fold_done = vc.n

        for pc in range(NCD):
            PS = slice(pc * 128, (pc + 1) * 128)
            def ld_p(sy, pc=pc, tct=tc.n):
                if pc >= 1:
                    sy.wait_ge(tc.sem, pair_mm[pc - 1])
                ld.inc(sy.dma_start(out=rTb[:], in_=rT_in[PS, :]), 16)
            if pc == 0:
                pair_mm = []
            block.sync(ld_p)
            ldt = ld.n

            def v_spad(v, pc=pc, vct=fold_done, tct=tc.n):
                v.wait_ge(vc.sem, vct)
                v.wait_ge(tc.sem, tct)
                v.memset(Spad[:], 0.0)
                vc.inc(v.engine_nop())
                v.wait_ge(vc.sem, vc.n)
                v.tensor_copy(Spad[0:64, 0:64], Sin[0:64, pc, :])
                vc.inc(v.tensor_copy(Spad[64:128, 64:128], Sin[64:128, pc, :]))
            block.vector(v_spad)
            spad_done = vc.n

            for tw in range(4):
                TS_ = slice(tw * 512, (tw + 1) * 512)
                def mm_corr(te, pc=pc, tw=tw, ldt=ldt, vct=spad_done, evt=vc.n):
                    te.wait_ge(ld.sem, ldt)
                    te.wait_ge(vc.sem, vct)
                    if pc * 4 + tw >= 2:
                        te.wait_ge(vc.sem, evt)
                    if _os.environ.get("L1ENOCORR"):
                        tc.inc(te.matmul(pY[:, tw % 2, 0:128], Spad[:], Spad[:, 0:128],
                                         start=True, stop=True))
                    else:
                        tc.inc(te.matmul(pY[:, tw % 2, :], Spad[:], rTb[:, TS_],
                                         start=True, stop=True))
                block.tensor(mm_corr)

                def v_ycomb(v, pc=pc, tw=tw, tct=tc.n):
                    v.wait_ge(tc.sem, tct)
                    if _os.environ.get("L1ENOCOMB"):
                        vc.inc(v.engine_nop())
                    else:
                        vc.inc(v.tensor_tensor(out=yb[:, pc, TS_], in0=yb[:, pc, TS_],
                                               in1=pY[:, tw % 2, :], op=ALU.add))
                block.vector(v_ycomb)
            pair_mm.append(tc.n)
        y_done = vc.n

        wo_r = wout_in.rearrange("(cc p) n -> p cc n", p=128)
        block.sync(lambda sy: ld.inc(sy.dma_start(out=wob[:], in_=wo_r), 16))
        ldt_wo = ld.n
        for dw in range(4):
            for it in range(NT):
                def mm_o(te, it=it, dw=dw, ldt=ldt_wo, vct=y_done, evt=vc.n):
                    te.wait_ge(ld.sem, ldt)
                    te.wait_ge(vc.sem, vct)
                    slot = dw * NT + it
                    if slot >= 2:
                        te.wait_ge(vc.sem, evt)
                    for cc in range(NCD):
                        mm = te.matmul(pO[:, it % 2, :], yb[:, cc, it * 128:(it + 1) * 128],
                                       wob[:, cc, dw * 512:(dw + 1) * 512],
                                       start=(cc == 0), stop=(cc == NCD - 1))
                        if cc == NCD - 1:
                            tc.inc(mm)
                block.tensor(mm_o)

                def ev_o(v, it=it, tct=tc.n, stn=st.n):
                    v.wait_ge(tc.sem, tct)
                    v.wait_ge(st.sem, stn)
                    vc.inc(v.tensor_copy(otile[:, it % 2, :], pO[:, it % 2, :]))
                block.vector(ev_o)

                def st_o(gp, it=it, dw=dw, vct=vc.n):
                    gp.wait_ge(vc.sem, vct)
                    st.inc(gp.dma_start(out=s1_out[it * 128:(it + 1) * 128, dw * 512:(dw + 1) * 512],
                                        in_=otile[:, it % 2, :]), 16)
                block.gpsimd(st_o)

        block.sync(lambda sy: sy.wait_ge(st.sem, st.n))
    return nc


def build_moe():
    nc = bass.Bass("TRN2", target_bir_lowering=False)
    es = ExitStack()
    NSB = ISH // 128
    NEB = EI // 128
    with es:
        xn1_in = nc.dram_tensor("xn1", [TPC, D], BF16, kind="ExternalInput").ap()
        x1_in = nc.dram_tensor("x1", [TPC, D], BF16, kind="ExternalInput").ap()
        s1_in = nc.dram_tensor("s1", [TPC, D], BF16, kind="ExternalInput").ap()
        gw_in = nc.dram_tensor("gwT", [D, NE], BF16, kind="ExternalInput").ap()
        sg_in = nc.dram_tensor("sgT", [D, ISH], BF16, kind="ExternalInput").ap()
        su_in = nc.dram_tensor("suT", [D, ISH], BF16, kind="ExternalInput").ap()
        sd_in = nc.dram_tensor("sdT", [ISH, D], BF16, kind="ExternalInput").ap()
        eg_in = [nc.dram_tensor(f"eg{e}T", [D, EI], BF16, kind="ExternalInput").ap() for e in range(NE)]
        eu_in = [nc.dram_tensor(f"eu{e}T", [D, EI], BF16, kind="ExternalInput").ap() for e in range(NE)]
        ed_in = [nc.dram_tensor(f"ed{e}T", [EI, D], BF16, kind="ExternalInput").ap() for e in range(NE)]
        out_o = nc.dram_tensor("out", [TPC, D], F32, kind="ExternalOutput").ap()

        xn1b = es.enter_context(nc.sbuf_tensor("xn1b", [128, D], BF16))
        xnT = es.enter_context(nc.sbuf_tensor("xnT", [128, NCD, TPC], BF16))
        acc = es.enter_context(nc.sbuf_tensor("acc", [128, 4, D], F32))
        hgb = es.enter_context(nc.sbuf_tensor("hgb", [128, NSB, 512], BF16))
        wgu = es.enter_context(nc.sbuf_tensor("wgu", [128, 2, 2, NCD, 128], BF16))
        dwb = es.enter_context(nc.sbuf_tensor("dwb", [128, NSB, 512], BF16))
        gwb = es.enter_context(nc.sbuf_tensor("gwb", [128, NCD, NE], BF16))
        silub = es.enter_context(nc.sbuf_tensor("silub", [128, 512], BF16))
        ceb = es.enter_context(nc.sbuf_tensor("ceb", [128, NT, NE], F32))
        rtmp = es.enter_context(nc.sbuf_tensor("rtmp", [128, 10, NE], F32))
        rtm2 = es.enter_context(nc.sbuf_tensor("rtm2", [128, 8], F32))
        ridx = es.enter_context(nc.sbuf_tensor("ridx", [128, 8], mybir.dt.uint32))
        x1b = es.enter_context(nc.sbuf_tensor("x1b", [128, 2, 512], BF16))
        s1b = es.enter_context(nc.sbuf_tensor("s1b", [128, 2, 512], BF16))
        gsil = es.enter_context(nc.sbuf_tensor("gsil", [128, 2, 512], F32))
        outb = es.enter_context(nc.sbuf_tensor("outb", [128, 2, 512], F32))

        pG = es.enter_context(nc.psum_tensor("pG", [128, 512], F32))
        pU = es.enter_context(nc.psum_tensor("pU", [128, 512], F32))
        pD = es.enter_context(nc.psum_tensor("pD", [128, 2, 512], F32))
        pR = es.enter_context(nc.psum_tensor("pR", [128, NE], F32))
        pst = es.enter_context(nc.psum_tensor("pstT", [128, 128], BF16))

        ld = Ctr(es.enter_context(nc.semaphore("ld")))
        vc = Ctr(es.enter_context(nc.semaphore("vc")))
        tc = Ctr(es.enter_context(nc.semaphore("tc")))
        sc = Ctr(es.enter_context(nc.semaphore("sc")))
        st = Ctr(es.enter_context(nc.semaphore("st")))
        gc = Ctr(es.enter_context(nc.semaphore("gc")))
        block = es.enter_context(nc.Block())

        identf, identb, rowi, coli, const_fin = setup_consts(nc, es, block, gc)
        block.vector(lambda v: (v.wait_ge(gc.sem, gc.n), const_fin(v, vc)))

        gw_r = gw_in.rearrange("(cc p) e -> p cc e", p=128)
        block.sync(lambda sy: ld.inc(sy.dma_start(out=gwb[:], in_=gw_r), 16))
        gw_ld = ld.n

        xn_evac = []
        for it in range(NT):
            def ld_x(sy, it=it, vct=vc.n):
                if it >= 1:
                    sy.wait_ge(vc.sem, xn_done[it - 1])
                ld.inc(sy.dma_start(out=xn1b[:], in_=xn1_in[it * 128:(it + 1) * 128, :]), 16)
            if it == 0:
                xn_done = []
            block.sync(ld_x)
            ldt = ld.n

            for cc in range(NCD):
                gti = it * NCD + cc
                def tr_x(te, it=it, cc=cc, ldt=ldt, gti=gti):
                    te.wait_ge(ld.sem, ldt)
                    if gti >= 1:
                        te.wait_ge(vc.sem, xn_evac[gti - 1])
                    tc.inc(te.transpose(pst[:], xn1b[:, cc * 128:(cc + 1) * 128], identity=identb[:]))
                block.tensor(tr_x)

                def ev_x(v, it=it, cc=cc, tct=tc.n):
                    v.wait_ge(tc.sem, tct)
                    vc.inc(v.tensor_copy(xnT[:, cc, it * 128:(it + 1) * 128], pst[:]))
                block.vector(ev_x)
                xn_evac.append(vc.n)
            xn_done.append(vc.n)

            def mm_r(te, it=it, gwt=gw_ld, vct=vc.n):
                te.wait_ge(ld.sem, gwt)
                te.wait_ge(vc.sem, vct)
                for cc in range(NCD):
                    mm = te.matmul(pR[:], xnT[:, cc, it * 128:(it + 1) * 128], gwb[:, cc, :],
                                   start=(cc == 0), stop=(cc == NCD - 1))
                    if cc == NCD - 1:
                        tc.inc(mm)
            block.tensor(mm_r)

            def v_sm(v, it=it, tct=tc.n):
                v.wait_ge(tc.sem, tct)
                v.tensor_reduce(out=rtmp[:, 0, 0:1], in_=pR[:], op=ALU.max, axis=mybir.AxisListType.X)
                vc.inc(v.tensor_scalar(out=rtmp[:, 1, :], in0=pR[:], scalar1=rtmp[:, 0, 0:1],
                                       scalar2=None, op0=ALU.subtract, op1=ALU.bypass))
            block.vector(v_sm)

            def s_exp(s, it=it, vct=vc.n):
                s.wait_ge(vc.sem, vct)
                sc.inc(s.activation(out=rtmp[:, 2, :], in_=rtmp[:, 1, :], func=AF.Exp,
                                    bias=0.0, scale=1.0))
            block.scalar(s_exp)

            def v_ce(v, it=it, sct=sc.n):
                v.wait_ge(sc.sem, sct)
                v.tensor_reduce(out=rtmp[:, 3, 0:1], in_=rtmp[:, 2, :], op=ALU.add,
                                axis=mybir.AxisListType.X)
                vc.inc(v.engine_nop())
                v.wait_ge(vc.sem, vc.n)
                v.reciprocal(out=rtmp[:, 3, 0:1], in_=rtmp[:, 3, 0:1])
                vc.inc(v.engine_nop())
                v.wait_ge(vc.sem, vc.n)
                v.tensor_scalar(out=rtmp[:, 4, :], in0=rtmp[:, 2, :], scalar1=rtmp[:, 3, 0:1],
                                scalar2=None, op0=ALU.mult, op1=ALU.bypass)
                vc.inc(v.engine_nop())
                v.wait_ge(vc.sem, vc.n)
                v.max_with_indices(rtm2[:, :], ridx[:, :], rtmp[:, 4, :])
                vc.inc(v.engine_nop())
                v.wait_ge(vc.sem, vc.n)
                v.tensor_tensor(out=rtmp[:, 5, 0:1], in0=rtm2[:, 0:1], in1=rtm2[:, 1:2], op=ALU.add)
                vc.inc(v.engine_nop())
                v.wait_ge(vc.sem, vc.n)
                v.tensor_scalar(out=rtmp[:, 5, 0:1], in0=rtmp[:, 5, 0:1], scalar1=1e-20,
                                scalar2=None, op0=ALU.add, op1=ALU.bypass)
                vc.inc(v.engine_nop())
                v.wait_ge(vc.sem, vc.n)
                v.reciprocal(out=rtmp[:, 5, 0:1], in_=rtmp[:, 5, 0:1])
                vc.inc(v.engine_nop())
                v.wait_ge(vc.sem, vc.n)
                v.tensor_scalar(out=rtmp[:, 6, :], in0=rtmp[:, 4, :], scalar1=rtm2[:, 0:1],
                                scalar2=None, op0=ALU.is_equal, op1=ALU.bypass)
                v.tensor_scalar(out=rtmp[:, 7, :], in0=rtmp[:, 4, :], scalar1=rtm2[:, 1:2],
                                scalar2=None, op0=ALU.is_equal, op1=ALU.bypass)
                vc.inc(v.engine_nop())
                v.wait_ge(vc.sem, vc.n)
                v.tensor_scalar(out=rtmp[:, 8, :], in0=rtmp[:, 6, :], scalar1=rtm2[:, 0:1],
                                scalar2=None, op0=ALU.mult, op1=ALU.bypass)
                v.tensor_scalar(out=rtmp[:, 9, :], in0=rtmp[:, 7, :], scalar1=rtm2[:, 1:2],
                                scalar2=None, op0=ALU.mult, op1=ALU.bypass)
                vc.inc(v.engine_nop())
                v.wait_ge(vc.sem, vc.n)
                v.tensor_tensor(out=rtmp[:, 8, :], in0=rtmp[:, 8, :], in1=rtmp[:, 9, :], op=ALU.add)
                vc.inc(v.engine_nop())
                v.wait_ge(vc.sem, vc.n)
                vc.inc(v.tensor_scalar(out=ceb[:, it, :], in0=rtmp[:, 8, :], scalar1=rtmp[:, 5, 0:1],
                                       scalar2=None, op0=ALU.mult, op1=ALU.bypass))
            block.vector(v_ce)
        router_done = vc.n

        units = [("s", None)] + [("e", e) for e in range(NE)]
        for q in range(4):
            QT = slice(q * 512, (q + 1) * 512)
            for kindu, e in units:
                nib = NSB if kindu == "s" else NEB
                gu_src = (sg_in, su_in) if kindu == "s" else (eg_in[e], eu_in[e])
                d_src = sd_in if kindu == "s" else ed_in[e]
                g_r = gu_src[0].rearrange("(cc p) n -> p cc n", p=128)
                u_r = gu_src[1].rearrange("(cc p) n -> p cc n", p=128)
                d_r = d_src.rearrange("(ib p) n -> p ib n", p=128)
                for ib in range(nib):
                    def ld_gu(sy, ib=ib, g_r=g_r, u_r=u_r, vct=vc.n):
                        slotw = len(wgu_free)
                        if slotw >= 2:
                            sy.wait_ge(vc.sem, wgu_free[slotw - 2])
                        sl = slice(ib * 128, (ib + 1) * 128)
                        ld.inc(sy.dma_start(out=wgu[:, ib % 2, 0, :, :], in_=g_r[:, :, sl]), 16)
                        ld.inc(sy.dma_start(out=wgu[:, ib % 2, 1, :, :], in_=u_r[:, :, sl]), 16)
                    if q == 0 and kindu == "s" and ib == 0:
                        wgu_free = []
                    block.sync(ld_gu)
                    ldt = ld.n

                    def mm_gu(te, ib=ib, q=q, ldt=ldt, rdt=router_done, vct=vc.n):
                        te.wait_ge(ld.sem, ldt)
                        te.wait_ge(vc.sem, max(rdt, vct))
                        for cc in range(NCD):
                            te.matmul(pG[:], wgu[:, ib % 2, 0, cc, :], xnT[:, cc, QT],
                                      start=(cc == 0), stop=(cc == NCD - 1))
                            mm = te.matmul(pU[:], wgu[:, ib % 2, 1, cc, :], xnT[:, cc, QT],
                                           start=(cc == 0), stop=(cc == NCD - 1))
                            if cc == NCD - 1:
                                tc.inc(mm)
                    block.tensor(mm_gu)

                    def s_sil(s, ib=ib, tct=tc.n):
                        s.wait_ge(tc.sem, tct)
                        sc.inc(s.activation(out=silub[:], in_=pG[:], func=AF.Silu, bias=0.0, scale=1.0))
                    block.scalar(s_sil)

                    def v_h(v, ib=ib, sct=sc.n):
                        v.wait_ge(sc.sem, sct)
                        vc.inc(v.tensor_tensor(out=hgb[:, ib, :], in0=silub[:], in1=pU[:], op=ALU.mult))
                    block.vector(v_h)
                    wgu_free.append(vc.n)
                h_done = vc.n

                for dw in range(4):
                    def ld_d(sy, dw=dw, d_r=d_r, nib=nib, tct=tc.n):
                        sy.wait_ge(tc.sem, tct)
                        ld.inc(sy.dma_start(out=dwb[:, 0:nib, :],
                                            in_=d_r[:, :, dw * 512:(dw + 1) * 512]), 16)
                    block.sync(ld_d)
                    ldt = ld.n

                    for ts4 in range(4):
                        it = q * 4 + ts4
                        def mm_d(te, ts4=ts4, dw=dw, nib=nib, ldt=ldt, hct=h_done, vct=vc.n):
                            te.wait_ge(ld.sem, ldt)
                            te.wait_ge(vc.sem, max(hct, vct))
                            for ic in range(nib):
                                mm = te.matmul(pD[:, ts4 % 2, :], hgb[:, ic, ts4 * 128:(ts4 + 1) * 128],
                                               dwb[:, ic, :], start=(ic == 0), stop=(ic == nib - 1))
                                if ic == nib - 1:
                                    tc.inc(mm)
                        block.tensor(mm_d)

                        def v_acc(v, it=it, ts4=ts4, dw=dw, kindu=kindu, e=e, tct=tc.n):
                            v.wait_ge(tc.sem, tct)
                            dst = acc[:, ts4, dw * 512:(dw + 1) * 512]
                            if kindu == "s":
                                vc.inc(v.tensor_copy(dst, pD[:, ts4 % 2, :]))
                            else:
                                vc.inc(v.scalar_tensor_tensor(out=dst, in0=pD[:, ts4 % 2, :],
                                                              scalar=ceb[:, it, e:e + 1],
                                                              op0=ALU.mult, op1=ALU.add, in1=dst))
                        block.vector(v_acc)

            for ts4 in range(4):
                it = q * 4 + ts4
                for dw in range(4):
                    slot = ts4 * 4 + dw
                    def ld_xs(sy, it=it, dw=dw, slot=slot, vct=vc.n):
                        if len(comb_done) >= 2:
                            sy.wait_ge(vc.sem, comb_done[-2])
                        ld.inc(sy.dma_start(out=x1b[:, slot % 2, :],
                                            in_=x1_in[it * 128:(it + 1) * 128, dw * 512:(dw + 1) * 512]), 16)
                        ld.inc(sy.dma_start(out=s1b[:, slot % 2, :],
                                            in_=s1_in[it * 128:(it + 1) * 128, dw * 512:(dw + 1) * 512]), 16)
                    if q == 0 and ts4 == 0 and dw == 0:
                        comb_done = []
                    block.sync(ld_xs)
                    ldt = ld.n

                    def s_gsil(s, ts4=ts4, dw=dw, slot=slot, vct=vc.n, sct=sc.n):
                        s.wait_ge(vc.sem, vct)
                        sc.inc(s.activation(out=gsil[:, slot % 2, :], in_=acc[:, ts4, dw * 512:(dw + 1) * 512],
                                            func=AF.Silu, bias=0.0, scale=1.0))
                    block.scalar(s_gsil)

                    def v_out(v, slot=slot, ldt=ldt, sct=sc.n, stn=st.n):
                        v.wait_ge(sc.sem, sct)
                        v.wait_ge(ld.sem, ldt)
                        if stn:
                            v.wait_ge(st.sem, stn)
                        v.tensor_tensor(out=outb[:, slot % 2, :], in0=gsil[:, slot % 2, :],
                                        in1=s1b[:, slot % 2, :], op=ALU.mult)
                        vc.inc(v.engine_nop())
                        v.wait_ge(vc.sem, vc.n)
                        vc.inc(v.tensor_tensor(out=outb[:, slot % 2, :], in0=outb[:, slot % 2, :],
                                               in1=x1b[:, slot % 2, :], op=ALU.add))
                    block.vector(v_out)
                    comb_done.append(vc.n)

                    def st_out(gp, it=it, dw=dw, slot=slot, vct=vc.n):
                        gp.wait_ge(vc.sem, vct)
                        st.inc(gp.dma_start(out=out_o[it * 128:(it + 1) * 128, dw * 512:(dw + 1) * 512],
                                            in_=outb[:, slot % 2, :]), 16)
                    block.gpsimd(st_out)

        block.sync(lambda sy: sy.wait_ge(st.sem, st.n))
    return nc


def kernel(**inputs):
    W = run_prep(inputs)
    x = inputs["L0_attn_x"].astype(np.float32)

    nc = build_l0front()
    in_maps = []
    for c in range(NCORES):
        b, q = c // 4, c % 4
        in_maps.append({
            "x": np.ascontiguousarray(x[b, q * TPC:(q + 1) * TPC]),
            "WQpT": W["WQpT"], "WKpT": W["WKpT"], "WVpT": W["WVpT"],
            "posrow": np.arange(q * TPC, (q + 1) * TPC, dtype=np.float32).reshape(1, TPC),
        })
    r2 = run_bass_kernel_spmd(nc, in_maps, core_ids=list(range(NCORES))).results

    kT_full = {b: np.ascontiguousarray(np.concatenate(
        [np.asarray(r2[b * 4 + q]["kT"]) for q in range(4)], axis=2)) for b in range(2)}
    v_full = {b: np.ascontiguousarray(np.concatenate(
        [np.asarray(r2[b * 4 + q]["vtok"]) for q in range(4)], axis=1)) for b in range(2)}

    nc = build_l0attn()
    in_maps = [{"qT": np.asarray(r2[c]["qT"]), "kTf": kT_full[c // 4],
                "vf": v_full[c // 4], "WOT": W["WOT"]} for c in range(NCORES)]
    r3 = run_bass_kernel_spmd(nc, in_maps, core_ids=list(range(NCORES))).results

    nc = build_l0ffn()
    in_maps = []
    for c in range(NCORES):
        b, q = c // 4, c % 4
        in_maps.append({"x": np.ascontiguousarray(x[b, q * TPC:(q + 1) * TPC]),
                        "xnTo": np.asarray(r2[c]["xnTo"]), "s_att": np.asarray(r3[c]["s_att"]),
                        "ggT": W["ggT"], "guT": W["guT"], "gdT": W["gdT"]})
    r4 = run_bass_kernel_spmd(nc, in_maps, core_ids=list(range(NCORES))).results

    nc = build_l1front()
    gam = np.ascontiguousarray(inputs["L1_attn_gamma"].reshape(NCD, 128).astype(np.float32))
    bet = np.ascontiguousarray(inputs["L1_attn_beta"].reshape(NCD, 128).astype(np.float32))
    bw2 = np.ascontiguousarray(inputs["L1_attn_bw"].reshape(NCD, 128).astype(np.float32))
    bwf = np.ascontiguousarray(inputs["L1_attn_bw"].reshape(1, D).astype(np.float32))
    in_maps = []
    for c in range(NCORES):
        m = {"x1": np.asarray(r4[c]["x1"]), "gamma": gam, "beta": bet, "bw": bw2, "bwf": bwf}
        for k in ["Wx", "Ww", "Wk", "Wv", "Wr"]:
            m[k] = W[k]
        in_maps.append(m)
    r5 = run_bass_kernel_spmd(nc, in_maps, core_ids=list(range(NCORES))).results

    nc = build_l1fix()
    ident_S = np.zeros((D, HS1), np.float32)
    ident_D = np.ones((128, NCD), np.float32)
    in_maps = []
    for c in range(NCORES):
        b, q = c // 4, c % 4
        m = {"y_feat": np.asarray(r5[c]["y_feat"]), "rT": np.asarray(r5[c]["rT"]),
             "Wout": W["Wout"]}
        preds = [b * 4 + qq for qq in range(q)]
        preds = [None] * (3 - len(preds)) + preds
        for i, p in enumerate(preds):
            if p is None:
                m[f"Sp{i}"] = ident_S
                m[f"Dp{i}"] = ident_D
            else:
                m[f"Sp{i}"] = np.ascontiguousarray(np.asarray(r5[p]["Srange"], dtype=np.float32))
                m[f"Dp{i}"] = np.ascontiguousarray(
                    np.asarray(r5[p]["Drange"], dtype=np.float32).reshape(NCD, 128).T)
        in_maps.append(m)
    r6 = run_bass_kernel_spmd(nc, in_maps, core_ids=list(range(NCORES))).results

    nc = build_moe()
    in_maps = []
    for c in range(NCORES):
        m = {"xn1": np.asarray(r5[c]["xn1"]), "x1": np.asarray(r4[c]["x1"]),
             "s1": np.asarray(r6[c]["s1"]), "gwT": W["gwT"], "sgT": W["sgT"],
             "suT": W["suT"], "sdT": W["sdT"]}
        for e in range(NE):
            m[f"eg{e}T"] = W[f"eg{e}T"]
            m[f"eu{e}T"] = W[f"eu{e}T"]
            m[f"ed{e}T"] = W[f"ed{e}T"]
        in_maps.append(m)
    r7 = run_bass_kernel_spmd(nc, in_maps, core_ids=list(range(NCORES))).results

    out = np.empty((B, S, D), np.float32)
    for c in range(NCORES):
        b, q = c // 4, c % 4
        out[b, q * TPC:(q + 1) * TPC] = np.asarray(r7[c]["out"], np.float32)
    return out


# version 05 (final)
# speedup vs baseline: 1.2950x; 1.2950x over previous
import numpy as np
import ml_dtypes
import concourse.bass as bass
import concourse.mybir as mybir
from concourse.bass_utils import run_bass_kernel_spmd
from contextlib import ExitStack

F32 = mybir.dt.float32
BF16 = mybir.dt.bfloat16
I32 = mybir.dt.int32
AF = mybir.ActivationFunctionType
ALU = mybir.AluOpType
BFNP = ml_dtypes.bfloat16

NCORES = 8
B, S, D = 2, 8192, 2048
TPC = S // 4
NT = TPC // 128
NCD = D // 128
HEADS, HD, KVH, GQ = 16, 128, 8, 2
GI = 5504
EI = 1408
ISH = 2816
NE = 8
NH1, HS1 = 32, 64
CC = 16
LAM = 5.0
EPS = 1e-5
MAGIC = 12582912.0
INVSQ = 1.0 / np.sqrt(np.float32(D))


class Ctr:
    def __init__(self, sem):
        self.sem = sem
        self.n = 0

    def inc(self, instr, k=1):
        instr.then_inc(self.sem, k)
        self.n += k
        return instr


def f32(x):
    return np.ascontiguousarray(x, dtype=np.float32)


def bfn(x):
    return np.ascontiguousarray(x, dtype=BFNP)


def setup_consts(nc, es, block, gc: Ctr):
    rowi = es.enter_context(nc.sbuf_tensor("c_rowi", [128, 1], F32))
    coli = es.enter_context(nc.sbuf_tensor("c_coli", [128, 128], F32))
    identf = es.enter_context(nc.sbuf_tensor("c_identf", [128, 128], F32))
    identb = es.enter_context(nc.sbuf_tensor("c_identb", [128, 128], BF16))

    def g(gp):
        gp.iota(rowi[:], pattern=[[0, 1]], base=0, channel_multiplier=1,
                allow_small_or_imprecise_dtypes=True)
        gp.iota(coli[:], pattern=[[1, 128]], base=0, channel_multiplier=0,
                allow_small_or_imprecise_dtypes=True)
        gc.inc(gp.engine_nop())
    block.gpsimd(g)

    def finish(v, vc: Ctr):
        v.tensor_scalar(out=identf[:], in0=coli[:], scalar1=rowi[:, 0:1], scalar2=None,
                        op0=ALU.is_equal, op1=ALU.bypass)
        vc.inc(v.tensor_copy(identb[:], identf[:]))
    return identf, identb, rowi, coli, finish


PREP_T = [("WQp", 2048, 2048), ("WKp", 1024, 2048), ("WVp", 1024, 2048), ("WO", 2048, 2048)] + \
         [(f"eg{e}", EI, 2048) for e in range(NE)] + \
         [(f"eu{e}", EI, 2048) for e in range(NE)] + \
         [(f"ed{e}", 2048, EI) for e in range(NE)] + \
         [("sg", ISH, 2048), ("su", ISH, 2048), ("sd", 2048, ISH), ("gw", 8, 2048)]
PREP_N = [("Wx", 2048, 2048), ("Ww", 2048, 2048), ("Wk", 2048, 2048),
          ("Wv", 2048, 2048), ("Wr", 2048, 2048), ("Wout", 2048, 2048)]
PREP_G = [("gg", GI, 2048), ("gu", GI, 2048), ("gd", 2048, GI)]


def row_slices(rows):
    q, r = divmod(rows, NCORES)
    out, s = [], 0
    for c in range(NCORES):
        n = q + (1 if c < r else 0)
        out.append((s, n))
        s += n
    return out


def build_prep(sl):
    nc = bass.Bass("TRN2", target_bir_lowering=False)
    es = ExitStack()
    with es:
        ins, outs = {}, {}
        for key, rows, cols in PREP_T:
            ins[key] = nc.dram_tensor(key, [sl[key], cols], F32, kind="ExternalInput").ap()
            outs[key] = nc.dram_tensor("o_" + key, [cols, sl[key]], BF16, kind="ExternalOutput").ap()
        for key, rows, cols in PREP_N:
            ins[key] = nc.dram_tensor(key, [sl[key], cols], F32, kind="ExternalInput").ap()
            outs[key] = nc.dram_tensor("o_" + key, [sl[key], cols], BF16, kind="ExternalOutput").ap()
        for key, rows, cols in PREP_G:
            ins[key] = nc.dram_tensor(key, [sl[key], cols], I32, kind="ExternalInput").ap()
            outs[key] = nc.dram_tensor("o_" + key, [cols, sl[key]], BF16, kind="ExternalOutput").ap()
        ins["lut"] = nc.dram_tensor("lut", [1, 16], F32, kind="ExternalInput").ap()

        raw = es.enter_context(nc.sbuf_tensor("raw", [128, 2, 5504], F32))
        selm = es.enter_context(nc.sbuf_tensor("selm", [128, 5504], BF16))
        fidx = es.enter_context(nc.sbuf_tensor("fidx", [128, 5504], BF16))
        mask = es.enter_context(nc.sbuf_tensor("mask", [128, 5504], BF16))
        stage = es.enter_context(nc.sbuf_tensor("stage", [128, 11264], BF16))
        lutrow = es.enter_context(nc.sbuf_tensor("lutrow", [1, 16], F32))
        lutb = es.enter_context(nc.sbuf_tensor("lutb", [128, 16], F32))
        onesf = es.enter_context(nc.sbuf_tensor("onesf", [1, 128], F32))
        pbc = es.enter_context(nc.psum_tensor("pbc", [128, 16], F32))
        pst = [es.enter_context(nc.psum_tensor(f"pst{i}", [128, 128], BF16)) for i in range(2)]

        ld = Ctr(es.enter_context(nc.semaphore("ld")))
        vc = Ctr(es.enter_context(nc.semaphore("vc")))
        tc = Ctr(es.enter_context(nc.semaphore("tc")))
        st = Ctr(es.enter_context(nc.semaphore("st")))
        gc = Ctr(es.enter_context(nc.semaphore("gc")))
        block = es.enter_context(nc.Block())

        identf, identb, rowi, coli, const_fin = setup_consts(nc, es, block, gc)

        block.sync(lambda sy: ld.inc(sy.dma_start(out=lutrow[:], in_=ins["lut"][:]), 16))

        def v0(v):
            v.wait_ge(ld.sem, ld.n)
            v.wait_ge(gc.sem, gc.n)
            const_fin(v, vc)
            vc.inc(v.tensor_copy(onesf[:], identb[0:1, 0:128]) if False else v.memset(onesf[:], 1.0))
        block.vector(v0)

        def t0(te):
            te.wait_ge(vc.sem, vc.n)
            tc.inc(te.matmul(pbc[:], onesf[:], lutrow[:], start=True, stop=True))
        block.tensor(t0)

        def v1(v):
            v.wait_ge(tc.sem, tc.n)
            vc.inc(v.tensor_copy(lutb[:], pbc[:]))
        block.vector(v1)

        import os
        plans = [(k, sl[k], c, "t") for k, r, c in PREP_T] + \
                [(k, sl[k], c, "n") for k, r, c in PREP_N] + \
                [(k, sl[k], c, "g") for k, r, c in PREP_G]
        if os.environ.get("PREP_ONLY"):
            keys = set(os.environ["PREP_ONLY"].split(","))
            plans = [p for p in plans if p[0] in keys]

        load_done = []
        tr_evac_done = []
        for key, rs, cols, kind in plans:
            ntile = (rs + 127) // 128
            ncc = cols // 128
            for it in range(ntile):
                p = min(128, rs - it * 128)
                slot = len(load_done)
                buf = slot % 2

                def do_load(sy, key=key, it=it, p=p, buf=buf, cols=cols, kind=kind, slot=slot):
                    if slot >= 2:
                        sy.wait_ge(vc.sem, load_done[slot - 2])
                    dst = raw[0:p, buf, 0:cols]
                    if kind == "g":
                        dst = dst.bitcast(I32)
                    ld.inc(sy.dma_start(out=dst, in_=ins[key][it * 128:it * 128 + p, :]), 16)
                block.sync(do_load)
                ldt = ld.n

                if kind == "n":
                    def cast_n(v, p=p, buf=buf, cols=cols, ldt=ldt):
                        v.wait_ge(ld.sem, ldt)
                        if st.n:
                            v.wait_ge(st.sem, st.n)
                        vc.inc(v.tensor_copy(selm[0:p, 0:cols], raw[0:p, buf, 0:cols]))
                    block.vector(cast_n)
                    load_done.append(vc.n)

                    def store_n(gp, key=key, it=it, p=p, cols=cols, vct=vc.n):
                        gp.wait_ge(vc.sem, vct)
                        st.inc(gp.dma_start(out=outs[key][it * 128:it * 128 + p, :],
                                            in_=selm[0:p, 0:cols]), 16)
                    block.gpsimd(store_n)
                    continue

                if kind == "t":
                    def cast_t(v, p=p, buf=buf, cols=cols, ldt=ldt, tct=tc.n, stn=st.n):
                        v.wait_ge(ld.sem, ldt)
                        v.wait_ge(tc.sem, tct)
                        if stn:
                            v.wait_ge(st.sem, stn)
                        vc.inc(v.tensor_copy(selm[0:p, 0:cols], raw[0:p, buf, 0:cols]))
                    block.vector(cast_t)
                else:
                    def cast_g(v, p=p, buf=buf, cols=cols, ldt=ldt, tct=tc.n, stn=st.n):
                        v.wait_ge(ld.sem, ldt)
                        v.wait_ge(tc.sem, tct)
                        if stn:
                            v.wait_ge(st.sem, stn)
                        v.tensor_copy(fidx[0:p, 0:cols], raw[0:p, buf, 0:cols].bitcast(I32))
                        v.memset(selm[0:p, 0:cols], 0.0)
                        for j in range(16):
                            v.tensor_scalar(out=mask[0:p, 0:cols], in0=fidx[0:p, 0:cols],
                                            scalar1=float(j), scalar2=None,
                                            op0=ALU.is_equal, op1=ALU.bypass)
                            v.scalar_tensor_tensor(out=selm[0:p, 0:cols], in0=mask[0:p, 0:cols],
                                                   scalar=lutb[0:p, j:j + 1], op0=ALU.mult,
                                                   op1=ALU.add, in1=selm[0:p, 0:cols])
                        vc.inc(v.engine_nop())
                    block.vector(cast_g)
                load_done.append(vc.n)
                vct_tile = vc.n

                for cc in range(ncc):
                    gti = len(tr_evac_done)
                    def do_tr(te, p=p, cc=cc, vct=vct_tile, gti=gti):
                        te.wait_ge(vc.sem, vct)
                        if gti >= 2:
                            te.wait_ge(vc.sem, tr_evac_done[gti - 2])
                        tc.inc(te.transpose(pst[gti % 2][:, 0:p],
                                            selm[0:p, cc * 128:cc * 128 + 128],
                                            identity=identb[0:p, 0:p]))
                    block.tensor(do_tr)

                    def do_evac(v, p=p, it=it, cc=cc, rs=rs, tct=tc.n, gti=gti):
                        v.wait_ge(tc.sem, tct)
                        vc.inc(v.tensor_copy(stage[:, cc * rs + it * 128: cc * rs + it * 128 + p],
                                             pst[gti % 2][:, 0:p]))
                    block.vector(do_evac)
                    tr_evac_done.append(vc.n)

            if kind in ("t", "g"):
                def store_t(gp, key=key, rs=rs, ncc=ncc, vct=vc.n):
                    gp.wait_ge(vc.sem, vct)
                    for cc in range(ncc):
                        st.inc(gp.dma_start(out=outs[key][cc * 128:cc * 128 + 128, :],
                                            in_=stage[:, cc * rs: cc * rs + rs]), 16)
                block.gpsimd(store_t)
                def guard(v, stn=st.n):
                    v.wait_ge(st.sem, stn)
                    vc.inc(v.engine_nop())
                block.vector(guard)

        def fin(sy):
            sy.wait_ge(st.sem, st.n)
        block.sync(fin)
    return nc


def run_prep(inputs):
    WQp = f32(inputs["L0_attn_W_Q"].reshape(HD, HEADS, D).transpose(1, 0, 2).reshape(D, D))
    WKp = f32(inputs["L0_attn_W_K"].reshape(HD, KVH, D).transpose(1, 0, 2).reshape(KVH * HD, D))
    WVp = f32(inputs["L0_attn_W_V"].reshape(HD, KVH, D).transpose(1, 0, 2).reshape(KVH * HD, D))
    full = {
        "WQp": WQp, "WKp": WKp, "WVp": WVp, "WO": f32(inputs["L0_attn_W_O"]),
        "sg": f32(inputs["L1_ffn_shared_gate"]), "su": f32(inputs["L1_ffn_shared_up"]),
        "sd": f32(inputs["L1_ffn_shared_down"]), "gw": f32(inputs["L1_ffn_gate_w"]),
        "Wx": f32(inputs["L1_attn_Wx"]), "Ww": f32(inputs["L1_attn_Ww"]),
        "Wk": f32(inputs["L1_attn_Wk"]), "Wv": f32(inputs["L1_attn_Wv"]),
        "Wr": f32(inputs["L1_attn_Wr"]), "Wout": f32(inputs["L1_attn_Wout"]),
        "gg": np.ascontiguousarray(inputs["L0_ffn_gate_idx"], dtype=np.int32),
        "gu": np.ascontiguousarray(inputs["L0_ffn_up_idx"], dtype=np.int32),
        "gd": np.ascontiguousarray(inputs["L0_ffn_down_idx"], dtype=np.int32),
    }
    for e in range(NE):
        full[f"eg{e}"] = f32(inputs["L1_ffn_expert_gate"][e])
        full[f"eu{e}"] = f32(inputs["L1_ffn_expert_up"][e])
        full[f"ed{e}"] = f32(inputs["L1_ffn_expert_down"][e])
    lut = f32(inputs["L0_ffn_lut"]).reshape(1, 16)

    all_specs = PREP_T + PREP_N + PREP_G
    slices = {k: row_slices(r) for k, r, c in all_specs}
    sl = {k: max(n for _, n in slices[k]) for k, r, c in all_specs}
    nc = build_prep(sl)
    in_maps = []
    for c in range(NCORES):
        m = {"lut": lut}
        for k, r, cols in all_specs:
            s0, n = slices[k][c]
            arr = full[k][s0:s0 + n]
            if n < sl[k]:
                pad = np.zeros((sl[k] - n,) + arr.shape[1:], arr.dtype)
                arr = np.concatenate([arr, pad], 0)
            m[k] = np.ascontiguousarray(arr)
        in_maps.append(m)
    res = run_bass_kernel_spmd(nc, in_maps, core_ids=list(range(NCORES)))
    W = {}
    for k, r, cols in PREP_T + PREP_G:
        parts = []
        for c in range(NCORES):
            s0, n = slices[k][c]
            parts.append(np.asarray(res.results[c]["o_" + k])[:, :n])
        W[k + "T"] = np.ascontiguousarray(np.concatenate(parts, axis=1))
    for k, r, cols in PREP_N:
        parts = []
        for c in range(NCORES):
            s0, n = slices[k][c]
            parts.append(np.asarray(res.results[c]["o_" + k])[:n])
        W[k] = np.ascontiguousarray(np.concatenate(parts, axis=0))
    return W





def build_l0front():
    nc = bass.Bass("TRN2", target_bir_lowering=False)
    es = ExitStack()
    with es:
        x_in = nc.dram_tensor("x", [TPC, D], F32, kind="ExternalInput").ap()
        wq_in = nc.dram_tensor("WQpT", [D, D], BF16, kind="ExternalInput").ap()
        wk_in = nc.dram_tensor("WKpT", [D, KVH * HD], BF16, kind="ExternalInput").ap()
        wv_in = nc.dram_tensor("WVpT", [D, KVH * HD], BF16, kind="ExternalInput").ap()
        pos_in = nc.dram_tensor("posrow", [1, TPC], F32, kind="ExternalInput").ap()
        xn_out = nc.dram_tensor("xn", [TPC, D], BF16, kind="ExternalOutput").ap()
        xnT_out = nc.dram_tensor("xnTo", [D, TPC], BF16, kind="ExternalOutput").ap()
        qT_out = nc.dram_tensor("qT", [HEADS, HD, TPC], BF16, kind="ExternalOutput").ap()
        kT_out = nc.dram_tensor("kT", [KVH, HD, TPC], BF16, kind="ExternalOutput").ap()
        v_out = nc.dram_tensor("vtok", [KVH, TPC, HD], BF16, kind="ExternalOutput").ap()

        xbuf = es.enter_context(nc.sbuf_tensor("xbuf", [128, 2, D], F32))
        sqb = es.enter_context(nc.sbuf_tensor("sqb", [128, D], BF16))
        stats = es.enter_context(nc.sbuf_tensor("stats", [128, 6, NT], F32))
        xnt = es.enter_context(nc.sbuf_tensor("xnt", [128, 2, D], BF16))
        xnT = es.enter_context(nc.sbuf_tensor("xnT", [128, NCD, TPC], BF16))
        posr = es.enter_context(nc.sbuf_tensor("posr", [1, TPC], F32))
        drow = es.enter_context(nc.sbuf_tensor("drow", [1, 6, 128], F32))
        cosb = es.enter_context(nc.sbuf_tensor("cosb", [128, TPC], F32))
        sinb = es.enter_context(nc.sbuf_tensor("sinb", [128, TPC], F32))
        tred = es.enter_context(nc.sbuf_tensor("tred", [128, TPC], F32))
        tred2 = es.enter_context(nc.sbuf_tensor("tred2", [128, TPC], F32))
        permM = es.enter_context(nc.sbuf_tensor("permM", [128, 128], BF16))
        prow = es.enter_context(nc.sbuf_tensor("prow", [128, 2, 128], F32))
        wbuf = es.enter_context(nc.sbuf_tensor("wbuf", [128, 2, NCD, 128], BF16))
        wvb = es.enter_context(nc.sbuf_tensor("wvb", [128, NCD, 512], BF16))
        hsb = es.enter_context(nc.sbuf_tensor("hsb", [128, D], BF16))
        rope1 = es.enter_context(nc.sbuf_tensor("rope1", [128, D], F32))
        hrot = es.enter_context(nc.sbuf_tensor("hrot", [128, D], BF16))
        vsb = es.enter_context(nc.sbuf_tensor("vsb", [128, 1024], BF16))

        pA = es.enter_context(nc.psum_tensor("pA", [128, D], F32))
        pst = [es.enter_context(nc.psum_tensor(f"pst{i}", [128, 128], BF16)) for i in range(2)]
        pP = es.enter_context(nc.psum_tensor("pP", [128, 512], F32))
        pR = es.enter_context(nc.psum_tensor("pR", [128, 512], F32))

        ld = Ctr(es.enter_context(nc.semaphore("ld")))
        vc = Ctr(es.enter_context(nc.semaphore("vc")))
        tc = Ctr(es.enter_context(nc.semaphore("tc")))
        sc = Ctr(es.enter_context(nc.semaphore("sc")))
        st = Ctr(es.enter_context(nc.semaphore("st")))
        gc = Ctr(es.enter_context(nc.semaphore("gc")))
        block = es.enter_context(nc.Block())

        identf, identb, rowi, coli, const_fin = setup_consts(nc, es, block, gc)

        def drow_i(gp):
            gp.iota(drow[0:1, 0, :], pattern=[[1, 128]], base=0, channel_multiplier=0,
                    allow_small_or_imprecise_dtypes=True)
            gc.inc(gp.engine_nop())
        block.gpsimd(drow_i)

        block.sync(lambda sy: ld.inc(sy.dma_start(out=posr[:], in_=pos_in[:]), 16))

        def v_consts(v):
            v.wait_ge(gc.sem, gc.n)
            v.wait_ge(ld.sem, ld.n)
            const_fin(v, vc)
            v.tensor_scalar(out=prow[:, 0, 0:1], in0=rowi[:, 0:1], scalar1=-64.0, scalar2=None,
                            op0=ALU.add, op1=ALU.bypass)
            v.tensor_scalar(out=prow[:, 1, 0:1], in0=rowi[:, 0:1], scalar1=64.0, scalar2=None,
                            op0=ALU.add, op1=ALU.bypass)
            v.tensor_scalar(out=tred[0:128, 0:128], in0=coli[:], scalar1=prow[:, 0, 0:1],
                            scalar2=None, op0=ALU.is_equal, op1=ALU.bypass)
            v.tensor_scalar(out=tred2[0:128, 0:128], in0=coli[:], scalar1=prow[:, 1, 0:1],
                            scalar2=None, op0=ALU.is_equal, op1=ALU.bypass)
            v.tensor_tensor(out=permM[:], in0=tred2[0:128, 0:128], in1=tred[0:128, 0:128],
                            op=ALU.subtract)
            v.tensor_scalar(out=drow[0:1, 1, :], in0=drow[0:1, 0, :], scalar1=64.0, scalar2=None,
                            op0=ALU.is_ge, op1=ALU.bypass)
            v.scalar_tensor_tensor(out=drow[0:1, 2, :], in0=drow[0:1, 1, :], scalar=-64.0,
                                   op0=ALU.mult, op1=ALU.add, in1=drow[0:1, 0, :])
            vc.inc(v.engine_nop())
        block.vector(v_consts)

        def s_invf(s):
            s.wait_ge(vc.sem, vc.n)
            sc.inc(s.activation(out=drow[0:1, 3, :], in_=drow[0:1, 2, :], func=AF.Exp,
                                bias=0.0, scale=-2.0 * float(np.log(10000.0)) / HD))
        block.scalar(s_invf)

        def v_invf2(v):
            v.wait_ge(sc.sem, sc.n)
            vc.inc(v.tensor_scalar(out=drow[0:1, 4, :], in0=drow[0:1, 3, :],
                                   scalar1=float(1.0 / (2 * np.pi)), scalar2=None,
                                   op0=ALU.mult, op1=ALU.bypass))
        block.vector(v_invf2)

        def t_ang(te):
            te.wait_ge(vc.sem, vc.n)
            for w in range(4):
                tc.inc(te.matmul(pA[:, w * 512:(w + 1) * 512], drow[0:1, 4, :],
                                 posr[:, w * 512:(w + 1) * 512], start=True, stop=True))
        block.tensor(t_ang)

        def v_rangered(v):
            v.wait_ge(tc.sem, tc.n)
            v.tensor_scalar(out=tred[:], in0=pA[:], scalar1=MAGIC, scalar2=MAGIC,
                            op0=ALU.add, op1=ALU.subtract)
            v.tensor_tensor(out=tred2[:], in0=pA[:], in1=tred[:], op=ALU.subtract)
            vc.inc(v.tensor_scalar(out=tred[:], in0=tred2[:], scalar1=float(2 * np.pi),
                                   scalar2=None, op0=ALU.mult, op1=ALU.bypass))
        block.vector(v_rangered)

        def s_sin(s):
            s.wait_ge(vc.sem, vc.n)
            sc.inc(s.activation(out=sinb[:], in_=tred[:], func=AF.Sin, bias=0.0, scale=1.0))
        block.scalar(s_sin)

        def v_rangered2(v):
            v.wait_ge(sc.sem, sc.n)
            v.tensor_scalar(out=tred2[:], in0=pA[:], scalar1=0.25, scalar2=None,
                            op0=ALU.add, op1=ALU.bypass)
            v.tensor_scalar(out=tred[:], in0=tred2[:], scalar1=MAGIC, scalar2=MAGIC,
                            op0=ALU.add, op1=ALU.subtract)
            v.tensor_tensor(out=tred2[:], in0=tred2[:], in1=tred[:], op=ALU.subtract)
            vc.inc(v.tensor_scalar(out=tred[:], in0=tred2[:], scalar1=float(2 * np.pi),
                                   scalar2=None, op0=ALU.mult, op1=ALU.bypass))
        block.vector(v_rangered2)

        def s_cos(s):
            s.wait_ge(vc.sem, vc.n)
            sc.inc(s.activation(out=cosb[:], in_=tred[:], func=AF.Sin, bias=0.0, scale=1.0))
        block.scalar(s_cos)
        tables_done = sc.n

        pass1_cast = []
        for it in range(NT):
            def ld1(sy, it=it, slot=it):
                if slot >= 2:
                    sy.wait_ge(vc.sem, pass1_cast[slot - 2])
                ld.inc(sy.dma_start(out=xbuf[:, it % 2, :], in_=x_in[it * 128:(it + 1) * 128, :]), 16)
            block.sync(ld1)
            ldt = ld.n

            def v_sum(v, it=it, ldt=ldt):
                v.wait_ge(ld.sem, ldt)
                vc.inc(v.tensor_reduce(out=stats[:, 0, it:it + 1], in_=xbuf[:, it % 2, :],
                                       op=ALU.add, axis=mybir.AxisListType.X))
            block.vector(v_sum)

            def s_sq(s, it=it, ldt=ldt, vct=vc.n):
                s.wait_ge(ld.sem, ldt)
                if it >= 1:
                    s.wait_ge(vc.sem, vct)
                sc.inc(s.activation(out=sqb[:], in_=xbuf[:, it % 2, :], func=AF.Square,
                                    bias=0.0, scale=1.0))
            block.scalar(s_sq)

            def v_sqsum(v, it=it, sct=sc.n):
                v.wait_ge(sc.sem, sct)
                vc.inc(v.tensor_reduce(out=stats[:, 1, it:it + 1], in_=sqb[:],
                                       op=ALU.add, axis=mybir.AxisListType.X))
            block.vector(v_sqsum)
            pass1_cast.append(vc.n)

        def v_stats(v):
            v.tensor_scalar(out=stats[:, 2, :], in0=stats[:, 0, :], scalar1=1.0 / D, scalar2=None,
                            op0=ALU.mult, op1=ALU.bypass)
            v.tensor_tensor(out=stats[:, 5, :], in0=stats[:, 2, :], in1=stats[:, 2, :], op=ALU.mult)
            v.tensor_scalar(out=stats[:, 3, :], in0=stats[:, 1, :], scalar1=1.0 / D, scalar2=None,
                            op0=ALU.mult, op1=ALU.bypass)
            v.tensor_tensor(out=stats[:, 3, :], in0=stats[:, 3, :], in1=stats[:, 5, :], op=ALU.subtract)
            vc.inc(v.tensor_scalar(out=stats[:, 3, :], in0=stats[:, 3, :], scalar1=EPS, scalar2=None,
                                   op0=ALU.add, op1=ALU.bypass))
        block.vector(v_stats)

        def s_rstd(s):
            s.wait_ge(vc.sem, vc.n)
            sc.inc(s.activation(out=stats[:, 4, :], in_=stats[:, 3, :], func=AF.Sqrt,
                                bias=0.0, scale=1.0))
        block.scalar(s_rstd)

        def v_recip(v):
            v.wait_ge(sc.sem, sc.n)
            vc.inc(v.reciprocal(out=stats[:, 4, :], in_=stats[:, 4, :]))
        block.vector(v_recip)
        stats_done = vc.n

        xn_cast = []
        for it in range(NT):
            def ld2(sy, it=it, slot=it, p1=pass1_cast[-1]):
                if slot >= 2:
                    sy.wait_ge(vc.sem, xn_cast[slot - 2][0])
                else:
                    sy.wait_ge(vc.sem, p1)
                ld.inc(sy.dma_start(out=xbuf[:, it % 2, :], in_=x_in[it * 128:(it + 1) * 128, :]), 16)
            block.sync(ld2)
            ldt = ld.n

            def v_xn(v, it=it, ldt=ldt, tct=tc.n, stn=st.n):
                v.wait_ge(ld.sem, ldt)
                v.wait_ge(vc.sem, stats_done)
                v.wait_ge(tc.sem, tct)
                v.wait_ge(st.sem, stn)
                vc.inc(v.tensor_scalar(out=xnt[:, it % 2, :], in0=xbuf[:, it % 2, :],
                                       scalar1=stats[:, 2, it:it + 1], scalar2=stats[:, 4, it:it + 1],
                                       op0=ALU.subtract, op1=ALU.mult))
            block.vector(v_xn)
            vct = vc.n
            xn_cast.append((vct, None))

            def st_xn(gp, it=it, vct=vct):
                gp.wait_ge(vc.sem, vct)
                st.inc(gp.dma_start(out=xn_out[it * 128:(it + 1) * 128, :], in_=xnt[:, it % 2, :]), 16)
            block.gpsimd(st_xn)

            for cc in range(NCD):
                gti = it * NCD + cc
                def tr_xn(te, it=it, cc=cc, vct=vct, gti=gti):
                    te.wait_ge(vc.sem, vct)
                    if gti >= 2:
                        te.wait_ge(vc.sem, xn_evac[gti - 2])
                    tc.inc(te.transpose(pst[gti % 2][:], xnt[:, it % 2, cc * 128:(cc + 1) * 128],
                                        identity=identb[:]))
                if gti == 0:
                    xn_evac = []
                block.tensor(tr_xn)

                def ev_xn(v, it=it, cc=cc, tct=tc.n, gti=gti):
                    v.wait_ge(tc.sem, tct)
                    vc.inc(v.tensor_copy(xnT[:, cc, it * 128:(it + 1) * 128], pst[gti % 2][:]))
                block.vector(ev_xn)
                xn_evac.append(vc.n)
        xnT_done = vc.n

        def st_xnT(gp):
            gp.wait_ge(vc.sem, xnT_done)
            for cc in range(NCD):
                st.inc(gp.dma_start(out=xnT_out[cc * 128:(cc + 1) * 128, :], in_=xnT[:, cc, :]), 16)
        block.gpsimd(st_xnT)

        wq_r = wq_in.rearrange("(cc p) n -> p cc n", p=128)
        wk_r = wk_in.rearrange("(cc p) n -> p cc n", p=128)
        wv_r = wv_in.rearrange("(cc p) n -> p cc n", p=128)
        units = [("q", h) for h in range(HEADS)] + [("k", h) for h in range(KVH)]
        unit_done = []
        for ui, (kindu, h) in enumerate(units):
            def ldw(sy, ui=ui, kindu=kindu, h=h):
                if ui >= 2:
                    sy.wait_ge(vc.sem, unit_done[ui - 2])
                src = wq_r if kindu == "q" else wk_r
                ld.inc(sy.dma_start(out=wbuf[:, ui % 2, :, :],
                                    in_=src[:, :, h * 128:(h + 1) * 128]), 16)
            block.sync(ldw)
            ldt = ld.n

            for w in range(4):
                def mm_proj(te, ui=ui, w=w, ldt=ldt, vct=xnT_done, evt=vc.n):
                    te.wait_ge(ld.sem, ldt)
                    te.wait_ge(vc.sem, vct)
                    if w >= 2:
                        te.wait_ge(vc.sem, evt)
                    pb = pP if w % 2 == 0 else pR
                    for cc in range(NCD):
                        mm = te.matmul(pb[:], wbuf[:, ui % 2, cc, :],
                                       xnT[:, cc, w * 512:(w + 1) * 512],
                                       start=(cc == 0), stop=(cc == NCD - 1))
                        if cc == NCD - 1:
                            tc.inc(mm)
                block.tensor(mm_proj)

                def ev_proj(v, w=w, tct=tc.n):
                    v.wait_ge(tc.sem, tct)
                    pb = pP if w % 2 == 0 else pR
                    vc.inc(v.tensor_copy(hsb[:, w * 512:(w + 1) * 512], pb[:]))
                block.vector(ev_proj)
            head_evac = vc.n

            for w in range(4):
                def mm_rot(te, w=w, vct=head_evac, evt=vc.n):
                    te.wait_ge(vc.sem, vct)
                    if w >= 2:
                        te.wait_ge(vc.sem, evt)
                    pb = pP if w % 2 == 0 else pR
                    tc.inc(te.matmul(pb[:], permM[:], hsb[:, w * 512:(w + 1) * 512],
                                     start=True, stop=True))
                block.tensor(mm_rot)

                def v_rope(v, w=w, tct=tc.n, sct=tables_done):
                    v.wait_ge(tc.sem, tct)
                    v.wait_ge(sc.sem, sct)
                    pb = pP if w % 2 == 0 else pR
                    sl = slice(w * 512, (w + 1) * 512)
                    v.tensor_tensor(out=tred[:, 0:512], in0=pb[:], in1=sinb[:, sl], op=ALU.mult)
                    v.tensor_tensor(out=tred2[:, 0:512], in0=hsb[:, sl], in1=cosb[:, sl], op=ALU.mult)
                    vc.inc(v.tensor_tensor(out=hrot[:, sl], in0=tred[:, 0:512],
                                           in1=tred2[:, 0:512], op=ALU.add))
                block.vector(v_rope)
            unit_done.append(vc.n)

            def st_head(gp, kindu=kindu, h=h, vct=vc.n):
                gp.wait_ge(vc.sem, vct)
                dst = qT_out[h] if kindu == "q" else kT_out[h]
                st.inc(gp.dma_start(out=dst[:, :], in_=hrot[:]), 16)
            block.gpsimd(st_head)
            def gd(v, stn=st.n):
                v.wait_ge(st.sem, stn)
                vc.inc(v.engine_nop())
            block.vector(gd)
            unit_done[-1] = vc.n

        v_r = v_out.rearrange("kv (tt p) d -> tt p kv d", p=128)
        for wnd in range(2):
            def ldwv(sy, wnd=wnd):
                if wnd >= 1:
                    sy.wait_ge(tc.sem, tc.n)
                ld.inc(sy.dma_start(out=wvb[:, :, :], in_=wv_r[:, :, wnd * 512:(wnd + 1) * 512]), 16)
            block.sync(ldwv)
            ldt = ld.n
            for it in range(NT):
                def mm_v(te, it=it, wnd=wnd, ldt=ldt, evt=vc.n):
                    te.wait_ge(ld.sem, ldt)
                    if it >= 2:
                        te.wait_ge(vc.sem, evt)
                    pb = pP if it % 2 == 0 else pR
                    for cc in range(NCD):
                        mm = te.matmul(pb[:], xnT[:, cc, it * 128:(it + 1) * 128],
                                       wvb[:, cc, :], start=(cc == 0), stop=(cc == NCD - 1))
                        if cc == NCD - 1:
                            tc.inc(mm)
                block.tensor(mm_v)

                def ev_v(v, it=it, tct=tc.n, stn=st.n):
                    v.wait_ge(tc.sem, tct)
                    v.wait_ge(st.sem, stn)
                    pb = pP if it % 2 == 0 else pR
                    vc.inc(v.tensor_copy(vsb[:, 0:512], pb[:]))
                block.vector(ev_v)

                def st_v(gp, it=it, wnd=wnd, vct=vc.n):
                    gp.wait_ge(vc.sem, vct)
                    dst = v_r[it, :, wnd * 4:(wnd + 1) * 4, :]
                    st.inc(gp.dma_start(out=dst, in_=vsb[:, 0:512]), 16)
                block.gpsimd(st_v)

        def fin(sy):
            sy.wait_ge(st.sem, st.n)
        block.sync(fin)
    return nc


def build_l0attn():
    nc = bass.Bass("TRN2", target_bir_lowering=False)
    es = ExitStack()
    with es:
        qT_in = nc.dram_tensor("qT", [HEADS, HD, TPC], BF16, kind="ExternalInput").ap()
        kT_in = nc.dram_tensor("kTf", [KVH, HD, S], BF16, kind="ExternalInput").ap()
        v_in = nc.dram_tensor("vf", [KVH, S, HD], BF16, kind="ExternalInput").ap()
        wo_in = nc.dram_tensor("WOT", [D, D], BF16, kind="ExternalInput").ap()
        s_out = nc.dram_tensor("s_att", [TPC, D], BF16, kind="ExternalOutput").ap()

        NKC = S // 128
        NGRP = NKC // 2

        kvb = es.enter_context(nc.sbuf_tensor("kvb", [128, 1, S], BF16))
        vvb = es.enter_context(nc.sbuf_tensor("vvb", [128, 1, NKC, 128], BF16))
        qhb = es.enter_context(nc.sbuf_tensor("qhb", [128, 2, TPC], BF16))
        esb = es.enter_context(nc.sbuf_tensor("esb", [128, 2, 1024], BF16))
        ctxa = es.enter_context(nc.sbuf_tensor("ctxa", [128, HEADS, TPC], BF16))
        ctmp = es.enter_context(nc.sbuf_tensor("ctmp", [128, 512], BF16))
        wot = es.enter_context(nc.sbuf_tensor("wot", [128, NCD, D], BF16))
        rsum = es.enter_context(nc.sbuf_tensor("rsum", [1, 2, 512], F32))
        ones128 = es.enter_context(nc.sbuf_tensor("ones128", [128, 1], BF16))
        ones1f = es.enter_context(nc.sbuf_tensor("ones1f", [1, 128], F32))
        otile = es.enter_context(nc.sbuf_tensor("otile", [128, 2, 512], BF16))

        pS = es.enter_context(nc.psum_tensor("pS", [128, 2, 1024], F32))
        pCtx = es.enter_context(nc.psum_tensor("pCtx", [128, 512], F32))
        pSum = es.enter_context(nc.psum_tensor("pSum", [1, 512], F32))
        pBC = es.enter_context(nc.psum_tensor("pBC", [128, 512], F32))
        pO = es.enter_context(nc.psum_tensor("pO", [128, 512], F32))

        ld = Ctr(es.enter_context(nc.semaphore("ld")))
        vc = Ctr(es.enter_context(nc.semaphore("vc")))
        tc = Ctr(es.enter_context(nc.semaphore("tc")))
        sc = Ctr(es.enter_context(nc.semaphore("sc")))
        st = Ctr(es.enter_context(nc.semaphore("st")))
        gc = Ctr(es.enter_context(nc.semaphore("gc")))
        block = es.enter_context(nc.Block())

        def g0(gp):
            gc.inc(gp.engine_nop())
        block.gpsimd(g0)

        def v0(v):
            v.memset(ones128[:], 1.0)
            vc.inc(v.memset(ones1f[:], 1.0))
        block.vector(v0)

        wot_r = wo_in.rearrange("(cc p) n -> p cc n", p=128)
        block.sync(lambda sy: ld.inc(sy.dma_start(out=wot[:], in_=wot_r), 16))

        kv_ready = []
        q_ready = []
        for kv in range(KVH):
            def ldkv(sy, kv=kv, tct=tc.n):
                if kv >= 1:
                    sy.wait_ge(tc.sem, kv_free[kv - 1])
                ld.inc(sy.dma_start(out=kvb[:, 0, :], in_=kT_in[kv]), 16)
                vr = v_in[kv].rearrange("(c p) d -> p c d", p=128)
                ld.inc(sy.dma_start(out=vvb[:, 0, :, :], in_=vr), 16)
            if kv == 0:
                kv_free = []
            block.sync(ldkv)
            kv_ready.append(ld.n)

            for g in range(GQ):
                h = kv * GQ + g
                def ldq(sy, h=h, tct=tc.n):
                    if h >= 2:
                        sy.wait_ge(tc.sem, q_free[h - 2])
                    ld.inc(sy.dma_start(out=qhb[:, h % 2, :], in_=qT_in[h]), 16)
                if h == 0:
                    q_free = []
                block.sync(ldq)
                q_ready.append(ld.n)

                for w in range(4):
                    exp_done = []
                    pv_done = []
                    def emit_sc(grp, first):
                        def mm_sc(te, kv=kv, h=h, w=w, grp=grp, first=first,
                                  ldt=max(kv_ready[kv], q_ready[h]), vct=vc.n):
                            te.wait_ge(ld.sem, ldt)
                            if first:
                                te.wait_ge(vc.sem, vct)
                            if grp >= 2:
                                te.wait_ge(sc.sem, exp_done[grp - 2])
                            for j in range(2):
                                kc = grp * 2 + j
                                mm = te.matmul(pS[:, grp % 2, j * 512:j * 512 + 512],
                                               kvb[:, 0, kc * 128:(kc + 1) * 128],
                                               qhb[:, h % 2, w * 512:(w + 1) * 512],
                                               start=True, stop=True)
                                if j == 1:
                                    tc.inc(mm)
                        block.tensor(mm_sc)

                    def emit_exp(grp):
                        def s_exp(s, grp=grp, tct=tc.n):
                            s.wait_ge(tc.sem, tct)
                            if grp >= 2:
                                s.wait_ge(tc.sem, pv_done[grp - 2])
                            sc.inc(s.activation(out=esb[:, grp % 2, :], in_=pS[:, grp % 2, :],
                                                func=AF.Exp, bias=0.0, scale=float(INVSQ)))
                        block.scalar(s_exp)
                        exp_done.append(sc.n)

                    def emit_spv(grp):
                        def mm_spv(te, kv=kv, grp=grp, sct=exp_done[grp]):
                            te.wait_ge(sc.sem, sct)
                            for j in range(2):
                                kc = grp * 2 + j
                                te.matmul(pSum[:], ones128[:],
                                          esb[:, grp % 2, j * 512:j * 512 + 512],
                                          start=(kc == 0), stop=(kc == NKC - 1))
                                mm = te.matmul(pCtx[:], vvb[:, 0, kc, :],
                                               esb[:, grp % 2, j * 512:j * 512 + 512],
                                               start=(kc == 0), stop=(kc == NKC - 1))
                                if j == 1:
                                    tc.inc(mm)
                        block.tensor(mm_spv)
                        pv_done.append(tc.n)

                    emit_sc(0, True)
                    emit_exp(0)
                    for grp in range(1, NGRP):
                        emit_sc(grp, False)
                        emit_exp(grp)
                        emit_spv(grp - 1)
                    emit_spv(NGRP - 1)

                    def v_rs(v, w=w, tct=tc.n):
                        v.wait_ge(tc.sem, tct)
                        v.reciprocal(out=rsum[:, w % 2, :], in_=pSum[:])
                        vc.inc(v.tensor_copy(ctmp[:], pCtx[:]))
                    block.vector(v_rs)

                    def mm_bc(te, w=w, vct=vc.n):
                        te.wait_ge(vc.sem, vct)
                        tc.inc(te.matmul(pBC[:], ones1f[:], rsum[:, w % 2, :],
                                         start=True, stop=True))
                    block.tensor(mm_bc)

                    def v_norm(v, h=h, w=w, tct=tc.n):
                        v.wait_ge(tc.sem, tct)
                        vc.inc(v.tensor_tensor(out=ctxa[:, h, w * 512:(w + 1) * 512],
                                               in0=ctmp[:], in1=pBC[:], op=ALU.mult))
                    block.vector(v_norm)
                q_free.append(tc.n)
            kv_free.append(tc.n)
        ctx_done = vc.n

        for it in range(NT):
            for dw in range(4):
                def mm_o(te, it=it, dw=dw, vct=ctx_done, evt=vc.n):
                    te.wait_ge(vc.sem, vct)
                    te.wait_ge(ld.sem, 16)
                    slot = it * 4 + dw
                    if slot >= 2:
                        te.wait_ge(vc.sem, evt)
                    for cc in range(NCD):
                        mm = te.matmul(pO[:] if False else (pS[:, 0, 0:512] if slot % 2 == 0 else pS[:, 1, 0:512]),
                                       ctxa[:, cc, it * 128:(it + 1) * 128],
                                       wot[:, cc, dw * 512:(dw + 1) * 512],
                                       start=(cc == 0), stop=(cc == NCD - 1))
                        if cc == NCD - 1:
                            tc.inc(mm)
                block.tensor(mm_o)

                def ev_o(v, it=it, dw=dw, tct=tc.n, stn=st.n):
                    v.wait_ge(tc.sem, tct)
                    if st.n:
                        v.wait_ge(st.sem, stn)
                    slot = it * 4 + dw
                    vc.inc(v.tensor_copy(otile[:, slot % 2, :],
                                         pS[:, 0, 0:512] if slot % 2 == 0 else pS[:, 1, 0:512]))
                block.vector(ev_o)

                def st_o(gp, it=it, dw=dw, vct=vc.n):
                    gp.wait_ge(vc.sem, vct)
                    slot = it * 4 + dw
                    st.inc(gp.dma_start(out=s_out[it * 128:(it + 1) * 128, dw * 512:(dw + 1) * 512],
                                        in_=otile[:, slot % 2, :]), 16)
                block.gpsimd(st_o)

        block.sync(lambda sy: sy.wait_ge(st.sem, st.n))
    return nc


def build_l0ffn():
    nc = bass.Bass("TRN2", target_bir_lowering=False)
    es = ExitStack()
    with es:
        NIB = GI // 128
        x_in = nc.dram_tensor("x", [TPC, D], F32, kind="ExternalInput").ap()
        xnT_in = nc.dram_tensor("xnTo", [D, TPC], BF16, kind="ExternalInput").ap()
        s_in = nc.dram_tensor("s_att", [TPC, D], BF16, kind="ExternalInput").ap()
        gg_in = nc.dram_tensor("ggT", [D, GI], BF16, kind="ExternalInput").ap()
        gu_in = nc.dram_tensor("guT", [D, GI], BF16, kind="ExternalInput").ap()
        gd_in = nc.dram_tensor("gdT", [GI, D], BF16, kind="ExternalInput").ap()
        x1_out = nc.dram_tensor("x1", [TPC, D], BF16, kind="ExternalOutput").ap()

        xnT = es.enter_context(nc.sbuf_tensor("xnT", [128, NCD, TPC], BF16))
        wgu = es.enter_context(nc.sbuf_tensor("wgu", [128, 2, 2, NCD, 128], BF16))
        hgb = es.enter_context(nc.sbuf_tensor("hgb", [128, NIB, 512], BF16))
        gdw = es.enter_context(nc.sbuf_tensor("gdw", [128, 1, NIB, 512], BF16))
        silub = es.enter_context(nc.sbuf_tensor("silub", [128, 512], BF16))
        sb = es.enter_context(nc.sbuf_tensor("sb", [128, 2, 512], BF16))
        xb = es.enter_context(nc.sbuf_tensor("xb", [128, 2, 512], F32))
        gtl = es.enter_context(nc.sbuf_tensor("gtl", [128, 2, 512], BF16))
        x1b = es.enter_context(nc.sbuf_tensor("x1b", [128, 2, 512], BF16))

        pG = es.enter_context(nc.psum_tensor("pG", [128, 512], F32))
        pU = es.enter_context(nc.psum_tensor("pU", [128, 512], F32))
        pD = es.enter_context(nc.psum_tensor("pD", [128, 2, 512], F32))

        ld = Ctr(es.enter_context(nc.semaphore("ld")))
        vc = Ctr(es.enter_context(nc.semaphore("vc")))
        tc = Ctr(es.enter_context(nc.semaphore("tc")))
        sc = Ctr(es.enter_context(nc.semaphore("sc")))
        st = Ctr(es.enter_context(nc.semaphore("st")))
        gc = Ctr(es.enter_context(nc.semaphore("gc")))
        block = es.enter_context(nc.Block())

        block.gpsimd(lambda gp: gc.inc(gp.engine_nop()))

        xnT_r = xnT_in.rearrange("(cc p) n -> p cc n", p=128)
        block.sync(lambda sy: ld.inc(sy.dma_start(out=xnT[:], in_=xnT_r), 16))
        xnT_ld = ld.n

        gg_r = gg_in.rearrange("(cc p) n -> p cc n", p=128)
        gu_r = gu_in.rearrange("(cc p) n -> p cc n", p=128)
        gd_r = gd_in.rearrange("(ib p) n -> p ib n", p=128)

        for ts in range(4):
            tok = slice(ts * 512, (ts + 1) * 512)
            silu_done = []
            for ib in range(NIB):
                def ldw(sy, ib=ib, ts=ts, vct=vc.n):
                    slot = ts * NIB + ib
                    if slot >= 2:
                        sy.wait_ge(vc.sem, wgu_free[slot - 2])
                    ld.inc(sy.dma_start(out=wgu[:, ib % 2, 0, :, :],
                                         in_=gg_r[:, :, ib * 128:(ib + 1) * 128]), 16)
                    ld.inc(sy.dma_start(out=wgu[:, ib % 2, 1, :, :],
                                        in_=gu_r[:, :, ib * 128:(ib + 1) * 128]), 16)
                if ts == 0 and ib == 0:
                    wgu_free = []
                block.sync(ldw)
                ldt = ld.n

                def mm_gu(te, ib=ib, ts=ts, ldt=ldt, sct=sc.n, vct=vc.n):
                    te.wait_ge(ld.sem, ldt)
                    te.wait_ge(ld.sem, xnT_ld)
                    if ib >= 1:
                        te.wait_ge(vc.sem, vct)
                    for cc in range(NCD):
                        te.matmul(pG[:], wgu[:, ib % 2, 0, cc, :], xnT[:, cc, ts * 512:(ts + 1) * 512],
                                  start=(cc == 0), stop=(cc == NCD - 1))
                        mm = te.matmul(pU[:], wgu[:, ib % 2, 1, cc, :], xnT[:, cc, ts * 512:(ts + 1) * 512],
                                       start=(cc == 0), stop=(cc == NCD - 1))
                        if cc == NCD - 1:
                            tc.inc(mm)
                block.tensor(mm_gu)

                def s_silu(s, ib=ib, tct=tc.n):
                    s.wait_ge(tc.sem, tct)
                    sc.inc(s.activation(out=silub[:], in_=pG[:], func=AF.Silu, bias=0.0, scale=1.0))
                block.scalar(s_silu)

                def v_h(v, ib=ib, sct=sc.n):
                    v.wait_ge(sc.sem, sct)
                    vc.inc(v.tensor_tensor(out=hgb[:, ib, :], in0=silub[:], in1=pU[:], op=ALU.mult))
                block.vector(v_h)
                wgu_free.append(vc.n)
                silu_done.append(vc.n)
            h_done = vc.n

            for hw in range(4):
                def ldd(sy, ts=ts, hw=hw, tct=tc.n):
                    slot = ts * 4 + hw
                    if slot >= 1:
                        sy.wait_ge(tc.sem, gdw_free[slot - 1])
                    ld.inc(sy.dma_start(out=gdw[:, 0, :, :],
                                        in_=gd_r[:, :, hw * 512:(hw + 1) * 512]), 16)
                if ts == 0 and hw == 0:
                    gdw_free = []
                block.sync(ldd)
                ldt = ld.n

                for tsub in range(4):
                    it = ts * 4 + tsub
                    def ld_xs(sy, it=it, tsub=tsub, vct=vc.n):
                        slot2 = (ts * 4 + hw) * 4 + tsub
                        if slot2 >= 2:
                            sy.wait_ge(vc.sem, xs_free[slot2 - 2])
                        ld.inc(sy.dma_start(out=sb[:, tsub % 2, :],
                                             in_=s_in[it * 128:(it + 1) * 128, hw * 512:(hw + 1) * 512]), 16)
                        ld.inc(sy.dma_start(out=xb[:, tsub % 2, :],
                                            in_=x_in[it * 128:(it + 1) * 128, hw * 512:(hw + 1) * 512]), 16)
                    if ts == 0 and hw == 0 and tsub == 0:
                        xs_free = []
                    block.sync(ld_xs)
                    ldt2 = ld.n

                    def mm_d(te, ts=ts, hw=hw, tsub=tsub, ldt=ldt, hct=h_done, sct=sc.n):
                        te.wait_ge(ld.sem, ldt)
                        te.wait_ge(vc.sem, hct)
                        if tsub >= 2:
                            te.wait_ge(sc.sem, sct)
                        for ic in range(NIB):
                            mm = te.matmul(pD[:, tsub % 2, :], hgb[:, ic, tsub * 128:(tsub + 1) * 128],
                                           gdw[:, 0, ic, :], start=(ic == 0), stop=(ic == NIB - 1))
                            if ic == NIB - 1:
                                tc.inc(mm)
                    block.tensor(mm_d)

                    def s_sg(s, tsub=tsub, tct=tc.n, vct=vc.n):
                        s.wait_ge(tc.sem, tct)
                        if vc.n:
                            s.wait_ge(vc.sem, vct)
                        sc.inc(s.activation(out=gtl[:, tsub % 2, :], in_=pD[:, tsub % 2, :],
                                            func=AF.Silu, bias=0.0, scale=1.0))
                    block.scalar(s_sg)

                    def v_comb(v, tsub=tsub, ldt2=ldt2, sct=sc.n, stn=st.n):
                        v.wait_ge(sc.sem, sct)
                        v.wait_ge(ld.sem, ldt2)
                        if st.n:
                            v.wait_ge(st.sem, stn)
                        v.tensor_tensor(out=x1b[:, tsub % 2, :], in0=gtl[:, tsub % 2, :],
                                        in1=sb[:, tsub % 2, :], op=ALU.mult)
                        vc.inc(v.tensor_tensor(out=x1b[:, tsub % 2, :], in0=x1b[:, tsub % 2, :],
                                               in1=xb[:, tsub % 2, :], op=ALU.add))
                    block.vector(v_comb)
                    xs_free.append(vc.n)

                    def st_x1(gp, it=it, hw=hw, tsub=tsub, vct=vc.n):
                        gp.wait_ge(vc.sem, vct)
                        st.inc(gp.dma_start(out=x1_out[it * 128:(it + 1) * 128, hw * 512:(hw + 1) * 512],
                                            in_=x1b[:, tsub % 2, :]), 16)
                    block.gpsimd(st_x1)
                gdw_free.append(tc.n)

        block.sync(lambda sy: sy.wait_ge(st.sem, st.n))
    return nc


def build_l1front():
    nc = bass.Bass("TRN2", target_bir_lowering=False)
    es = ExitStack()
    NCH = TPC // CC
    NG = NCH // CC
    with es:
        x1_in = nc.dram_tensor("x1", [TPC, D], BF16, kind="ExternalInput").ap()
        W_in = {k: nc.dram_tensor(k, [D, D], BF16, kind="ExternalInput").ap()
                for k in ["Wx", "Ww", "Wk", "Wv", "Wr"]}
        gam_in = nc.dram_tensor("gamma", [NCD, 128], F32, kind="ExternalInput").ap()
        bet_in = nc.dram_tensor("beta", [NCD, 128], F32, kind="ExternalInput").ap()
        bw_in = nc.dram_tensor("bw", [NCD, 128], F32, kind="ExternalInput").ap()
        bwf_in = nc.dram_tensor("bwf", [1, D], F32, kind="ExternalInput").ap()
        xn1_out = nc.dram_tensor("xn1", [TPC, D], BF16, kind="ExternalOutput").ap()
        y_out = nc.dram_tensor("y_feat", [D, TPC], BF16, kind="ExternalOutput").ap()
        rT_out = nc.dram_tensor("rT", [D, TPC], BF16, kind="ExternalOutput").ap()
        Sr_out = nc.dram_tensor("Srange", [D, HS1], F32, kind="ExternalOutput").ap()
        Dr_out = nc.dram_tensor("Drange", [D, 1], F32, kind="ExternalOutput").ap()
        import os as _os
        DBG = bool(_os.environ.get("L1DBG"))
        if DBG:
            dbg_l = nc.dram_tensor("dbg_l", [128, TPC], F32, kind="ExternalOutput").ap()
            dbg_Lc = nc.dram_tensor("dbg_Lc", [128, TPC], F32, kind="ExternalOutput").ap()
            dbg_e1 = nc.dram_tensor("dbg_e1", [128, TPC], F32, kind="ExternalOutput").ap()
            dbg_x2 = nc.dram_tensor("dbg_x2", [128, TPC], BF16, kind="ExternalOutput").ap()
            dbg_t1 = nc.dram_tensor("dbg_t1", [128, TPC], BF16, kind="ExternalOutput").ap()
            dbg_pre = nc.dram_tensor("dbg_pre", [128, TPC], F32, kind="ExternalOutput").ap()
            dbg_A = nc.dram_tensor("dbg_A", [16, 8, 512], BF16, kind="ExternalOutput").ap()
        vscr = nc.dram_tensor("vscr", [TPC, D], BF16, kind="Internal").ap()
        kscr = nc.dram_tensor("kscr", [2, TPC, 128], BF16, kind="Internal").ap()
        lscr = nc.dram_tensor("lscr", [D, TPC], F32, kind="Internal").ap()

        xn2T = es.enter_context(nc.sbuf_tensor("xn2T", [128, NCD, TPC], BF16))
        sqb = es.enter_context(nc.sbuf_tensor("sqb", [128, D], BF16))
        stt = es.enter_context(nc.sbuf_tensor("stt", [128, 8, NT], F32))
        gbrow = es.enter_context(nc.sbuf_tensor("gbrow", [128, 3, NCD], F32))
        grow3 = es.enter_context(nc.sbuf_tensor("grow3", [16, 3, 128], F32))
        bwfs = es.enter_context(nc.sbuf_tensor("bwfs", [1, D], F32))
        onesb = es.enter_context(nc.sbuf_tensor("onesb", [1, TPC], BF16))
        bwrow = es.enter_context(nc.sbuf_tensor("bwrow", [1, D], BF16))
        wslA = es.enter_context(nc.sbuf_tensor("wslA", [128, 2, NCD, 128], BF16))
        lfb = es.enter_context(nc.sbuf_tensor("lfb", [128, TPC], F32))
        Lcb = es.enter_context(nc.sbuf_tensor("Lcb", [128, TPC], F32))
        scr2 = es.enter_context(nc.sbuf_tensor("scr2", [128, 512], F32))
        scr2b = es.enter_context(nc.sbuf_tensor("scr2b", [128, 512], F32))
        dbgpre_sb = es.enter_context(nc.sbuf_tensor("dbgpre_sb", [128, TPC], F32)) if DBG else None
        yev = es.enter_context(nc.sbuf_tensor("yev", [128, 2, 512], BF16))

        pP = es.enter_context(nc.psum_tensor("pP", [128, 2, 512], F32))
        pA = es.enter_context(nc.psum_tensor("pA", [16, 512], F32))
        pM = es.enter_context(nc.psum_tensor("pM", [128, 128], F32))
        pY = es.enter_context(nc.psum_tensor("pY", [128, 2, 512], F32))
        pst = es.enter_context(nc.psum_tensor("pstT", [128, 128], BF16))

        ld = Ctr(es.enter_context(nc.semaphore("ld")))
        vc = Ctr(es.enter_context(nc.semaphore("vc")))
        tc = Ctr(es.enter_context(nc.semaphore("tc")))
        sc = Ctr(es.enter_context(nc.semaphore("sc")))
        st = Ctr(es.enter_context(nc.semaphore("st")))
        gc = Ctr(es.enter_context(nc.semaphore("gc")))
        block = es.enter_context(nc.Block())

        identf, identb, rowi, coli, const_fin = setup_consts(nc, es, block, gc)

        def loads1(sy):
            ld.inc(sy.dma_start(out=grow3[:, 0, :], in_=gam_in[:, :]), 16)
            ld.inc(sy.dma_start(out=grow3[:, 1, :], in_=bet_in[:, :]), 16)
            ld.inc(sy.dma_start(out=grow3[:, 2, :], in_=bw_in[:, :]), 16)
            ld.inc(sy.dma_start(out=bwfs[:, :], in_=bwf_in[:, :]), 16)
        block.sync(loads1)
        gb_ld = ld.n

        def v00(v):
            v.wait_ge(gc.sem, gc.n)
            v.wait_ge(ld.sem, gb_ld)
            const_fin(v, vc)
            v.memset(onesb[:], 1.0)
            vc.inc(v.engine_nop())
        block.vector(v00)

        for j in range(3):
            def tr_g(te, j=j, vct=vc.n):
                te.wait_ge(vc.sem, vct)
                if j >= 1:
                    te.wait_ge(vc.sem, gev[j - 1])
                tc.inc(te.transpose(pst.bitcast(F32)[:, 0:16], grow3[:, j, :],
                                    identity=identf[0:16, 0:16]))
            if j == 0:
                gev = []
            block.tensor(tr_g)

            def ev_g(v, j=j, tct=tc.n):
                v.wait_ge(tc.sem, tct)
                vc.inc(v.tensor_copy(gbrow[:, j, :], pst.bitcast(F32)[:, 0:16]))
            block.vector(ev_g)
            gev.append(vc.n)

        def v_bwrow(v):
            vc.inc(v.tensor_copy(bwrow[:], bwfs[:]))
        block.vector(v_bwrow)
        prelim_done = vc.n

        esA = ExitStack()
        x1b = esA.enter_context(nc.sbuf_tensor("x1b", [128, D], BF16))
        xnsc = esA.enter_context(nc.sbuf_tensor("xnsc", [128, D], BF16))
        lnsc = esA.enter_context(nc.sbuf_tensor("lnsc", [128, D], BF16))
        xn_evac = []
        for it in range(NT):
            def ld_x1(sy, it=it, vct=vc.n):
                if it >= 1:
                    sy.wait_ge(vc.sem, x1_free[it - 1])
                ld.inc(sy.dma_start(out=x1b[:], in_=x1_in[it * 128:(it + 1) * 128, :]), 16)
            if it == 0:
                x1_free = []
            block.sync(ld_x1)
            ldt = ld.n

            def v_st1(v, it=it, ldt=ldt):
                v.wait_ge(ld.sem, ldt)
                vc.inc(v.tensor_reduce(out=stt[:, 0, it:it + 1], in_=x1b[:],
                                       op=ALU.add, axis=mybir.AxisListType.X))
            block.vector(v_st1)

            def s_sq1(s, it=it, ldt=ldt, vct=vc.n):
                s.wait_ge(ld.sem, ldt)
                s.wait_ge(vc.sem, vct)
                sc.inc(s.activation(out=sqb[:], in_=x1b[:], func=AF.Square, bias=0.0, scale=1.0))
            block.scalar(s_sq1)

            def v_st2(v, it=it, sct=sc.n):
                v.wait_ge(sc.sem, sct)
                v.tensor_reduce(out=stt[:, 1, it:it + 1], in_=sqb[:], op=ALU.add,
                                axis=mybir.AxisListType.X)
                v.tensor_scalar(out=stt[:, 2, it:it + 1], in0=stt[:, 0, it:it + 1],
                                scalar1=1.0 / D, scalar2=None, op0=ALU.mult, op1=ALU.bypass)
                v.tensor_tensor(out=stt[:, 3, it:it + 1], in0=stt[:, 2, it:it + 1],
                                in1=stt[:, 2, it:it + 1], op=ALU.mult)
                v.tensor_scalar(out=stt[:, 4, it:it + 1], in0=stt[:, 1, it:it + 1],
                                scalar1=1.0 / D, scalar2=None, op0=ALU.mult, op1=ALU.bypass)
                v.tensor_tensor(out=stt[:, 4, it:it + 1], in0=stt[:, 4, it:it + 1],
                                in1=stt[:, 3, it:it + 1], op=ALU.subtract)
                vc.inc(v.tensor_scalar(out=stt[:, 4, it:it + 1], in0=stt[:, 4, it:it + 1],
                                       scalar1=EPS, scalar2=None, op0=ALU.add, op1=ALU.bypass))
            block.vector(v_st2)

            def s_rstd1(s, it=it, vct=vc.n):
                s.wait_ge(vc.sem, vct)
                sc.inc(s.activation(out=stt[:, 5, it:it + 1], in_=stt[:, 4, it:it + 1],
                                    func=AF.Sqrt, bias=0.0, scale=1.0))
            block.scalar(s_rstd1)

            def v_xn1(v, it=it, sct=sc.n, stn=st.n):
                v.wait_ge(sc.sem, sct)
                if stn:
                    v.wait_ge(st.sem, stn)
                v.reciprocal(out=stt[:, 5, it:it + 1], in_=stt[:, 5, it:it + 1])
                vc.inc(v.tensor_scalar(out=xnsc[:], in0=x1b[:],
                                       scalar1=stt[:, 2, it:it + 1], scalar2=stt[:, 5, it:it + 1],
                                       op0=ALU.subtract, op1=ALU.mult))
            block.vector(v_xn1)
            xn1_done = vc.n

            def st_xn1(gp, it=it, vct=xn1_done):
                gp.wait_ge(vc.sem, vct)
                st.inc(gp.dma_start(out=xn1_out[it * 128:(it + 1) * 128, :], in_=xnsc[:]), 16)
            block.gpsimd(st_xn1)

            def v_st3(v, it=it, vct=xn1_done):
                v.wait_ge(vc.sem, vct)
                vc.inc(v.tensor_reduce(out=stt[:, 0, it:it + 1], in_=xnsc[:],
                                       op=ALU.add, axis=mybir.AxisListType.X))
            block.vector(v_st3)

            def s_sq2(s, it=it, vct=vc.n):
                s.wait_ge(vc.sem, vct)
                sc.inc(s.activation(out=sqb[:], in_=xnsc[:], func=AF.Square, bias=0.0, scale=1.0))
            block.scalar(s_sq2)

            def v_st4(v, it=it, sct=sc.n):
                v.wait_ge(sc.sem, sct)
                v.tensor_reduce(out=stt[:, 1, it:it + 1], in_=sqb[:], op=ALU.add,
                                axis=mybir.AxisListType.X)
                v.tensor_scalar(out=stt[:, 2, it:it + 1], in0=stt[:, 0, it:it + 1],
                                scalar1=1.0 / D, scalar2=None, op0=ALU.mult, op1=ALU.bypass)
                v.tensor_tensor(out=stt[:, 3, it:it + 1], in0=stt[:, 2, it:it + 1],
                                in1=stt[:, 2, it:it + 1], op=ALU.mult)
                v.tensor_scalar(out=stt[:, 4, it:it + 1], in0=stt[:, 1, it:it + 1],
                                scalar1=1.0 / D, scalar2=None, op0=ALU.mult, op1=ALU.bypass)
                v.tensor_tensor(out=stt[:, 4, it:it + 1], in0=stt[:, 4, it:it + 1],
                                in1=stt[:, 3, it:it + 1], op=ALU.subtract)
                vc.inc(v.tensor_scalar(out=stt[:, 4, it:it + 1], in0=stt[:, 4, it:it + 1],
                                       scalar1=EPS, scalar2=None, op0=ALU.add, op1=ALU.bypass))
            block.vector(v_st4)

            def s_rstd2(s, it=it, vct=vc.n):
                s.wait_ge(vc.sem, vct)
                sc.inc(s.activation(out=stt[:, 5, it:it + 1], in_=stt[:, 4, it:it + 1],
                                    func=AF.Sqrt, bias=0.0, scale=1.0))
            block.scalar(s_rstd2)

            def v_ln2(v, it=it, sct=sc.n, tct=tc.n):
                v.wait_ge(sc.sem, sct)
                v.wait_ge(tc.sem, tct)
                v.reciprocal(out=stt[:, 5, it:it + 1], in_=stt[:, 5, it:it + 1])
                vc.inc(v.tensor_scalar(out=lnsc[:], in0=xnsc[:],
                                       scalar1=stt[:, 2, it:it + 1], scalar2=stt[:, 5, it:it + 1],
                                       op0=ALU.subtract, op1=ALU.mult))
            block.vector(v_ln2)
            vct_tile = vc.n
            x1_free.append(vc.n)

            for cc in range(NCD):
                gti = it * NCD + cc
                def tr2(te, it=it, cc=cc, vct=vct_tile, gti=gti):
                    te.wait_ge(vc.sem, vct)
                    if gti >= 1:
                        te.wait_ge(vc.sem, xn_evac[gti - 1])
                    tc.inc(te.transpose(pst[:], lnsc[:, cc * 128:(cc + 1) * 128],
                                        identity=identb[:]))
                block.tensor(tr2)

                def ev2(v, it=it, cc=cc, tct=tc.n):
                    v.wait_ge(tc.sem, tct)
                    vc.inc(v.tensor_scalar(out=xn2T[:, cc, it * 128:(it + 1) * 128], in0=pst[:],
                                           scalar1=gbrow[:, 0, cc:cc + 1], scalar2=gbrow[:, 1, cc:cc + 1],
                                           op0=ALU.mult, op1=ALU.add))
                block.vector(ev2)
                xn_evac.append(vc.n)
        xn2T_done = vc.n
        esA.close()

        if DBG:
            def st_dbgx2(gp, vct=xn2T_done):
                gp.wait_ge(vc.sem, vct)
                st.inc(gp.dma_start(out=dbg_x2[:, :], in_=xn2T[:, 0, :]), 16)
            block.gpsimd(st_dbgx2)

        esB = ExitStack()
        t1 = esB.enter_context(nc.sbuf_tensor("t1", [128, NCD, TPC], BF16))
        wv5 = esB.enter_context(nc.sbuf_tensor("wv5", [128, NCD, 512], BF16))
        wx_r = W_in["Wx"].rearrange("(cc p) n -> p cc n", p=128)
        wv_r = W_in["Wv"].rearrange("(cc p) n -> p cc n", p=128)
        ww_r = W_in["Ww"].rearrange("(cc p) n -> p cc n", p=128)
        for db in range(NCD):
            def ld_wx(sy, db=db):
                if db >= 2:
                    sy.wait_ge(vc.sem, wx_free[db - 2])
                ld.inc(sy.dma_start(out=wslA[:, db % 2, :, :], in_=wx_r[:, :, db * 128:(db + 1) * 128]), 16)
            if db == 0:
                wx_free = []
            block.sync(ld_wx)
            ldt = ld.n
            for w in range(4):
                def mm_t1(te, db=db, w=w, ldt=ldt, vct=xn2T_done, evt=vc.n):
                    te.wait_ge(ld.sem, ldt)
                    te.wait_ge(vc.sem, vct)
                    if db * 4 + w >= 2:
                        te.wait_ge(vc.sem, evt)
                    for cc in range(NCD):
                        mm = te.matmul(pP[:, w % 2, :], wslA[:, db % 2, cc, :],
                                       xn2T[:, cc, w * 512:(w + 1) * 512],
                                       start=(cc == 0), stop=(cc == NCD - 1))
                        if cc == NCD - 1:
                            tc.inc(mm)
                block.tensor(mm_t1)

                def ev_t1(v, db=db, w=w, tct=tc.n):
                    v.wait_ge(tc.sem, tct)
                    vc.inc(v.tensor_copy(t1[:, db, w * 512:(w + 1) * 512], pP[:, w % 2, :]))
                block.vector(ev_t1)
            wx_free.append(vc.n)
        t1_done = vc.n
        if DBG:
            def st_dbgt1(gp, vct=t1_done):
                gp.wait_ge(vc.sem, vct)
                st.inc(gp.dma_start(out=dbg_t1[:, :], in_=t1[:, 0, :]), 16)
            block.gpsimd(st_dbgt1)

        stl_hist = []
        for pc in range(NCD):
            def ld_ww(sy, pc=pc, t1mm=tc.n):
                if pc >= 2:
                    sy.wait_ge(tc.sem, ww_free[pc - 2])
                else:
                    sy.wait_ge(tc.sem, t1mm)
                ld.inc(sy.dma_start(out=wslA[:, pc % 2, :, :], in_=ww_r[:, :, pc * 128:(pc + 1) * 128]), 16)
            if pc == 0:
                ww_free = []
            block.sync(ld_ww)
            ldt = ld.n
            for w in range(4):
                WSL = slice(w * 512, (w + 1) * 512)
                def mm_pre(te, pc=pc, w=w, ldt=ldt, t1t=t1_done, vct=vc.n):
                    te.wait_ge(ld.sem, ldt)
                    te.wait_ge(vc.sem, t1t)
                    if pc * 4 + w >= 2:
                        te.wait_ge(vc.sem, vct)
                    for cc in range(NCD):
                        te.matmul(pP[:, w % 2, :], wslA[:, pc % 2, cc, :], t1[:, cc, WSL],
                                  start=(cc == 0), stop=False)
                    tc.inc(te.matmul(pP[:, w % 2, :], bwrow[0:1, pc * 128:(pc + 1) * 128],
                                     onesb[:, 0:512], start=False, stop=True))
                block.tensor(mm_pre)
                mmt = tc.n

                def s_abs(s, w=w, tct=mmt, vct=vc.n):
                    s.wait_ge(tc.sem, tct)
                    if vc.n:
                        s.wait_ge(vc.sem, vct)
                    sc.inc(s.activation(out=scr2[:, :], in_=pP[:, w % 2, :], func=AF.Abs,
                                        bias=0.0, scale=1.0))
                block.scalar(s_abs)

                def s_exp(s, w=w, sct=sc.n):
                    s.wait_ge(sc.sem, sct)
                    sc.inc(s.activation(out=scr2b[:, :], in_=scr2[:, :], func=AF.Exp,
                                        bias=0.0, scale=-1.0))
                block.scalar(s_exp)

                def v_relu(v, w=w, tct=mmt, sct=sc.n, stp=stl_hist[-1] if stl_hist else 0):
                    v.wait_ge(tc.sem, tct)
                    v.wait_ge(sc.sem, sct)
                    if w == 0 and stp:
                        v.wait_ge(st.sem, stp)
                    v.tensor_scalar(out=lfb[:, WSL], in0=pP[:, w % 2, :], scalar1=0.0,
                                    scalar2=None, op0=ALU.max, op1=ALU.bypass)
                    if DBG and pc == 0:
                        v.tensor_copy(dbgpre_sb[:, WSL], pP[:, w % 2, :])
                    vc.inc(v.tensor_scalar(out=scr2b[:, :], in0=scr2b[:, :], scalar1=1.0,
                                           scalar2=None, op0=ALU.add, op1=ALU.bypass))
                block.vector(v_relu)

                def s_ln(s, w=w, vct=vc.n):
                    s.wait_ge(vc.sem, vct)
                    sc.inc(s.activation(out=scr2[:, :], in_=scr2b[:, :], func=AF.Ln,
                                        bias=0.0, scale=1.0))
                block.scalar(s_ln)

                def v_l(v, w=w, sct=sc.n):
                    v.wait_ge(sc.sem, sct)
                    vc.inc(v.tensor_tensor(out=lfb[:, WSL], in0=lfb[:, WSL], in1=scr2[:, :], op=ALU.add))
                    v.wait_ge(vc.sem, vc.n)
                    vc.inc(v.tensor_scalar(out=lfb[:, WSL], in0=lfb[:, WSL], scalar1=-1.0,
                                           scalar2=-LAM, op0=ALU.mult, op1=ALU.max))
                block.vector(v_l)
            ww_free.append(tc.n)

            def st_l(gp, pc=pc, vct=vc.n, stl=st.n):
                gp.wait_ge(vc.sem, vct)
                if pc >= 1:
                    gp.wait_ge(st.sem, stl)
                st.inc(gp.dma_start(out=lscr[pc * 128:(pc + 1) * 128, :], in_=lfb[:]), 16)
            block.gpsimd(st_l)
            if DBG and pc == 0:
                def st_dbgpre(gp, vct=vc.n):
                    gp.wait_ge(vc.sem, vct)
                    st.inc(gp.dma_start(out=dbg_pre[:, :], in_=dbgpre_sb[:]), 16)
                block.gpsimd(st_dbgpre)
            stl_hist.append(st.n)
        l_stored = st.n

        for wnd in range(4):
            def ld_wv(sy, wnd=wnd, tct=tc.n):
                sy.wait_ge(tc.sem, tct)
                ld.inc(sy.dma_start(out=wv5[:, :, :], in_=wv_r[:, :, wnd * 512:(wnd + 1) * 512]), 16)
            block.sync(ld_wv)
            ldt = ld.n
            for it in range(NT):
                def mm_v(te, it=it, wnd=wnd, ldt=ldt, evt=vc.n):
                    te.wait_ge(ld.sem, ldt)
                    if wnd * NT + it >= 2:
                        te.wait_ge(vc.sem, evt)
                    for cc in range(NCD):
                        mm = te.matmul(pP[:, it % 2, :], xn2T[:, cc, it * 128:(it + 1) * 128],
                                       wv5[:, cc, :], start=(cc == 0), stop=(cc == NCD - 1))
                        if cc == NCD - 1:
                            tc.inc(mm)
                block.tensor(mm_v)

                def ev_v(v, it=it, tct=tc.n, stn=st.n):
                    v.wait_ge(tc.sem, tct)
                    v.wait_ge(st.sem, stn)
                    vc.inc(v.tensor_copy(yev[:, it % 2, :], pP[:, it % 2, :]))
                block.vector(ev_v)

                def st_v(gp, it=it, wnd=wnd, vct=vc.n):
                    gp.wait_ge(vc.sem, vct)
                    st.inc(gp.dma_start(out=vscr[it * 128:(it + 1) * 128, wnd * 512:(wnd + 1) * 512],
                                        in_=yev[:, it % 2, :]), 16)
                block.gpsimd(st_v)
        v_done_st = st.n
        esB.close()

        esC = ExitStack()
        rb = esC.enter_context(nc.sbuf_tensor("rb", [128, TPC], BF16))
        kb = esC.enter_context(nc.sbuf_tensor("kb", [128, TPC], BF16))
        rt_ = esC.enter_context(nc.sbuf_tensor("rt_", [128, TPC], BF16))
        kt_ = esC.enter_context(nc.sbuf_tensor("kt_", [128, TPC], BF16))
        kbar = esC.enter_context(nc.sbuf_tensor("kbar", [128, TPC], BF16))
        rring = esC.enter_context(nc.sbuf_tensor("rring", [128, TPC], BF16))
        ri = esC.enter_context(nc.sbuf_tensor("ri", [128, 2 * TPC], BF16))
        LE = esC.enter_context(nc.sbuf_tensor("LE", [128, NCH], F32))
        Bc = esC.enter_context(nc.sbuf_tensor("Bc", [128, NCH], F32))
        dec = esC.enter_context(nc.sbuf_tensor("dec", [128, NCH], F32))
        dec0 = esC.enter_context(nc.sbuf_tensor("dec0", [128, NCH], F32))
        Ssb = esC.enter_context(nc.sbuf_tensor("Ssb", [128, 128], BF16))
        Sf32 = esC.enter_context(nc.sbuf_tensor("Sf32", [128, HS1], F32))
        ktr = esC.enter_context(nc.sbuf_tensor("ktr", [128, NT, 128], BF16))
        vpad = esC.enter_context(nc.sbuf_tensor("vpad", [64, 2, CC, 128], BF16))
        kpad = esC.enter_context(nc.sbuf_tensor("kpad", [64, 2, CC, 128], BF16))
        amask = esC.enter_context(nc.sbuf_tensor("amask", [16, 256], BF16))
        maskraw = esC.enter_context(nc.sbuf_tensor("maskraw", [16, 256], F32))
        asb = esC.enter_context(nc.sbuf_tensor("asb", [64, 2, 256], BF16))
        e1f = esC.enter_context(nc.sbuf_tensor("e1f", [128, TPC], F32))

        def g_mask(gp):
            gp.wait_ge(vc.sem, prelim_done)
            gp.iota(maskraw[:, :], pattern=[[0, 16], [1, 16]], base=0,
                    channel_multiplier=0, allow_small_or_imprecise_dtypes=True)
            gc.inc(gp.engine_nop())
        block.gpsimd(g_mask)

        def v_mask(v):
            v.wait_ge(gc.sem, gc.n)
            v.memset(ri[:], 0.0)
            v.memset(vpad[:], 0.0)
            v.memset(kpad[:], 0.0)
            v.memset(asb[:], 0.0)
            vc.inc(v.tensor_scalar(out=amask[:, :], in0=maskraw[:, :], scalar1=rowi[0:16, 0:1],
                                   scalar2=None, op0=ALU.is_ge, op1=ALU.bypass))
        block.vector(v_mask)
        mask_done = vc.n

        wk_r = W_in["Wk"].rearrange("(cc p) n -> p cc n", p=128)
        wr_r = W_in["Wr"].rearrange("(cc p) n -> p cc n", p=128)
        yev_done = []
        string_hist, stkb_hist, stS_hist = [], [], []
        for pc in range(NCD):
            PS = slice(pc * 128, (pc + 1) * 128)
            def ld_wp(sy, pc=pc, lst=l_stored,
                      krp=kr_hist[-1] if pc else 0, scp=scan_hist[-1] if pc else 0):
                if pc >= 1:
                    sy.wait_ge(tc.sem, krp)
                    sy.wait_ge(vc.sem, scp)
                sy.wait_ge(st.sem, lst)
                ld.inc(sy.dma_start(out=wslA[:, 0, :, :], in_=wk_r[:, :, PS]), 16)
                ld.inc(sy.dma_start(out=wslA[:, 1, :, :], in_=wr_r[:, :, PS]), 16)
                ld.inc(sy.dma_start(out=lfb[:], in_=lscr[PS, :]), 16)
            if pc == 0:
                kr_hist, scan_hist = [], []
            block.sync(ld_wp)
            ldt = ld.n

            for w in range(4):
                WSL = slice(w * 512, (w + 1) * 512)
                def mm_kr(te, pc=pc, w=w, ldt=ldt, vct=vc.n, mk=mask_done):
                    te.wait_ge(ld.sem, ldt)
                    te.wait_ge(vc.sem, mk)
                    if not (pc == 0 and w == 0):
                        te.wait_ge(vc.sem, vct)
                    for cc in range(NCD):
                        te.matmul(pP[:, 0, :], wslA[:, 0, cc, :], xn2T[:, cc, WSL],
                                  start=(cc == 0), stop=(cc == NCD - 1))
                        mm = te.matmul(pP[:, 1, :], wslA[:, 1, cc, :], xn2T[:, cc, WSL],
                                       start=(cc == 0), stop=(cc == NCD - 1))
                        if cc == NCD - 1:
                            tc.inc(mm)
                block.tensor(mm_kr)

                def s_sig(s, w=w, tct=tc.n, vct=vc.n):
                    s.wait_ge(tc.sem, tct)
                    sc.inc(s.activation(out=rb[:, WSL], in_=pP[:, 1, :], func=AF.Sigmoid,
                                        bias=0.0, scale=1.0))
                block.scalar(s_sig)

                def v_k(v, w=w, tct=tc.n, sct=sc.n):
                    v.wait_ge(tc.sem, tct)
                    v.wait_ge(sc.sem, sct)
                    vc.inc(v.tensor_copy(kb[:, WSL], pP[:, 0, :]))
                block.vector(v_k)
            kr_done = vc.n
            kr_hist.append(tc.n)

            def v_scan(v, pc=pc, vct=kr_done, ldt=ldt):
                v.wait_ge(ld.sem, ldt)
                v.wait_ge(vc.sem, vct)
                vc.inc(v.tensor_tensor_scan(out=Lcb[:], data0=lfb[:], data1=lfb[:], initial=0.0,
                                            op0=ALU.add, op1=ALU.bypass))
                v.wait_ge(vc.sem, vc.n)
                vc.inc(v.tensor_copy(LE[:], Lcb[:].rearrange("p (c s) -> p c s", s=CC)[:, :, CC - 1].squeeze()))
                v.wait_ge(vc.sem, vc.n)
                v.memset(Bc[:, 0:1], 0.0)
                vc.inc(v.tensor_copy(Bc[:, 1:NCH], LE[:, 0:NCH - 1]))
                v.wait_ge(vc.sem, vc.n)
                vc.inc(v.engine_nop())
            block.vector(v_scan)
            scan_hist.append(vc.n)

            if DBG and pc == 0:
                def st_dbg1(gp, ldt=ldt, vct=vc.n):
                    gp.wait_ge(ld.sem, ldt)
                    gp.wait_ge(vc.sem, vct)
                    st.inc(gp.dma_start(out=dbg_l[:, :], in_=lfb[:]), 16)
                block.gpsimd(st_dbg1)

            def v_lrel(v, pc=pc):
                for c in range(NCH):
                    v.tensor_scalar(out=Lcb[:, c * CC:(c + 1) * CC], in0=Lcb[:, c * CC:(c + 1) * CC],
                                    scalar1=Bc[:, c:c + 1], scalar2=None,
                                    op0=ALU.subtract, op1=ALU.bypass)
                vc.inc(v.engine_nop())
            block.vector(v_lrel)
            lrel_done = vc.n

            if DBG and pc == 0:
                def st_dbg2(gp, vct=lrel_done):
                    gp.wait_ge(vc.sem, vct)
                    st.inc(gp.dma_start(out=dbg_Lc[:, :], in_=Lcb[:]), 16)
                block.gpsimd(st_dbg2)

            def s_exps(s, pc=pc, vct=lrel_done):
                s.wait_ge(vc.sem, vct)
                s.activation(out=e1f[:], in_=Lcb[:], func=AF.Exp, bias=0.0, scale=1.0)
                sc.inc(s.activation(out=kt_[:], in_=Lcb[:], func=AF.Exp, bias=0.0, scale=-1.0))
            block.scalar(s_exps)

            if DBG and pc == 0:
                def st_dbg3(gp, sct=sc.n):
                    gp.wait_ge(sc.sem, sct)
                    st.inc(gp.dma_start(out=dbg_e1[:, :], in_=e1f[:]), 16)
                block.gpsimd(st_dbg3)

            def v_prod(v, pc=pc, sct=sc.n):
                v.wait_ge(sc.sem, sct)
                v.tensor_tensor(out=rt_[:], in0=rb[:], in1=e1f[:], op=ALU.mult)
                vc.inc(v.tensor_tensor(out=kt_[:], in0=kb[:], in1=kt_[:], op=ALU.mult))
                v.wait_ge(vc.sem, vc.n)
                vc.inc(v.tensor_copy(dec[:], e1f[:].rearrange("p (c s) -> p c s", s=CC)[:, :, CC - 1].squeeze()))
                v.wait_ge(vc.sem, vc.n)
                vc.inc(v.tensor_copy(dec0[:], dec[:]))
                v.wait_ge(vc.sem, vc.n)
                for c in range(NCH):
                    v.tensor_scalar(out=kbar[:, c * CC:(c + 1) * CC], in0=kt_[:, c * CC:(c + 1) * CC],
                                    scalar1=dec[:, c:c + 1], scalar2=None, op0=ALU.mult, op1=ALU.bypass)
                riv = ri[:].rearrange("p (c g s) -> p c g s", g=2, s=CC)
                rtv = rt_[:].rearrange("p (c s) -> p c s", s=CC)
                vc.inc(v.engine_nop())
                v.wait_ge(vc.sem, vc.n)
                v.tensor_copy(riv[0:64, :, 0, :], rtv[0:64, :, :])
                vc.inc(v.tensor_copy(riv[64:128, :, 1, :], rtv[64:128, :, :]))
                v.wait_ge(vc.sem, vc.n)
                vc.inc(v.engine_nop())
            block.vector(v_prod)
            prod_done = vc.n

            def v_ring(v, pc=pc, vct=prod_done, srp=string_hist[-1] if pc else 0):
                if srp:
                    v.wait_ge(st.sem, srp)
                vc.inc(v.tensor_tensor_scan(out=scr2[:, 0:NCH], data0=dec[:], data1=dec[:],
                                            initial=1.0, op0=ALU.mult, op1=ALU.bypass))
                v.wait_ge(vc.sem, vc.n)
                v.memset(dec[:, 0:1], 1.0)
                vc.inc(v.tensor_copy(dec[:, 1:NCH], scr2[:, 0:NCH - 1]))
                v.wait_ge(vc.sem, vc.n)
                for c in range(NCH):
                    v.tensor_scalar(out=rring[:, c * CC:(c + 1) * CC], in0=rt_[:, c * CC:(c + 1) * CC],
                                    scalar1=dec[:, c:c + 1], scalar2=None, op0=ALU.mult, op1=ALU.bypass)
                vc.inc(v.engine_nop())
            block.vector(v_ring)
            ring_done = vc.n

            def st_ring(gp, pc=pc, vct=ring_done):
                gp.wait_ge(vc.sem, vct)
                st.inc(gp.dma_start(out=rT_out[PS, :], in_=rring[:]), 16)
                st.inc(gp.dma_start(out=Dr_out[PS, :], in_=scr2[:, NCH - 1:NCH]), 16)
            block.gpsimd(st_ring)
            string_hist.append(st.n)

            ktr_ev = []
            for tt in range(NT):
                def tr_kb(te, tt=tt, vct=prod_done, kbp=stkb_hist[-1] if stkb_hist else 0):
                    te.wait_ge(vc.sem, vct)
                    if tt == 0 and kbp:
                        te.wait_ge(st.sem, kbp)
                    if tt >= 1:
                        te.wait_ge(vc.sem, ktr_ev[tt - 1])
                    tc.inc(te.transpose(pst[:], kbar[:, tt * 128:(tt + 1) * 128], identity=identb[:]))
                block.tensor(tr_kb)

                def ev_kb(v, tt=tt, tct=tc.n):
                    v.wait_ge(tc.sem, tct)
                    vc.inc(v.tensor_copy(ktr[:, tt, :], pst[:]))
                block.vector(ev_kb)
                ktr_ev.append(vc.n)

            def st_kb(gp, pc=pc, vct=vc.n, rlp=None):
                gp.wait_ge(vc.sem, vct)
                if pc >= 2:
                    gp.wait_ge(ld.sem, rec_ld_last[pc - 2])
                st.inc(gp.dma_start(out=kscr[pc % 2].rearrange("(tt p) d -> p tt d", p=128),
                                    in_=ktr[:]), 16)
            if pc == 0:
                rec_ld_last = {}
            block.gpsimd(st_kb)
            kb_st = st.n
            stkb_hist.append(st.n)

            def v_S0(v, vct=prod_done):
                v.memset(Ssb[:], 0.0)
                vc.inc(v.engine_nop())
                v.wait_ge(vc.sem, vc.n)
            block.vector(v_S0)
            s_state = vc.n

            for cg in range(NG):
                def ld_rec(sy, pc=pc, cg=cg, kst=kb_st, vdn=v_done_st, mmt=tc.n):
                    sy.wait_ge(st.sem, max(kst, vdn))
                    if cg >= 2:
                        sy.wait_ge(tc.sem, grp_mm_done[cg - 2])
                    kk_ = kscr[pc % 2].rearrange("(c s) d -> s c d", s=CC)[:, cg * CC:(cg + 1) * CC, :]
                    ld.inc(sy.dma_start(out=kpad[0:16, cg % 2, :, :], in_=kk_), 16)
                    ld.inc(sy.dma_start(out=kpad[32:48, cg % 2, :, :], in_=kk_), 16)
                    vv_ = vscr.rearrange("(c s) d -> s c d", s=CC)[:, cg * CC:(cg + 1) * CC, :]
                    ld.inc(sy.dma_start(out=vpad[0:16, cg % 2, :, 0:64],
                                        in_=vv_[:, :, pc * 128:pc * 128 + 64]), 16)
                    ld.inc(sy.dma_start(out=vpad[32:48, cg % 2, :, 64:128],
                                        in_=vv_[:, :, pc * 128 + 64:pc * 128 + 128]), 16)
                if cg == 0:
                    grp_mm_done = {}
                block.sync(ld_rec)
                rec_ld = ld.n
                rec_ld_last[pc] = rec_ld

                for ci in range(CC):
                    c = cg * CC + ci
                    def mm_A(te, pc=pc, c=c, ci=ci, vct=prod_done, evt=vc.n):
                        te.wait_ge(vc.sem, vct)
                        if ci == 0:
                            te.wait_ge(vc.sem, evt)
                        tc.inc(te.matmul(pA[:, ci * 32:(ci + 1) * 32],
                                         kt_[:, c * CC:(c + 1) * CC],
                                         ri[:, c * 2 * CC:(c + 1) * 2 * CC],
                                         start=(ci == 0), stop=(ci == CC - 1)))
                    block.tensor(mm_A)

                def v_amask(v, cg=cg, tct=tc.n):
                    v.wait_ge(tc.sem, tct)
                    pAv = pA[:].rearrange("s (c g t) -> s c g t", g=2, t=CC)
                    mkv = amask[:].rearrange("s (c t) -> s c t", t=CC)
                    abv0 = asb[0:16, cg % 2, :].rearrange("s (c t) -> s c t", t=CC)
                    abv1 = asb[32:48, cg % 2, :].rearrange("s (c t) -> s c t", t=CC)
                    vc.inc(v.tensor_tensor(out=abv0, in0=pAv[:, :, 0, :].squeeze(), in1=mkv, op=ALU.mult))
                    v.wait_ge(vc.sem, vc.n)
                    vc.inc(v.tensor_tensor(out=abv1, in0=pAv[:, :, 1, :].squeeze(), in1=mkv, op=ALU.mult))
                block.vector(v_amask)
                if DBG and pc == 0:
                    def st_dbgA(gp, cg=cg, vct=vc.n):
                        gp.wait_ge(vc.sem, vct)
                        st.inc(gp.dma_start(out=dbg_A[:, cg, :], in_=asb[:, cg % 2, :]), 16)
                    block.gpsimd(st_dbgA)
                a_done = vc.n

                for ci in range(CC):
                    c = cg * CC + ci
                    def mm_chunk(te, pc=pc, c=c, ci=ci, cg=cg, at=a_done, rl=rec_ld):
                        te.wait_ge(vc.sem, at)
                        te.wait_ge(ld.sem, rl)
                        te.wait_ge(vc.sem, supd_done[c - 1] if c > 0 else s_state)
                        if c % 32 == 0 and len(yev_done) >= 1:
                            te.wait_ge(vc.sem, yev_done[-1])
                        ycol = slice((c % 32) * 16, (c % 32) * 16 + 16)
                        pyb = pY[:, 0, :]
                        first = (c % 32 == 0)
                        last = (c % 32 == 31)
                        te.matmul(pyb[:, ycol], Ssb[:, :],
                                  rt_[:, c * CC:(c + 1) * CC], start=first, stop=False)
                        te.matmul(pyb[:, ycol], vpad[:, cg % 2, ci, :],
                                  asb[:, cg % 2, ci * CC:(ci + 1) * CC],
                                  start=False, stop=last)
                        tc.inc(te.matmul(pM[:], kpad[:, cg % 2, ci, :], vpad[:, cg % 2, ci, :],
                                         start=True, stop=True))
                    block.tensor(mm_chunk)

                    def v_supd(v, c=c, tct=tc.n):
                        v.wait_ge(tc.sem, tct)
                        if c > 0:
                            v.wait_ge(vc.sem, supd_done[c - 1])
                        v.scalar_tensor_tensor(out=Ssb[0:64, 0:64], in0=Ssb[0:64, 0:64],
                                               scalar=dec0[0:64, c:c + 1], op0=ALU.mult,
                                               op1=ALU.add, in1=pM[0:64, 0:64])
                        vc.inc(v.scalar_tensor_tensor(out=Ssb[64:128, 64:128], in0=Ssb[64:128, 64:128],
                                                      scalar=dec0[64:128, c:c + 1], op0=ALU.mult,
                                                      op1=ALU.add, in1=pM[64:128, 64:128]))
                    block.vector(v_supd)
                    if c == 0:
                        supd_done = {}
                    supd_done[c] = vc.n
                grp_mm_done[cg] = tc.n

                if cg % 2 == 1:
                    qw = cg // 2
                    def v_yev(v, qw=qw, tct=tc.n, stn=st.n):
                        v.wait_ge(tc.sem, tct)
                        v.wait_ge(st.sem, stn)
                        vc.inc(v.tensor_copy(yev[:, qw % 2, :], pY[:, 0, :]))
                    block.vector(v_yev)
                    yev_done.append(vc.n)

                    def st_y(gp, pc=pc, qw=qw, vct=vc.n):
                        gp.wait_ge(vc.sem, vct)
                        st.inc(gp.dma_start(out=y_out[PS, qw * 512:(qw + 1) * 512],
                                            in_=yev[:, qw % 2, :]), 16)
                    block.gpsimd(st_y)

            def v_Sout(v, vct=supd_done[NCH - 1], ssp=stS_hist[-1] if stS_hist else 0):
                v.wait_ge(vc.sem, vct)
                if ssp:
                    v.wait_ge(st.sem, ssp)
                v.tensor_copy(Sf32[0:64, :], Ssb[0:64, 0:64])
                vc.inc(v.tensor_copy(Sf32[64:128, :], Ssb[64:128, 64:128]))
            block.vector(v_Sout)

            def st_S(gp, pc=pc, vct=vc.n):
                gp.wait_ge(vc.sem, vct)
                st.inc(gp.dma_start(out=Sr_out[PS, :], in_=Sf32[:]), 16)
            block.gpsimd(st_S)
            stS_hist.append(st.n)
        esC.close()

        block.sync(lambda sy: sy.wait_ge(st.sem, st.n))
    return nc


def build_l1fix():
    import os as _os
    nc = bass.Bass("TRN2", target_bir_lowering=False)
    es = ExitStack()
    with es:
        y_in = nc.dram_tensor("y_feat", [D, TPC], BF16, kind="ExternalInput").ap()
        rT_in = nc.dram_tensor("rT", [D, TPC], BF16, kind="ExternalInput").ap()
        wout_in = nc.dram_tensor("Wout", [D, D], BF16, kind="ExternalInput").ap()
        Sp = [nc.dram_tensor(f"Sp{i}", [D, HS1], F32, kind="ExternalInput").ap() for i in range(3)]
        Dp = [nc.dram_tensor(f"Dp{i}", [128, NCD], F32, kind="ExternalInput").ap() for i in range(3)]
        s1_out = nc.dram_tensor("s1", [TPC, D], BF16, kind="ExternalOutput").ap()
        if _os.environ.get("L1EDBG"):
            dbg_yb = nc.dram_tensor("dbg_yb", [128, 2, TPC], BF16, kind="ExternalOutput").ap()

        yb = es.enter_context(nc.sbuf_tensor("yb", [128, NCD, TPC], BF16))
        rTb = es.enter_context(nc.sbuf_tensor("rTb", [128, TPC], BF16))
        Sin = es.enter_context(nc.sbuf_tensor("Sin", [128, NCD, HS1], F32))
        Din = es.enter_context(nc.sbuf_tensor("Din", [128, 3, NCD], F32))
        Spb = es.enter_context(nc.sbuf_tensor("Spb", [128, 2, NCD, HS1], F32))
        Spad = es.enter_context(nc.sbuf_tensor("Spad", [128, 128], BF16))
        wob = es.enter_context(nc.sbuf_tensor("wob", [128, NCD, D], BF16))
        otile = es.enter_context(nc.sbuf_tensor("otile", [128, 2, 512], BF16))
        pY = es.enter_context(nc.psum_tensor("pY", [128, 2, 512], F32))
        pO = es.enter_context(nc.psum_tensor("pO", [128, 2, 512], F32))

        ld = Ctr(es.enter_context(nc.semaphore("ld")))
        vc = Ctr(es.enter_context(nc.semaphore("vc")))
        tc = Ctr(es.enter_context(nc.semaphore("tc")))
        st = Ctr(es.enter_context(nc.semaphore("st")))
        gc = Ctr(es.enter_context(nc.semaphore("gc")))
        block = es.enter_context(nc.Block())

        block.gpsimd(lambda gp: gc.inc(gp.engine_nop()))

        y_r = y_in.rearrange("(cc p) n -> p cc n", p=128)
        def ld0(sy):
            ld.inc(sy.dma_start(out=yb[:], in_=y_r), 16)
            dsts = [Spb[:, 0, :, :], Spb[:, 1, :, :], Sin[:, :, :]]
            for i in range(3):
                ld.inc(sy.dma_start(out=dsts[i], in_=Sp[i].rearrange("(cc p) j -> p cc j", p=128)), 16)
                ld.inc(sy.dma_start(out=Din[:, i, :], in_=Dp[i][:, :]), 16)
        block.sync(ld0)
        ld_sum = ld.n

        if _os.environ.get("L1EDBG"):
            def st_dbgy(gp):
                gp.wait_ge(ld.sem, ld_sum)
                st.inc(gp.dma_start(out=dbg_yb[:, 0, :], in_=yb[:, 1, :]), 16)
                st.inc(gp.dma_start(out=dbg_yb[:, 1, :], in_=yb[:, 2, :]), 16)
            block.gpsimd(st_dbgy)

        def v_fold(v):
            v.wait_ge(gc.sem, gc.n)
            v.wait_ge(ld.sem, ld_sum)
            for cc in range(NCD):
                v.scalar_tensor_tensor(out=Spb[:, 1, cc, :], in0=Spb[:, 0, cc, :],
                                       scalar=Din[:, 1, cc:cc + 1], op0=ALU.mult,
                                       op1=ALU.add, in1=Spb[:, 1, cc, :])
            vc.inc(v.engine_nop())
            v.wait_ge(vc.sem, vc.n)
            for cc in range(NCD):
                v.scalar_tensor_tensor(out=Sin[:, cc, :], in0=Spb[:, 1, cc, :],
                                       scalar=Din[:, 2, cc:cc + 1], op0=ALU.mult,
                                       op1=ALU.add, in1=Sin[:, cc, :])
            vc.inc(v.engine_nop())
        block.vector(v_fold)
        fold_done = vc.n

        for pc in range(NCD):
            PS = slice(pc * 128, (pc + 1) * 128)
            def ld_p(sy, pc=pc, tct=tc.n):
                if pc >= 1:
                    sy.wait_ge(tc.sem, pair_mm[pc - 1])
                ld.inc(sy.dma_start(out=rTb[:], in_=rT_in[PS, :]), 16)
            if pc == 0:
                pair_mm = []
            block.sync(ld_p)
            ldt = ld.n

            def v_spad(v, pc=pc, vct=fold_done, tct=tc.n):
                v.wait_ge(vc.sem, vct)
                v.wait_ge(tc.sem, tct)
                v.memset(Spad[:], 0.0)
                vc.inc(v.engine_nop())
                v.wait_ge(vc.sem, vc.n)
                v.tensor_copy(Spad[0:64, 0:64], Sin[0:64, pc, :])
                vc.inc(v.tensor_copy(Spad[64:128, 64:128], Sin[64:128, pc, :]))
            block.vector(v_spad)
            spad_done = vc.n

            for tw in range(4):
                TS_ = slice(tw * 512, (tw + 1) * 512)
                def mm_corr(te, pc=pc, tw=tw, ldt=ldt, vct=spad_done, evt=vc.n):
                    te.wait_ge(ld.sem, ldt)
                    te.wait_ge(vc.sem, vct)
                    if pc * 4 + tw >= 2:
                        te.wait_ge(vc.sem, evt)
                    if _os.environ.get("L1ENOCORR"):
                        tc.inc(te.matmul(pY[:, tw % 2, 0:128], Spad[:], Spad[:, 0:128],
                                         start=True, stop=True))
                    else:
                        tc.inc(te.matmul(pY[:, tw % 2, :], Spad[:], rTb[:, TS_],
                                         start=True, stop=True))
                block.tensor(mm_corr)

                def v_ycomb(v, pc=pc, tw=tw, tct=tc.n):
                    v.wait_ge(tc.sem, tct)
                    if _os.environ.get("L1ENOCOMB"):
                        vc.inc(v.engine_nop())
                    else:
                        vc.inc(v.tensor_tensor(out=yb[:, pc, TS_], in0=yb[:, pc, TS_],
                                               in1=pY[:, tw % 2, :], op=ALU.add))
                block.vector(v_ycomb)
            pair_mm.append(tc.n)
        y_done = vc.n

        wo_r = wout_in.rearrange("(cc p) n -> p cc n", p=128)
        block.sync(lambda sy: ld.inc(sy.dma_start(out=wob[:], in_=wo_r), 16))
        ldt_wo = ld.n
        for dw in range(4):
            for it in range(NT):
                def mm_o(te, it=it, dw=dw, ldt=ldt_wo, vct=y_done, evt=vc.n):
                    te.wait_ge(ld.sem, ldt)
                    te.wait_ge(vc.sem, vct)
                    slot = dw * NT + it
                    if slot >= 2:
                        te.wait_ge(vc.sem, evt)
                    for cc in range(NCD):
                        mm = te.matmul(pO[:, it % 2, :], yb[:, cc, it * 128:(it + 1) * 128],
                                       wob[:, cc, dw * 512:(dw + 1) * 512],
                                       start=(cc == 0), stop=(cc == NCD - 1))
                        if cc == NCD - 1:
                            tc.inc(mm)
                block.tensor(mm_o)

                def ev_o(v, it=it, tct=tc.n, stn=st.n):
                    v.wait_ge(tc.sem, tct)
                    v.wait_ge(st.sem, stn)
                    vc.inc(v.tensor_copy(otile[:, it % 2, :], pO[:, it % 2, :]))
                block.vector(ev_o)

                def st_o(gp, it=it, dw=dw, vct=vc.n):
                    gp.wait_ge(vc.sem, vct)
                    st.inc(gp.dma_start(out=s1_out[it * 128:(it + 1) * 128, dw * 512:(dw + 1) * 512],
                                        in_=otile[:, it % 2, :]), 16)
                block.gpsimd(st_o)

        block.sync(lambda sy: sy.wait_ge(st.sem, st.n))
    return nc


def build_moe():
    nc = bass.Bass("TRN2", target_bir_lowering=False)
    es = ExitStack()
    NSB = ISH // 128
    NEB = EI // 128
    with es:
        xn1_in = nc.dram_tensor("xn1", [TPC, D], BF16, kind="ExternalInput").ap()
        x1_in = nc.dram_tensor("x1", [TPC, D], BF16, kind="ExternalInput").ap()
        s1_in = nc.dram_tensor("s1", [TPC, D], BF16, kind="ExternalInput").ap()
        gw_in = nc.dram_tensor("gwT", [D, NE], BF16, kind="ExternalInput").ap()
        sg_in = nc.dram_tensor("sgT", [D, ISH], BF16, kind="ExternalInput").ap()
        su_in = nc.dram_tensor("suT", [D, ISH], BF16, kind="ExternalInput").ap()
        sd_in = nc.dram_tensor("sdT", [ISH, D], BF16, kind="ExternalInput").ap()
        eg_in = [nc.dram_tensor(f"eg{e}T", [D, EI], BF16, kind="ExternalInput").ap() for e in range(NE)]
        eu_in = [nc.dram_tensor(f"eu{e}T", [D, EI], BF16, kind="ExternalInput").ap() for e in range(NE)]
        ed_in = [nc.dram_tensor(f"ed{e}T", [EI, D], BF16, kind="ExternalInput").ap() for e in range(NE)]
        out_o = nc.dram_tensor("out", [TPC, D], F32, kind="ExternalOutput").ap()

        xn1b = es.enter_context(nc.sbuf_tensor("xn1b", [128, D], BF16))
        xnT = es.enter_context(nc.sbuf_tensor("xnT", [128, NCD, TPC], BF16))
        acc = es.enter_context(nc.sbuf_tensor("acc", [128, 4, D], F32))
        hgb = es.enter_context(nc.sbuf_tensor("hgb", [128, NSB, 512], BF16))
        wgu = es.enter_context(nc.sbuf_tensor("wgu", [128, 2, 2, NCD, 128], BF16))
        dwb = es.enter_context(nc.sbuf_tensor("dwb", [128, 2, NSB, 512], BF16))
        gwb = es.enter_context(nc.sbuf_tensor("gwb", [128, NCD, NE], BF16))
        silub = es.enter_context(nc.sbuf_tensor("silub", [128, 512], BF16))
        ceb = es.enter_context(nc.sbuf_tensor("ceb", [128, NT, NE], F32))
        rtmp = es.enter_context(nc.sbuf_tensor("rtmp", [128, 10, NE], F32))
        rtm2 = es.enter_context(nc.sbuf_tensor("rtm2", [128, 8], F32))
        ridx = es.enter_context(nc.sbuf_tensor("ridx", [128, 8], mybir.dt.uint32))
        x1b = es.enter_context(nc.sbuf_tensor("x1b", [128, 2, 512], BF16))
        s1b = es.enter_context(nc.sbuf_tensor("s1b", [128, 2, 512], BF16))
        gsil = es.enter_context(nc.sbuf_tensor("gsil", [128, 2, 512], F32))
        outb = es.enter_context(nc.sbuf_tensor("outb", [128, 2, 512], F32))

        pG = es.enter_context(nc.psum_tensor("pG", [128, 512], F32))
        pU = es.enter_context(nc.psum_tensor("pU", [128, 512], F32))
        pD = es.enter_context(nc.psum_tensor("pD", [128, 2, 512], F32))
        pR = es.enter_context(nc.psum_tensor("pR", [128, NE], F32))
        pst = es.enter_context(nc.psum_tensor("pstT", [128, 128], BF16))

        ld = Ctr(es.enter_context(nc.semaphore("ld")))
        vc = Ctr(es.enter_context(nc.semaphore("vc")))
        tc = Ctr(es.enter_context(nc.semaphore("tc")))
        sc = Ctr(es.enter_context(nc.semaphore("sc")))
        st = Ctr(es.enter_context(nc.semaphore("st")))
        gc = Ctr(es.enter_context(nc.semaphore("gc")))
        block = es.enter_context(nc.Block())

        identf, identb, rowi, coli, const_fin = setup_consts(nc, es, block, gc)
        block.vector(lambda v: (v.wait_ge(gc.sem, gc.n), const_fin(v, vc)))

        gw_r = gw_in.rearrange("(cc p) e -> p cc e", p=128)
        block.sync(lambda sy: ld.inc(sy.dma_start(out=gwb[:], in_=gw_r), 16))
        gw_ld = ld.n

        xn_evac = []
        for it in range(NT):
            def ld_x(sy, it=it, vct=vc.n):
                if it >= 1:
                    sy.wait_ge(vc.sem, xn_done[it - 1])
                ld.inc(sy.dma_start(out=xn1b[:], in_=xn1_in[it * 128:(it + 1) * 128, :]), 16)
            if it == 0:
                xn_done = []
            block.sync(ld_x)
            ldt = ld.n

            for cc in range(NCD):
                gti = it * NCD + cc
                def tr_x(te, it=it, cc=cc, ldt=ldt, gti=gti):
                    te.wait_ge(ld.sem, ldt)
                    if gti >= 1:
                        te.wait_ge(vc.sem, xn_evac[gti - 1])
                    tc.inc(te.transpose(pst[:], xn1b[:, cc * 128:(cc + 1) * 128], identity=identb[:]))
                block.tensor(tr_x)

                def ev_x(v, it=it, cc=cc, tct=tc.n):
                    v.wait_ge(tc.sem, tct)
                    vc.inc(v.tensor_copy(xnT[:, cc, it * 128:(it + 1) * 128], pst[:]))
                block.vector(ev_x)
                xn_evac.append(vc.n)
            xn_done.append(vc.n)

            def mm_r(te, it=it, gwt=gw_ld, vct=vc.n):
                te.wait_ge(ld.sem, gwt)
                te.wait_ge(vc.sem, vct)
                for cc in range(NCD):
                    mm = te.matmul(pR[:], xnT[:, cc, it * 128:(it + 1) * 128], gwb[:, cc, :],
                                   start=(cc == 0), stop=(cc == NCD - 1))
                    if cc == NCD - 1:
                        tc.inc(mm)
            block.tensor(mm_r)

            def v_sm(v, it=it, tct=tc.n):
                v.wait_ge(tc.sem, tct)
                v.tensor_reduce(out=rtmp[:, 0, 0:1], in_=pR[:], op=ALU.max, axis=mybir.AxisListType.X)
                vc.inc(v.tensor_scalar(out=rtmp[:, 1, :], in0=pR[:], scalar1=rtmp[:, 0, 0:1],
                                       scalar2=None, op0=ALU.subtract, op1=ALU.bypass))
            block.vector(v_sm)

            def s_exp(s, it=it, vct=vc.n):
                s.wait_ge(vc.sem, vct)
                sc.inc(s.activation(out=rtmp[:, 2, :], in_=rtmp[:, 1, :], func=AF.Exp,
                                    bias=0.0, scale=1.0))
            block.scalar(s_exp)

            def v_ce(v, it=it, sct=sc.n):
                v.wait_ge(sc.sem, sct)
                v.tensor_reduce(out=rtmp[:, 3, 0:1], in_=rtmp[:, 2, :], op=ALU.add,
                                axis=mybir.AxisListType.X)
                vc.inc(v.engine_nop())
                v.wait_ge(vc.sem, vc.n)
                v.reciprocal(out=rtmp[:, 3, 0:1], in_=rtmp[:, 3, 0:1])
                vc.inc(v.engine_nop())
                v.wait_ge(vc.sem, vc.n)
                v.tensor_scalar(out=rtmp[:, 4, :], in0=rtmp[:, 2, :], scalar1=rtmp[:, 3, 0:1],
                                scalar2=None, op0=ALU.mult, op1=ALU.bypass)
                vc.inc(v.engine_nop())
                v.wait_ge(vc.sem, vc.n)
                v.max_with_indices(rtm2[:, :], ridx[:, :], rtmp[:, 4, :])
                vc.inc(v.engine_nop())
                v.wait_ge(vc.sem, vc.n)
                v.tensor_tensor(out=rtmp[:, 5, 0:1], in0=rtm2[:, 0:1], in1=rtm2[:, 1:2], op=ALU.add)
                vc.inc(v.engine_nop())
                v.wait_ge(vc.sem, vc.n)
                v.tensor_scalar(out=rtmp[:, 5, 0:1], in0=rtmp[:, 5, 0:1], scalar1=1e-20,
                                scalar2=None, op0=ALU.add, op1=ALU.bypass)
                vc.inc(v.engine_nop())
                v.wait_ge(vc.sem, vc.n)
                v.reciprocal(out=rtmp[:, 5, 0:1], in_=rtmp[:, 5, 0:1])
                vc.inc(v.engine_nop())
                v.wait_ge(vc.sem, vc.n)
                v.tensor_scalar(out=rtmp[:, 6, :], in0=rtmp[:, 4, :], scalar1=rtm2[:, 0:1],
                                scalar2=None, op0=ALU.is_equal, op1=ALU.bypass)
                v.tensor_scalar(out=rtmp[:, 7, :], in0=rtmp[:, 4, :], scalar1=rtm2[:, 1:2],
                                scalar2=None, op0=ALU.is_equal, op1=ALU.bypass)
                vc.inc(v.engine_nop())
                v.wait_ge(vc.sem, vc.n)
                v.tensor_scalar(out=rtmp[:, 8, :], in0=rtmp[:, 6, :], scalar1=rtm2[:, 0:1],
                                scalar2=None, op0=ALU.mult, op1=ALU.bypass)
                v.tensor_scalar(out=rtmp[:, 9, :], in0=rtmp[:, 7, :], scalar1=rtm2[:, 1:2],
                                scalar2=None, op0=ALU.mult, op1=ALU.bypass)
                vc.inc(v.engine_nop())
                v.wait_ge(vc.sem, vc.n)
                v.tensor_tensor(out=rtmp[:, 8, :], in0=rtmp[:, 8, :], in1=rtmp[:, 9, :], op=ALU.add)
                vc.inc(v.engine_nop())
                v.wait_ge(vc.sem, vc.n)
                vc.inc(v.tensor_scalar(out=ceb[:, it, :], in0=rtmp[:, 8, :], scalar1=rtmp[:, 5, 0:1],
                                       scalar2=None, op0=ALU.mult, op1=ALU.bypass))
            block.vector(v_ce)
        router_done = vc.n

        units = [("s", None)] + [("e", e) for e in range(NE)]
        for q in range(4):
            QT = slice(q * 512, (q + 1) * 512)
            for kindu, e in units:
                nib = NSB if kindu == "s" else NEB
                gu_src = (sg_in, su_in) if kindu == "s" else (eg_in[e], eu_in[e])
                d_src = sd_in if kindu == "s" else ed_in[e]
                g_r = gu_src[0].rearrange("(cc p) n -> p cc n", p=128)
                u_r = gu_src[1].rearrange("(cc p) n -> p cc n", p=128)
                d_r = d_src.rearrange("(ib p) n -> p ib n", p=128)
                for ib in range(nib):
                    def ld_gu(sy, ib=ib, g_r=g_r, u_r=u_r, vct=vc.n):
                        slotw = len(wgu_free)
                        if slotw >= 2:
                            sy.wait_ge(vc.sem, wgu_free[slotw - 2])
                        sl = slice(ib * 128, (ib + 1) * 128)
                        ld.inc(sy.dma_start(out=wgu[:, ib % 2, 0, :, :], in_=g_r[:, :, sl]), 16)
                        ld.inc(sy.dma_start(out=wgu[:, ib % 2, 1, :, :], in_=u_r[:, :, sl]), 16)
                    if q == 0 and kindu == "s" and ib == 0:
                        wgu_free = []
                    block.sync(ld_gu)
                    ldt = ld.n

                    def mm_gu(te, ib=ib, q=q, ldt=ldt, rdt=router_done, vct=vc.n):
                        te.wait_ge(ld.sem, ldt)
                        te.wait_ge(vc.sem, max(rdt, vct))
                        for cc in range(NCD):
                            te.matmul(pG[:], wgu[:, ib % 2, 0, cc, :], xnT[:, cc, QT],
                                      start=(cc == 0), stop=(cc == NCD - 1))
                            mm = te.matmul(pU[:], wgu[:, ib % 2, 1, cc, :], xnT[:, cc, QT],
                                           start=(cc == 0), stop=(cc == NCD - 1))
                            if cc == NCD - 1:
                                tc.inc(mm)
                    block.tensor(mm_gu)

                    def s_sil(s, ib=ib, tct=tc.n):
                        s.wait_ge(tc.sem, tct)
                        sc.inc(s.activation(out=silub[:], in_=pG[:], func=AF.Silu, bias=0.0, scale=1.0))
                    block.scalar(s_sil)

                    def v_h(v, ib=ib, sct=sc.n):
                        v.wait_ge(sc.sem, sct)
                        vc.inc(v.tensor_tensor(out=hgb[:, ib, :], in0=silub[:], in1=pU[:], op=ALU.mult))
                    block.vector(v_h)
                    wgu_free.append(vc.n)
                h_done = vc.n

                for dw in range(4):
                    if q == 0 and kindu == "s" and dw == 0:
                        dwb_free = []
                    dslot = len(dwb_free)
                    def ld_d(sy, dw=dw, d_r=d_r, nib=nib, dslot=dslot):
                        if dslot >= 2:
                            sy.wait_ge(tc.sem, dwb_free[dslot - 2])
                        ld.inc(sy.dma_start(out=dwb[:, dslot % 2, 0:nib, :],
                                            in_=d_r[:, :, dw * 512:(dw + 1) * 512]), 16)
                    block.sync(ld_d)
                    ldt = ld.n

                    for ts4 in range(4):
                        it = q * 4 + ts4
                        def mm_d(te, ts4=ts4, dw=dw, nib=nib, ldt=ldt, hct=h_done, vct=vc.n, dslot=dslot):
                            te.wait_ge(ld.sem, ldt)
                            te.wait_ge(vc.sem, max(hct, vct))
                            for ic in range(nib):
                                mm = te.matmul(pD[:, ts4 % 2, :], hgb[:, ic, ts4 * 128:(ts4 + 1) * 128],
                                               dwb[:, dslot % 2, ic, :], start=(ic == 0), stop=(ic == nib - 1))
                                if ic == nib - 1:
                                    tc.inc(mm)
                        block.tensor(mm_d)

                        def v_acc(v, it=it, ts4=ts4, dw=dw, kindu=kindu, e=e, tct=tc.n):
                            v.wait_ge(tc.sem, tct)
                            dst = acc[:, ts4, dw * 512:(dw + 1) * 512]
                            if kindu == "s":
                                vc.inc(v.tensor_copy(dst, pD[:, ts4 % 2, :]))
                            else:
                                vc.inc(v.scalar_tensor_tensor(out=dst, in0=pD[:, ts4 % 2, :],
                                                              scalar=ceb[:, it, e:e + 1],
                                                              op0=ALU.mult, op1=ALU.add, in1=dst))
                        block.vector(v_acc)
                    dwb_free.append(tc.n)

            for ts4 in range(4):
                it = q * 4 + ts4
                for dw in range(4):
                    slot = ts4 * 4 + dw
                    def ld_xs(sy, it=it, dw=dw, slot=slot, vct=vc.n):
                        if len(comb_done) >= 2:
                            sy.wait_ge(vc.sem, comb_done[-2])
                        ld.inc(sy.dma_start(out=x1b[:, slot % 2, :],
                                            in_=x1_in[it * 128:(it + 1) * 128, dw * 512:(dw + 1) * 512]), 16)
                        ld.inc(sy.dma_start(out=s1b[:, slot % 2, :],
                                            in_=s1_in[it * 128:(it + 1) * 128, dw * 512:(dw + 1) * 512]), 16)
                    if q == 0 and ts4 == 0 and dw == 0:
                        comb_done = []
                    block.sync(ld_xs)
                    ldt = ld.n

                    def s_gsil(s, ts4=ts4, dw=dw, slot=slot, vct=vc.n, sct=sc.n):
                        s.wait_ge(vc.sem, vct)
                        sc.inc(s.activation(out=gsil[:, slot % 2, :], in_=acc[:, ts4, dw * 512:(dw + 1) * 512],
                                            func=AF.Silu, bias=0.0, scale=1.0))
                    block.scalar(s_gsil)

                    def v_out(v, slot=slot, ldt=ldt, sct=sc.n, stn=st.n):
                        v.wait_ge(sc.sem, sct)
                        v.wait_ge(ld.sem, ldt)
                        if stn:
                            v.wait_ge(st.sem, stn)
                        v.tensor_tensor(out=outb[:, slot % 2, :], in0=gsil[:, slot % 2, :],
                                        in1=s1b[:, slot % 2, :], op=ALU.mult)
                        vc.inc(v.engine_nop())
                        v.wait_ge(vc.sem, vc.n)
                        vc.inc(v.tensor_tensor(out=outb[:, slot % 2, :], in0=outb[:, slot % 2, :],
                                               in1=x1b[:, slot % 2, :], op=ALU.add))
                    block.vector(v_out)
                    comb_done.append(vc.n)

                    def st_out(gp, it=it, dw=dw, slot=slot, vct=vc.n):
                        gp.wait_ge(vc.sem, vct)
                        st.inc(gp.dma_start(out=out_o[it * 128:(it + 1) * 128, dw * 512:(dw + 1) * 512],
                                            in_=outb[:, slot % 2, :]), 16)
                    block.gpsimd(st_out)

        block.sync(lambda sy: sy.wait_ge(st.sem, st.n))
    return nc


def kernel(**inputs):
    W = run_prep(inputs)
    x = inputs["L0_attn_x"].astype(np.float32)

    nc = build_l0front()
    in_maps = []
    for c in range(NCORES):
        b, q = c // 4, c % 4
        in_maps.append({
            "x": np.ascontiguousarray(x[b, q * TPC:(q + 1) * TPC]),
            "WQpT": W["WQpT"], "WKpT": W["WKpT"], "WVpT": W["WVpT"],
            "posrow": np.arange(q * TPC, (q + 1) * TPC, dtype=np.float32).reshape(1, TPC),
        })
    r2 = run_bass_kernel_spmd(nc, in_maps, core_ids=list(range(NCORES))).results

    kT_full = {b: np.ascontiguousarray(np.concatenate(
        [np.asarray(r2[b * 4 + q]["kT"]) for q in range(4)], axis=2)) for b in range(2)}
    v_full = {b: np.ascontiguousarray(np.concatenate(
        [np.asarray(r2[b * 4 + q]["vtok"]) for q in range(4)], axis=1)) for b in range(2)}

    nc = build_l0attn()
    in_maps = [{"qT": np.asarray(r2[c]["qT"]), "kTf": kT_full[c // 4],
                "vf": v_full[c // 4], "WOT": W["WOT"]} for c in range(NCORES)]
    r3 = run_bass_kernel_spmd(nc, in_maps, core_ids=list(range(NCORES))).results

    nc = build_l0ffn()
    in_maps = []
    for c in range(NCORES):
        b, q = c // 4, c % 4
        in_maps.append({"x": np.ascontiguousarray(x[b, q * TPC:(q + 1) * TPC]),
                        "xnTo": np.asarray(r2[c]["xnTo"]), "s_att": np.asarray(r3[c]["s_att"]),
                        "ggT": W["ggT"], "guT": W["guT"], "gdT": W["gdT"]})
    r4 = run_bass_kernel_spmd(nc, in_maps, core_ids=list(range(NCORES))).results

    nc = build_l1front()
    gam = np.ascontiguousarray(inputs["L1_attn_gamma"].reshape(NCD, 128).astype(np.float32))
    bet = np.ascontiguousarray(inputs["L1_attn_beta"].reshape(NCD, 128).astype(np.float32))
    bw2 = np.ascontiguousarray(inputs["L1_attn_bw"].reshape(NCD, 128).astype(np.float32))
    bwf = np.ascontiguousarray(inputs["L1_attn_bw"].reshape(1, D).astype(np.float32))
    in_maps = []
    for c in range(NCORES):
        m = {"x1": np.asarray(r4[c]["x1"]), "gamma": gam, "beta": bet, "bw": bw2, "bwf": bwf}
        for k in ["Wx", "Ww", "Wk", "Wv", "Wr"]:
            m[k] = W[k]
        in_maps.append(m)
    r5 = run_bass_kernel_spmd(nc, in_maps, core_ids=list(range(NCORES))).results

    nc = build_l1fix()
    ident_S = np.zeros((D, HS1), np.float32)
    ident_D = np.ones((128, NCD), np.float32)
    in_maps = []
    for c in range(NCORES):
        b, q = c // 4, c % 4
        m = {"y_feat": np.asarray(r5[c]["y_feat"]), "rT": np.asarray(r5[c]["rT"]),
             "Wout": W["Wout"]}
        preds = [b * 4 + qq for qq in range(q)]
        preds = [None] * (3 - len(preds)) + preds
        for i, p in enumerate(preds):
            if p is None:
                m[f"Sp{i}"] = ident_S
                m[f"Dp{i}"] = ident_D
            else:
                m[f"Sp{i}"] = np.ascontiguousarray(np.asarray(r5[p]["Srange"], dtype=np.float32))
                m[f"Dp{i}"] = np.ascontiguousarray(
                    np.asarray(r5[p]["Drange"], dtype=np.float32).reshape(NCD, 128).T)
        in_maps.append(m)
    r6 = run_bass_kernel_spmd(nc, in_maps, core_ids=list(range(NCORES))).results

    nc = build_moe()
    in_maps = []
    for c in range(NCORES):
        m = {"xn1": np.asarray(r5[c]["xn1"]), "x1": np.asarray(r4[c]["x1"]),
             "s1": np.asarray(r6[c]["s1"]), "gwT": W["gwT"], "sgT": W["sgT"],
             "suT": W["suT"], "sdT": W["sdT"]}
        for e in range(NE):
            m[f"eg{e}T"] = W[f"eg{e}T"]
            m[f"eu{e}T"] = W[f"eu{e}T"]
            m[f"ed{e}T"] = W[f"ed{e}T"]
        in_maps.append(m)
    r7 = run_bass_kernel_spmd(nc, in_maps, core_ids=list(range(NCORES))).results

    out = np.empty((B, S, D), np.float32)
    for c in range(NCORES):
        b, q = c // 4, c % 4
        out[b, q * TPC:(q + 1) * TPC] = np.asarray(r7[c]["out"], np.float32)
    return out
